# version 15
# baseline (speedup 1.0000x reference)
import os
from contextlib import ExitStack
import numpy as np
import ml_dtypes
import concourse.bass as bass
import concourse.mybir as mybir
from concourse.bass import IndirectOffsetOnAxis
from concourse.bass_utils import run_bass_kernel_spmd

F32 = mybir.dt.float32
BF16 = mybir.dt.bfloat16
I32 = mybir.dt.int32
I8 = mybir.dt.int8
AX = mybir.AxisListType.X
OP = mybir.AluOpType
ACT = mybir.ActivationFunctionType

NCORES = 8
T = 2048
D = 1024
C = 2048
FF = 4096
H = 16
DH = 64
NT = T // 128
ND = D // 128
NCC = C // 128
NF = FF // 128
RBIG = 12582912.0
EPS = 1e-5

_STAGE = int(os.environ.get("MK_STAGE", "99"))



def _route_moe(expert_indices, expert_weights, x_flat):
    flat_idx = np.asarray(expert_indices).reshape(-1)
    flat_w = np.asarray(expert_weights).reshape(-1).astype(np.float32)
    order = np.argsort(flat_idx, kind="stable")
    tok = order // 2
    counts = np.bincount(flat_idx, minlength=NCORES)
    starts = np.concatenate([[0], np.cumsum(counts)])
    npad = max(512, int(np.ceil((counts.max() + 1) / 128.0) * 128))

    xs_l, fw_l = [], []
    for e in range(NCORES):
        n = int(counts[e])
        xs = np.zeros((npad, D), np.float32)
        fw = np.zeros((npad, 1), np.float32)
        xs[1:1 + n] = x_flat[tok[starts[e]:starts[e + 1]]]
        fw[1:1 + n, 0] = flat_w[order[starts[e]:starts[e + 1]]]
        xs_l.append(np.ascontiguousarray(xs))
        fw_l.append(np.ascontiguousarray(fw))

    expert_of_row = flat_idx[order]
    i_loc = 1 + (np.arange(len(order)) - starts[expert_of_row])
    chk = i_loc // 512
    c_k = np.minimum(512, npad - chk * 512)
    agrow = (NCORES * 512 * chk + expert_of_row * c_k + (i_loc - chk * 512))
    gidx_full = np.zeros((NCORES, 2, T), np.int64)
    occ = np.zeros(NCORES * T, np.int64)
    for r in range(len(order)):
        g = int(tok[r])
        gidx_full[g // T, occ[g], g % T] = agrow[r]
        occ[g] += 1
    gidx_l = []
    for c in range(NCORES):
        g2 = gidx_full[c].reshape(2, NT, 128).transpose(2, 0, 1).reshape(128, 32)
        gidx_l.append(np.ascontiguousarray(g2.astype(np.int32)))
    return npad, xs_l, fw_l, gidx_l


def _col8(v):
    return np.ascontiguousarray(np.asarray(v, np.float32).reshape(ND, 128).T)


def _lay(w):
    m = w.shape[1]
    return np.ascontiguousarray(
        np.asarray(w).reshape(ND, 128, m).transpose(1, 0, 2).reshape(128, ND * m))


def _host_prep(inputs):
    x = np.asarray(inputs["L0_attn_hidden_states"], np.float32)
    npad, xs_l, fw_l, gidx_l = _route_moe(
        inputs["L0_ffn_expert_indices"], inputs["L0_ffn_expert_weights"],
        x.reshape(-1, D))

    w_inT = np.asarray(inputs["L0_attn_in_proj_w"], np.float32)[:C].T
    w_outT = np.asarray(inputs["L0_attn_out_proj_w"], np.float32).T
    w_outT_l = np.ascontiguousarray(
        w_outT.reshape(NCC, 128, D).transpose(1, 0, 2).reshape(128, NCC * D))

    cw = np.asarray(inputs["L0_attn_conv_w"], np.float32)[:, 0, :]
    conv_diag = np.zeros((128, NCC * 4 * 128), np.float32)
    for cc in range(NCC):
        for k in range(4):
            conv_diag[:, (cc * 4 + k) * 128:(cc * 4 + k + 1) * 128] = \
                np.diag(cw[cc * 128:(cc + 1) * 128, k])

    w1 = np.asarray(inputs["L0_ffn_w1"], np.float32)
    w2 = np.asarray(inputs["L0_ffn_w2"], np.float32)
    w1_l = [np.ascontiguousarray(
        w1[e].reshape(ND, 128, NF, 128).transpose(2, 1, 0, 3).reshape(NF, 128, ND * 128))
        for e in range(NCORES)]
    w2_l = [np.ascontiguousarray(w2[e].reshape(NF, 128, D)) for e in range(NCORES)]

    shared = dict(
        w_inT=_lay(w_inT), w_outT=w_outT_l, conv_diag=conv_diag,
        conv_b=np.ascontiguousarray(np.asarray(inputs["L0_attn_conv_b"], np.float32).reshape(NCC, 128).T),
        bias0_row=np.ascontiguousarray(np.asarray(inputs["L0_ffn_bias"], np.float32)[None, :]),
        Wq=_lay(np.asarray(inputs["L1_attn_Wq"], np.float32)),
        Wk=_lay(np.asarray(inputs["L1_attn_Wk"], np.float32)),
        Wv=_lay(np.asarray(inputs["L1_attn_Wv"], np.float32)),
        Wo=_lay(np.asarray(inputs["L1_attn_Wo"], np.float32)),
        bq_row=np.ascontiguousarray(np.asarray(inputs["L1_attn_bq"], np.float32)[None, :]),
        bv_row=np.ascontiguousarray(np.asarray(inputs["L1_attn_bv"], np.float32)[None, :]),
        bo_row=np.ascontiguousarray(np.asarray(inputs["L1_attn_bo"], np.float32)[None, :]),
        lnb_col=_col8(inputs["L1_attn_ln_b"]),
        bk_h64=np.ascontiguousarray(
            np.asarray(inputs["L1_attn_bk"], np.float32).reshape(16, 64).T),
        g_col=_col8(inputs["L1_attn_ln_g"]),
        b_col=_col8(inputs["L1_attn_ln_b"]),
        W8T=np.ascontiguousarray(
            np.asarray(inputs["L1_ffn_weight_int8"]).T.reshape(ND, 128, D)
            .transpose(1, 0, 2).reshape(128, ND * D)),
        wscale_col=_col8(np.asarray(inputs["L1_ffn_weight_scale"], np.float32)[:, 0]),
        bias1_col=_col8(inputs["L1_ffn_bias"]),
        ident=np.eye(128, dtype=np.float32),
        ones_row=np.ones((1, T), np.float32),
    )
    per_core = []
    for i in range(NCORES):
        m = dict(shared)
        m["x"] = np.ascontiguousarray(x[i])
        m["xs"] = xs_l[i]
        m["fw"] = np.ascontiguousarray(fw_l[i].reshape(-1, 128).T)
        m["gidx"] = gidx_l[i]
        m["w1"] = w1_l[i]
        m["w2"] = w2_l[i]
        per_core.append(m)
    return npad, per_core



_ALIAS = {}
for _t, _names in {
    "mtrT": ["l0trT", "l2trT", "gmT", "xsT", "qmmT", "vmmT", "qcT"],
    "mlnA": ["l0A", "l2A", "qlA"],
    "init_v": ["haloz", "vzV", "kbzV", "lnbV", "gmV", "xsV", "xsV2"],
    "mlnBv": ["l0Bv", "l2Bv", "qlBv"],
    "mapC": ["l0apC", "l2apC", "qcC"],
    "mtrV": ["l0trV", "l2trV", "qcV"],
    "mlnB": ["l0B", "l2B", "qlB"],
    "qevS": ["kevS"],
    "dsl0": ["ldxsA0", "ldxsC0", "ldxA0", "ldxC0"],
    "dsl1": ["ldxsA1", "ldxsC1", "ldxA1", "ldxC1"],
    "dsl2": ["ldxA2", "ldxC2"],
    "dsl3": ["ldxA3", "ldxC3"],
    "dslB0": ["ldr0", "ldq0", "ldf0"],
    "dslB1": ["ldr1", "ldq1", "ldf1"],
    "dslB2": ["ldr2", "ldq2", "ldf2"],
    "dslB3": ["ldr3", "ldq3", "ldf3"],
    "stx10": ["stq0", "stf0", "stdb0"],
    "stx11": ["stq1", "stf1", "stdb1"],
    "qmA": ["qmd0", "ldqb0"],
    "qmB": ["qmd1", "ldqb1"],
    "ldWq": ["ldWv", "ldWk", "ldWo"],
    "argi": ["argo"],
    "cc": ["cc2"],
}.items():
    for _n in _names:
        _ALIAS[_n] = _t


class P:
    def __init__(self, nc):
        self.nc = nc
        self._phys = {}
        self._base = {}
        self._guards = []

    def _ph(self, name):
        ph = _ALIAS.get(name, name)
        if ph not in self._phys:
            g = self.nc.semaphore(ph)
            self._phys[ph] = [g.__enter__(), 0]
            self._guards.append(g)
        s = self._phys[ph]
        if name not in self._base:
            self._base[name] = s[1]
        return s, self._base[name]

    def _s(self, name):
        s, b = self._ph(name)
        return [s[0], s[1] - b]

    def inc(self, name, instr, n=1):
        s, b = self._ph(name)
        instr.then_inc(s[0], n)
        s[1] += n
        return s[1] - b

    def val(self, name):
        s, b = self._ph(name)
        return s[1] - b

    def wait(self, eng, name, v=None):
        s, b = self._ph(name)
        eng.wait_ge(s[0], b + ((s[1] - b) if v is None else v))

    def close(self):
        for g in reversed(self._guards):
            g.__exit__(None, None, None)



def build(npad):
    nc = bass.Bass("TRN2", target_bir_lowering=False)
    NPT = npad // 128
    NTG_M = (npad + 511) // 512
    tgw_M = [min(512, npad - 512 * t) for t in range(NTG_M)]

    def din(name, shape, dt=F32):
        return nc.dram_tensor(name, shape, dt, kind="ExternalInput").ap()

    d_x = din("x", [T, D])
    d_xs = din("xs", [npad, D])
    d_fw = din("fw", [128, NPT])
    d_gidx = din("gidx", [128, 32], I32)
    d_winT = din("w_inT", [128, ND * C])
    d_woutT = din("w_outT", [128, NCC * D])
    d_cdiag = din("conv_diag", [128, NCC * 4 * 128])
    d_cb = din("conv_b", [128, NCC])
    d_b0 = din("bias0_row", [1, D])
    d_w1 = din("w1", [NF, 128, ND * 128])
    d_w2 = din("w2", [NF, 128, D])
    d_Wq = din("Wq", [128, ND * D])
    d_Wk = din("Wk", [128, ND * D])
    d_Wv = din("Wv", [128, ND * D])
    d_Wo = din("Wo", [128, ND * D])
    d_bq = din("bq_row", [1, D])
    d_bv = din("bv_row", [1, D])
    d_bo = din("bo_row", [1, D])
    d_lnbc = din("lnb_col", [128, ND])
    d_bk = din("bk_h64", [64, 16])
    d_g = din("g_col", [128, ND])
    d_b = din("b_col", [128, ND])
    d_W8T = din("W8T", [128, ND * D], I8)
    d_wsc = din("wscale_col", [128, ND])
    d_b1 = din("bias1_col", [128, ND])
    d_id = din("ident", [128, 128])
    d_ones = din("ones_row", [1, T])
    d_out = nc.dram_tensor("out", [T, D], F32, kind="ExternalOutput").ap()

    ag_in = nc.dram_tensor("ag_in", [npad, D], F32).ap()
    ag_out = nc.dram_tensor("ag_out", [NCORES * npad, D], F32).ap()
    ar_in = nc.dram_tensor("ar_in", [1, 128], F32).ap()
    ar_out = nc.dram_tensor("ar_out", [1, 128], F32).ap()
    x1_d = nc.dram_tensor("x1_d", [T, D], F32).ap()
    q_d = nc.dram_tensor("q_d", [T, D], F32).ap()
    m_d = nc.dram_tensor("m_d", [2, 16 * 128], BF16).ap()
    qln_d = nc.dram_tensor("qln_d", [128, ND * T], BF16).ap()

    es = ExitStack()
    with es:
        _PH = globals().get("_PHASES", None)
        def mark(name):
            if _PH is not None:
                _PH.append((name, nc.n_instructions()))
        sb = lambda *a: es.enter_context(nc.sbuf_tensor(*a))
        xn2T = sb("xn2T", [128, ND * T], BF16)
        xrot = sb("xrot", [128, 4 * D], F32)
        xorot = sb("xorot", [128, 2 * D], F32)
        ident_f = sb("ident_f", [128, 128], F32)
        ident_b = sb("ident_b", [128, 128], BF16)
        ones_b = sb("ones_b", [1, 512], BF16)
        ones_f = sb("ones_f", [1, 128], F32)
        rows_b = sb("rows_b", [1, 6 * D], BF16)
        bo_f = sb("bo_f", [1, D], F32)
        cols_f = sb("cols_f", [128, 8 * ND], F32)
        cb_c = sb("cb_c", [128, NCC], F32)
        bkh = sb("bkh", [64, 16], F32)
        gmax = sb("gmax", [128, 1], F32)
        armx = sb("armx", [1, 128], F32)
        xsp = sb("xsp", [1, 2], F32)
        xsb = sb("xsb", [128, 2], F32)
        gidx_sb = sb("gidx_sb", [128, 32], I32)
        stats_b = sb("stats_b", [128, 12], F32)
        mva = sb("mva", [128, 2 * NT], F32)
        rsa = sb("rsa", [128, NT], F32)
        ln1 = sb("ln1", [128, NT], F32)
        lnsc = sb("lnsc", [128, D], F32)
        lnbf = sb("lnbf", [128, 2 * D], BF16)
        block = es.enter_context(nc.Block())
        p = P(nc)
        R_B0, R_BQ, R_BV, R_BO, R_LNB, R_WOB = range(6)

        def row(i):
            return rows_b[0:1, i * D:(i + 1) * D]

        C_BK8, C_G, C_B, C_WSC, C_B1, C_SV, C_LNB = range(7)

        def col(i):
            return cols_f[:, i * ND:(i + 1) * ND]

        def g_const(g):
            for dst, src in [(ident_f[:], d_id), (ident_b[:], d_id),
                             (ones_b[:], d_ones[0:1, 0:512]), (ones_f[:], d_ones[0:1, 0:128]),
                             (row(R_B0), d_b0), (row(R_BQ), d_bq),
                             (row(R_BV), d_bv), (row(R_BO), d_bo),
                             (bo_f[:], d_bo),
                             (bkh[:], d_bk), (col(C_G), d_g),
                             (col(C_B), d_b), (col(C_WSC), d_wsc),
                             (col(C_B1), d_b1), (col(C_LNB), d_lnbc),
                             (cb_c[:], d_cb), (gidx_sb[:], d_gidx)]:
                p.inc("cst", g.dma_start(out=dst, in_=src), 16)
        block.gpsimd(g_const)
        CST = p.val("cst")

        def v_init(v):
            p.wait(v, "cst", CST)
            v.tensor_scalar(out=bkh[:], in0=bkh[:], scalar1=0.125,
                            scalar2=None, op0=OP.mult)
            p.inc("init_v", v.engine_nop())
        block.vector(v_init)

        def ln_passA(v, xt, slot):
            v.bn_stats(out=stats_b[:, 0:6], in_=xt[:, 0:512])
            v.bn_stats(out=stats_b[:, 6:12], in_=xt[:, 512:1024])
            v.drain()
            v.bn_aggr(out=mva[:, 2 * slot:2 * slot + 2], in_=stats_b[:, 0:12])
            v.drain()

        def ln_passB(ntiles, tag):
            def fv(v):
                p.wait(v, tag + "A", ntiles)
                v.tensor_scalar(out=ln1[:, 0:ntiles], in0=mva[:, 1:2 * ntiles:2],
                                scalar1=EPS, scalar2=None, op0=OP.add)
                v.drain()
                v.reciprocal(out=rsa[:, 0:ntiles], in_=ln1[:, 0:ntiles])
                p.inc(tag + "Bv", v.engine_nop())
            block.vector(fv)

            def fs(s):
                p.wait(s, tag + "Bv")
                s.activation(out=rsa[:, 0:ntiles], in_=rsa[:, 0:ntiles],
                             func=ACT.Sqrt)
                p.inc(tag + "B", s.activation(out=ln1[:, 0:1], in_=ln1[:, 0:1],
                                              func=ACT.Identity))
            block.scalar(fs)
            return p.val(tag + "B")

        def ln_apply(v, xt, out, slot):
            v.tensor_scalar(out=out, in0=xt, scalar1=mva[:, 2 * slot:2 * slot + 1],
                            scalar2=rsa[:, slot:slot + 1], op0=OP.subtract,
                            op1=OP.mult)

        es0 = ExitStack()
        with es0:
            sb0 = lambda *a: es0.enter_context(nc.sbuf_tensor(*a))
            winT = sb0("winT", [128, ND * C], BF16)
            def g_l0w(g):
                p.inc("ldwin", g.dma_start(out=winT[:], in_=d_winT), 16)
            block.gpsimd(g_l0w)

            esM = ExitStack()
            with esM:
                sbM = lambda *a: esM.enter_context(nc.sbuf_tensor(*a))
                xs_t = sbM("xs_t", [128, 2 * D], F32)
                xsnT = sbM("xsnT", [128, ND * 512], BF16)
                hT = sbM("hT", [128, NF * 512], BF16)
                w1b = sbM("w1b", [128, 2 * ND * 128], BF16)
                w2b = sbM("w2b", [128, 2 * D], BF16)
                w1f = sbM("w1f", [128, 2 * ND * 128], F32)
                w2f = sbM("w2f", [128, 4 * D], F32)
                fw_sb = sbM("fw_sb", [128, NPT], F32)
                ys_sb = sbM("ys_sb", [128, 2 * D], F32)
                psM = esM.enter_context(nc.psum_tensor("psM", [128, 2048], F32))
                psTb = esM.enter_context(nc.psum_tensor("psTb", [128, 2048], BF16))
                def g_mld(g):
                    p.inc("ldfw", g.dma_start(out=fw_sb[:], in_=d_fw), 16)
                block.gpsimd(g_mld)

                mark("M_lnA")
                for i in range(NPT):
                    def sy_ld(sy, i=i):
                        if i >= 2:
                            p.wait(sy, "mlnA", i - 1)
                        p.inc("ldxsA%d" % (i % 2), sy.dma_start(
                            out=xs_t[:, (i % 2) * D:((i % 2) + 1) * D],
                            in_=d_xs[i * 128:(i + 1) * 128, :]), 16)
                    block.sync(sy_ld)

                    def v_st(v, i=i):
                        p.wait(v, "ldxsA%d" % (i % 2), (i // 2 + 1) * 16)
                        ln_passA(v, xs_t[:, (i % 2) * D:((i % 2) + 1) * D], i)
                        p.inc("mlnA", v.engine_nop())
                    block.vector(v_st)
                MLNB = ln_passB(NPT, "mln")

                mark("M_tg")
                XLN_K = [0]
                for tg in range(NTG_M):
                    tw = tgw_M[tg]
                    ntl = tw // 128
                    for ii in range(ntl):
                        i = tg * 4 + ii

                        def sy_ld(sy, i=i):
                            if i < 2:
                                p.wait(sy, "mlnA", NPT)
                            else:
                                p.wait(sy, "mtrV", i - 1)
                            if i >= 2:
                                p.wait(sy, "mapC", i - 1)
                            p.inc("ldxsC%d" % (i % 2), sy.dma_start(
                                out=xs_t[:, (i % 2) * D:((i % 2) + 1) * D],
                                in_=d_xs[i * 128:(i + 1) * 128, :]), 16)
                        block.sync(sy_ld)

                        def v_ap(v, i=i):
                            p.wait(v, "ldxsC%d" % (i % 2), (i // 2 + 1) * 16)
                            p.wait(v, "mlnB", MLNB)
                            if i >= 2:
                                p.wait(v, "mtrT", i - 1)
                            ln_apply(v, xs_t[:, (i % 2) * D:((i % 2) + 1) * D],
                                     lnbf[:, (i % 2) * D:((i % 2) + 1) * D], i)
                            p.inc("mapC", v.engine_nop())
                        block.vector(v_ap)

                        def t_tr(t, i=i):
                            p.wait(t, "mapC", i + 1)
                            p.wait(t, "cst", CST)
                            if i >= 2:
                                p.wait(t, "mtrV", i - 1)
                            if i >= 4:
                                p.wait(t, "mhT", (i // 4) * NF)
                            last = None
                            for dk in range(ND):
                                last = t.transpose(
                                    psTb[:, (i % 2) * 1024 + dk * 128:
                                         (i % 2) * 1024 + (dk + 1) * 128],
                                    lnbf[:, (i % 2) * D + dk * 128:
                                         (i % 2) * D + (dk + 1) * 128], ident_b[:])
                            p.inc("mtrT", last)
                        block.tensor(t_tr)

                        def v_ev(v, i=i, ii=ii):
                            p.wait(v, "mtrT", i + 1)
                            for dk in range(ND):
                                v.tensor_copy(
                                    out=xsnT[:, dk * 512 + ii * 128:
                                             dk * 512 + (ii + 1) * 128],
                                    in_=psTb[:, (i % 2) * 1024 + dk * 128:
                                             (i % 2) * 1024 + (dk + 1) * 128])
                            p.inc("mtrV", v.engine_nop())
                        block.vector(v_ev)

                    def sy_w1(sy, tg=tg):
                        for fc in range(NF):
                            gi = tg * NF + fc
                            if gi >= 2:
                                p.wait(sy, "w1cV", gi - 1)
                            p.inc("ldw1%d" % (fc % 2), sy.dma_start(
                                out=w1f[:, (fc % 2) * 1024:((fc % 2) + 1) * 1024],
                                in_=d_w1[fc]), 16)
                    block.sync(sy_w1)

                    def w1cast(s, gi):
                        fc2 = gi % NF
                        p.wait(s, "ldw1%d" % (fc2 % 2), (gi // 2 + 1) * 16)
                        if gi >= 2:
                            p.wait(s, "mhT", gi - 1)
                        p.inc("w1cV", s.copy(
                            out=w1b[:, (fc2 % 2) * 1024:((fc2 % 2) + 1) * 1024],
                            in_=w1f[:, (fc2 % 2) * 1024:((fc2 % 2) + 1) * 1024]))

                    def t_h(t, tg=tg, tw=tw):
                        p.wait(t, "mtrV", tg * 4 + tw // 128)
                        p.wait(t, "myV", p.val("myV"))
                        for fc in range(NF):
                            gi = tg * NF + fc
                            p.wait(t, "w1cV", gi + 1)
                            if gi >= 2:
                                p.wait(t, "mhS", gi - 1)
                            last = None
                            for dk in range(ND):
                                last = t.matmul(
                                    psM[:, (fc % 2) * 512:(fc % 2) * 512 + tw],
                                    w1b[:, (fc % 2) * 1024 + dk * 128:
                                        (fc % 2) * 1024 + (dk + 1) * 128],
                                    xsnT[:, dk * 512:dk * 512 + tw],
                                    start=(dk == 0), stop=(dk == ND - 1))
                            p.inc("mhT", last)
                    block.tensor(t_h)

                    def s_h(s, tg=tg, tw=tw):
                        if tg == 0:
                            w1cast(s, 0)
                            w1cast(s, 1)
                        if tg >= 1:
                            p.wait(s, "myT", tg * 2 * NF)
                        for fc in range(NF):
                            gi = tg * NF + fc
                            p.wait(s, "mhT", gi + 1)
                            p.inc("mhS", s.activation(
                                out=hT[:, fc * 512:fc * 512 + tw],
                                in_=psM[:, (fc % 2) * 512:(fc % 2) * 512 + tw],
                                func=ACT.Gelu))
                            if fc + 2 < NF:
                                w1cast(s, gi + 2)
                    block.scalar(s_h)

                    nhalf = (ntl + 1) // 2
                    for hf in range(nhalf):
                        tis = [hf * 2 + k for k in range(2) if hf * 2 + k < ntl]

                        def sy_w2(sy, tg=tg, hf=hf):
                            for fb in range(NF // 2):
                                gi = (tg * 2 + hf) * NF // 2 + fb
                                if gi >= 2:
                                    p.wait(sy, "w2cV", (gi - 1) * 2)
                                p.inc("ldw2%d" % (fb % 2), sy.dma_start(
                                    out=w2f[:, (fb % 2) * 2048:((fb % 2) + 1) * 2048
                                            ].rearrange("p (a m) -> p a m", a=2),
                                    in_=d_w2[fb * 2:fb * 2 + 2].rearrange(
                                        "a p m -> p a m")), 16)
                        block.sync(sy_w2)

                        def v_w2(v, tg=tg, hf=hf):
                            for fc in range(NF):
                                fb = fc // 2
                                gi = (tg * 2 + hf) * NF + fc
                                p.wait(v, "ldw2%d" % (fb % 2),
                                       ((tg * 2 + hf) * 8 + fb // 2 + 1) * 16)
                                if gi >= 2:
                                    p.wait(v, "myT", gi - 1)
                                p.inc("w2cV", v.tensor_copy(
                                    out=w2b[:, (fc % 2) * D:((fc % 2) + 1) * D],
                                    in_=w2f[:, (fb % 2) * 2048 + (fc % 2) * D:
                                            (fb % 2) * 2048 + ((fc % 2) + 1) * D]))
                        block.vector(v_w2)

                        def s_w2(s, tg=tg, hf=hf):
                            if hf == nhalf - 1 and tg + 1 < NTG_M:
                                w1cast(s, (tg + 1) * NF)
                                w1cast(s, (tg + 1) * NF + 1)
                        block.scalar(s_w2)

                        def t_y(t, tg=tg, hf=hf, tis=tis):
                            p.wait(t, "mhS", (tg + 1) * NF)
                            p.wait(t, "myV", p.val("myV"))
                            for fc in range(NF):
                                gi = (tg * 2 + hf) * NF + fc
                                p.wait(t, "w2cV", gi + 1)
                                last = None
                                for ti in tis:
                                    for oh in range(2):
                                        last = t.matmul(
                                            psM[:, ((ti % 2) * 2 + oh) * 512:
                                                ((ti % 2) * 2 + oh + 1) * 512],
                                            hT[:, fc * 512 + ti * 128:
                                               fc * 512 + (ti + 1) * 128],
                                            w2b[:, (fc % 2) * D + oh * 512:
                                                (fc % 2) * D + (oh + 1) * 512],
                                            start=(fc == 0), stop=(fc == NF - 1))
                                p.inc("myT", last)
                        block.tensor(t_y)

                        def v_y(v, tg=tg, tis=tis):
                            p.wait(v, "ldfw", 16)
                            p.wait(v, "stys", p.val("stys"))
                            p.wait(v, "myT")
                            for ti in tis:
                                for oh in range(2):
                                    v.tensor_scalar(
                                        out=ys_sb[:, (ti % 2) * D + oh * 512:
                                                  (ti % 2) * D + (oh + 1) * 512],
                                        in0=psM[:, ((ti % 2) * 2 + oh) * 512:
                                                ((ti % 2) * 2 + oh + 1) * 512],
                                        scalar1=fw_sb[:, tg * 4 + ti:tg * 4 + ti + 1],
                                        scalar2=None, op0=OP.mult)
                                p.inc("myV", v.engine_nop())
                        block.vector(v_y)

                        def sy_y(sy, tg=tg, tis=tis):
                            p.wait(sy, "myV")
                            for ti in tis:
                                r0 = tg * 512 + ti * 128
                                p.inc("stys", sy.dma_start(
                                    out=ag_in[r0:r0 + 128, :],
                                    in_=ys_sb[:, (ti % 2) * D:((ti % 2) + 1) * D]),
                                    16)
                        block.sync(sy_y)

                    xln_sched = [i for i in range(NT)
                                 if (i // 4 if i < NPT else i % NTG_M) == tg]
                    for i in xln_sched:
                        k = XLN_K[0]
                        XLN_K[0] += 1

                        def g_xld(g, i=i, k=k):
                            if k >= 4:
                                p.wait(g, "l0A", k - 3)
                            p.inc("xla%d" % (k % 4), g.dma_start(
                                out=xrot[:, (k % 4) * D:((k % 4) + 1) * D],
                                in_=d_x[i * 128:(i + 1) * 128, :]), 16)
                        block.gpsimd(g_xld)

                        def v_xst(v, i=i, k=k):
                            p.wait(v, "xla%d" % (k % 4), (k // 4 + 1) * 16)
                            ln_passA(v, xrot[:, (k % 4) * D:((k % 4) + 1) * D], i)
                            p.inc("l0A", v.engine_nop())
                        block.vector(v_xst)

                mark("M_ag")
                def g_ag(g):
                    for k in range(NTG_M):
                        c0 = k * 512
                        ck = min(512, npad - c0)
                        p.wait(g, "stys", ((c0 + ck) // 128) * 16)
                        p.inc("cc", g.collective_compute(
                            "AllGather", OP.bypass,
                            replica_groups=[list(range(NCORES))],
                            ins=[ag_in[c0:c0 + ck, :]],
                            outs=[ag_out[NCORES * c0:NCORES * (c0 + ck), :]]))
                block.gpsimd(g_ag)

            esL = ExitStack()
            with esL:
                sbL = lambda *a: esL.enter_context(nc.sbuf_tensor(*a))
                xnT = sbL("xnT", [128, ND * 512], BF16)
                woutT = sbL("woutT", [128, NCC * D], BF16)
                cdiag = sbL("cdiag", [128, NCC * 4 * 128], BF16)
                y_sb = sbL("y_sb", [128, NCC * 512], BF16)
                xz_sb = sbL("xz_sb", [128, 2 * 520], BF16)
                halo = sbL("halo", [128, NCC * 4], BF16)
                plane = sbL("plane", [128, 4 * D], F32)
                psZ = esL.enter_context(nc.psum_tensor("psZ", [128, 1024], F32))
                psC2 = esL.enter_context(nc.psum_tensor("psC2", [128, 1024], F32))
                psO = esL.enter_context(nc.psum_tensor("psO", [128, 1024], F32))
                psT0 = esL.enter_context(nc.psum_tensor("psT0", [128, 2048], BF16))
                def g_l0w2(g):
                    p.wait(g, "myV", p.val("myV"))
                    p.wait(g, "stys", p.val("stys"))
                    p.inc("ldcd", g.dma_start(out=cdiag[:], in_=d_cdiag), 16)
                    p.inc("ldwo", g.dma_start(out=woutT[:], in_=d_woutT), 16)
                block.gpsimd(g_l0w2)

                mark("L0_lnA")
                L0B = ln_passB(NT, "l0")

                def v_hz(v):
                    p.wait(v, "cst", CST)
                    p.wait(v, "stys", p.val("stys"))
                    p.wait(v, "myV", p.val("myV"))
                    v.memset(halo[:], 0.0)
                    p.inc("haloz", v.engine_nop())
                block.vector(v_hz)

                mark("L0_tg")
                for tg in range(4):
                    for ii in range(4):
                        i = tg * 4 + ii

                        def sy_ld(sy, i=i):
                            if i < 4:
                                p.wait(sy, "l0A", 13 + i)
                            else:
                                p.wait(sy, "l0asm", i - 3)
                            p.inc("ldxC%d" % (i % 4), sy.dma_start(
                                out=xrot[:, (i % 4) * D:((i % 4) + 1) * D],
                                in_=d_x[i * 128:(i + 1) * 128, :]), 16)
                        block.sync(sy_ld)

                        def v_ap(v, i=i):
                            p.wait(v, "ldxC%d" % (i % 4), (i // 4 + 1) * 16)
                            p.wait(v, "l0B", L0B)
                            if i >= 2:
                                p.wait(v, "l0trT", i - 1)
                            if _STAGE == 0 and i >= 2:
                                p.wait(v, "stdbg%d" % (i % 2), (i // 2) * 16)
                            ln_apply(v, xrot[:, (i % 4) * D:((i % 4) + 1) * D],
                                     lnbf[:, (i % 2) * D:((i % 2) + 1) * D]
                                     if _STAGE != 0 else
                                     xorot[:, (i % 2) * D:((i % 2) + 1) * D], i)
                            if _STAGE == 0:
                                v.drain()
                                v.tensor_copy(out=lnbf[:, (i % 2) * D:
                                                       ((i % 2) + 1) * D],
                                              in_=xorot[:, (i % 2) * D:((i % 2) + 1) * D])
                            p.inc("l0apC", v.engine_nop())
                        block.vector(v_ap)

                        if _STAGE == 0:
                            def sy_dbg(sy, i=i):
                                p.wait(sy, "l0apC", i + 1)
                                p.inc("stdbg%d" % (i % 2), sy.dma_start(
                                    out=d_out[i * 128:(i + 1) * 128, :],
                                    in_=xorot[:, (i % 2) * D:((i % 2) + 1) * D]), 16)
                            block.sync(sy_dbg)

                        def t_tr(t, i=i):
                            p.wait(t, "l0apC", i + 1)
                            if i >= 2:
                                p.wait(t, "l0trV", i - 1)
                            if i == 0:
                                p.wait(t, "myV", p.val("myV"))
                            if i >= 4:
                                p.wait(t, "l0zT", (i // 4) * NCC)
                            last = None
                            for dk in range(ND):
                                last = t.transpose(
                                    psT0[:, (i % 2) * 1024 + dk * 128:
                                         (i % 2) * 1024 + (dk + 1) * 128],
                                    lnbf[:, (i % 2) * D + dk * 128:
                                         (i % 2) * D + (dk + 1) * 128], ident_b[:])
                            p.inc("l0trT", last)
                        block.tensor(t_tr)

                        def v_ev(v, i=i, ii=ii):
                            p.wait(v, "l0trT", i + 1)
                            for dk in range(ND):
                                v.tensor_copy(
                                    out=xnT[:, dk * 512 + ii * 128:
                                            dk * 512 + (ii + 1) * 128],
                                    in_=psT0[:, (i % 2) * 1024 + dk * 128:
                                             (i % 2) * 1024 + (dk + 1) * 128])
                            p.inc("l0trV", v.engine_nop())
                        block.vector(v_ev)

                    def t_yp(t, tg=tg):
                        p.wait(t, "l0trV", tg * 4 + 4)
                        if tg == 0:
                            p.wait(t, "ldwin", 16)
                            p.wait(t, "ldcd", 16)
                        for cc in range(NCC):
                            gi = tg * NCC + cc
                            if gi >= 2:
                                p.wait(t, "l0xz", 2 * (gi - 1))
                            last = None
                            for dk in range(ND):
                                last = t.matmul(
                                    psZ[:, (cc % 2) * 512:((cc % 2) + 1) * 512],
                                    winT[:, dk * C + cc * 128:dk * C + (cc + 1) * 128],
                                    xnT[:, dk * 512:(dk + 1) * 512],
                                    start=(dk == 0), stop=(dk == ND - 1))
                            p.inc("l0zT", last)
                            p.wait(t, "l0xz", 2 * gi + 2)
                            if gi >= 2:
                                p.wait(t, "l0sil", gi - 1)
                            last = None
                            for k in range(4):
                                last = t.matmul(
                                    psC2[:, (cc % 2) * 512:((cc % 2) + 1) * 512],
                                    cdiag[:, (cc * 4 + k) * 128:(cc * 4 + k + 1) * 128],
                                    xz_sb[:, (cc % 2) * 520 + k:(cc % 2) * 520 + k + 512],
                                    start=(k == 0), stop=(k == 3))
                            p.inc("l0cvT", last)
                    block.tensor(t_yp)

                    def s_yp(s, tg=tg):
                        for cc in range(NCC):
                            gi = tg * NCC + cc
                            p.wait(s, "l0zT", gi + 1)
                            if gi >= 2:
                                p.wait(s, "l0cvT", gi - 1)
                            if tg == 0 and cc == 0:
                                p.wait(s, "haloz", 1)
                            if tg >= 1 and cc == 0:
                                p.wait(s, "l0opT", tg * 8)
                            s.activation(out=xz_sb[:, (cc % 2) * 520 + 3:
                                                   (cc % 2) * 520 + 515],
                                         in_=psZ[:, (cc % 2) * 512:((cc % 2) + 1) * 512],
                                         func=ACT.Identity)
                            s.copy(out=xz_sb[:, (cc % 2) * 520:(cc % 2) * 520 + 3],
                                   in_=halo[:, cc * 4:cc * 4 + 3])
                            s.drain()
                            p.inc("l0xz", s.copy(
                                out=halo[:, cc * 4:cc * 4 + 3],
                                in_=xz_sb[:, (cc % 2) * 520 + 512:
                                          (cc % 2) * 520 + 515]), 2)
                            p.wait(s, "l0cvT", gi + 1)
                            p.inc("l0sil", s.activation(
                                out=y_sb[:, cc * 512:(cc + 1) * 512],
                                in_=psC2[:, (cc % 2) * 512:((cc % 2) + 1) * 512],
                                func=ACT.Silu, bias=cb_c[:, cc:cc + 1], scale=1.0))
                    block.scalar(s_yp)

                    def g_pl(g, tg=tg):
                        if _STAGE >= 2:
                            for ii in range(4):
                                tt = tg * 4 + ii
                                p.wait(g, "cc", NTG_M)
                                if tt >= 2:
                                    p.wait(g, "l0asm", tt - 1)
                                for pl in range(2):
                                    p.inc("gpl%d" % (tt % 2), g.indirect_dma_start(
                                        out=plane[:, ((tt % 2) * 2 + pl) * D:
                                                  ((tt % 2) * 2 + pl + 1) * D],
                                        out_offset=None,
                                        in_=ag_out,
                                        in_offset=IndirectOffsetOnAxis(
                                            gidx_sb[:, pl * 16 + tt:pl * 16 + tt + 1],
                                            0)), 16)
                    block.gpsimd(g_pl)

                    def t_op(t, tg=tg):
                        p.wait(t, "l0sil", (tg + 1) * NCC)
                        if tg == 0:
                            p.wait(t, "ldwo", 16)
                            p.wait(t, "cst", CST)
                        for ii in range(4):
                            tt = tg * 4 + ii
                            for oh in range(2):
                                if tt >= 1:
                                    p.wait(t, "l0asV", 2 * (tt - 1) + oh + 1)
                                last = None
                                for cc in range(NCC):
                                    last = t.matmul(
                                        psO[:, oh * 512:(oh + 1) * 512],
                                        y_sb[:, cc * 512 + ii * 128:
                                             cc * 512 + (ii + 1) * 128],
                                        woutT[:, cc * D + oh * 512:
                                              cc * D + (oh + 1) * 512],
                                        start=(cc == 0), stop=False)
                                last = t.matmul(
                                    psO[:, oh * 512:(oh + 1) * 512],
                                    ones_b[0:1, 0:128],
                                    rows_b[0:1, R_B0 * D + oh * 512:
                                           R_B0 * D + (oh + 1) * 512],
                                    start=False, stop=True)
                                p.inc("l0opT", last)
                    block.tensor(t_op)

                    def v_asm(v, tg=tg):
                        for ii in range(4):
                            tt = tg * 4 + ii
                            p.wait(v, "ldxC%d" % (tt % 4), (tt // 4 + 1) * 16)
                            if tt >= 2:
                                p.wait(v, "stx1%d" % (tt % 2), (tt // 2) * 16)
                            for oh in range(2):
                                p.wait(v, "l0opT", 2 * tt + oh + 1)
                                v.tensor_tensor(
                                    out=xorot[:, (tt % 2) * D + oh * 512:
                                              (tt % 2) * D + (oh + 1) * 512],
                                    in0=psO[:, oh * 512:(oh + 1) * 512],
                                    in1=xrot[:, (tt % 4) * D + oh * 512:
                                             (tt % 4) * D + (oh + 1) * 512],
                                    op=OP.add)
                                p.inc("l0asV", v.engine_nop())
                            v.drain()
                            if _STAGE >= 2:
                                p.wait(v, "gpl%d" % (tt % 2), (tt // 2 + 1) * 32)
                                for pl in range(2):
                                    v.tensor_tensor(
                                        out=xorot[:, (tt % 2) * D:((tt % 2) + 1) * D],
                                        in0=xorot[:, (tt % 2) * D:((tt % 2) + 1) * D],
                                        in1=plane[:, ((tt % 2) * 2 + pl) * D:
                                                  ((tt % 2) * 2 + pl + 1) * D],
                                        op=OP.add)
                                    v.drain()
                            if _STAGE >= 3:
                                ln_passA(v, xorot[:, (tt % 2) * D:((tt % 2) + 1) * D],
                                         tt)
                                p.inc("l2A", v.engine_nop())
                            p.inc("l0asm", v.engine_nop())
                    block.vector(v_asm)

                    def sy_st(sy, tg=tg):
                        for ii in range(4):
                            tt = tg * 4 + ii
                            p.wait(sy, "l0asm", tt + 1)
                            dst = d_out if _STAGE in (1, 2) else x1_d
                            p.inc("stx1%d" % (tt % 2), sy.dma_start(
                                out=dst[tt * 128:(tt + 1) * 128, :],
                                in_=xorot[:, (tt % 2) * D:((tt % 2) + 1) * D]), 16)
                    block.sync(sy_st)

        if _STAGE <= 2:
            def sy_fin(sy):
                p.wait(sy, "stx10", p.val("stx10"))
                p.wait(sy, "stx11", p.val("stx11"))
            block.sync(sy_fin)
            p.close()
            return nc

        mark("L1")
        _build_l1(nc, p, block, dict(
            d_x=d_x, d_Wq=d_Wq, d_Wk=d_Wk, d_Wv=d_Wv, d_Wo=d_Wo, d_W8T=d_W8T,
            d_out=d_out, x1_d=x1_d, q_d=q_d, qln_d=qln_d,
            ar_in=ar_in, ar_out=ar_out, m_d=m_d,
            xn2T=xn2T, xrot=xrot, xorot=xorot, ident_f=ident_f, ident_b=ident_b,
            ones_b=ones_b, ones_f=ones_f, rows_b=rows_b, bo_f=bo_f,
            cols_f=cols_f, gmax=gmax, armx=armx, xsp=xsp, xsb=xsb,
            stats_b=stats_b, mva=mva, rsa=rsa, ln1=ln1, lnsc=lnsc, lnbf=lnbf,
            bkh=bkh,
            row=row, col=col, CST=CST,
            R_B0=R_B0, R_BQ=R_BQ, R_BV=R_BV, R_BO=R_BO, R_LNB=R_LNB,
            R_WOB=R_WOB, C_BK8=C_BK8, C_G=C_G, C_B=C_B, C_WSC=C_WSC,
            C_B1=C_B1, C_SV=C_SV, C_LNB=C_LNB,
            ln_passA=ln_passA, ln_passB=ln_passB, ln_apply=ln_apply,
            mark=mark))
        p.close()
    return nc


def _build_l1(nc, p, block, e):
    g = dict(e)
    d_Wq, d_Wk, d_Wv, d_Wo, d_W8T = g["d_Wq"], g["d_Wk"], g["d_Wv"], g["d_Wo"], g["d_W8T"]
    d_out, x1_d, q_d, qln_d = g["d_out"], g["x1_d"], g["q_d"], g["qln_d"]
    ar_in, ar_out = g["ar_in"], g["ar_out"]
    m_d = g["m_d"]
    xn2T, xrot, xorot = g["xn2T"], g["xrot"], g["xorot"]
    ident_f, ident_b, ones_b, ones_f = g["ident_f"], g["ident_b"], g["ones_b"], g["ones_f"]
    rows_b, bo_f, cols_f = g["rows_b"], g["bo_f"], g["cols_f"]
    gmax, armx, xsp, xsb = g["gmax"], g["armx"], g["xsp"], g["xsb"]
    mva, rsa, ln1, lnsc, lnbf = g["mva"], g["rsa"], g["ln1"], g["lnsc"], g["lnbf"]
    bkh = g["bkh"]
    row, col, CST = g["row"], g["col"], g["CST"]
    R_BQ, R_BV, R_BO, R_WOB = g["R_BQ"], g["R_BV"], g["R_BO"], g["R_WOB"]
    C_BK8, C_G, C_B, C_WSC, C_B1, C_SV, C_LNB = (
        g["C_BK8"], g["C_G"], g["C_B"], g["C_WSC"], g["C_B1"], g["C_SV"], g["C_LNB"])
    ln_passA, ln_passB, ln_apply = g["ln_passA"], g["ln_passB"], g["ln_apply"]

    es0 = ExitStack()
    with es0:
        attn_acc = es0.enter_context(
            nc.sbuf_tensor("attn_acc", [128, NT * D], BF16))
        wfull = es0.enter_context(
            nc.sbuf_tensor("wfull", [128, ND * D], BF16))
        lnbc_b = es0.enter_context(nc.sbuf_tensor("lnbc_b", [128, ND], BF16))

        esA = ExitStack()
        with esA:
            sbA = lambda *a: esA.enter_context(nc.sbuf_tensor(*a))
            ktp = sbA("ktp", [64, T], BF16)
            VAUG = sbA("VAUG", [128, NT * H * 65], BF16)
            QM = sbA("QM", [65, 2 * T], BF16)
            KB = sbA("KB", [65, 2 * T], BF16)
            PT = sbA("PT", [128, 4 * D], BF16)
            qst = PT
            auT = sbA("auT", [80, 2 * T], BF16)
            blk = sbA("blk", [128, 2 * 16 * 80], BF16)

            nrmA = sbA("nrmA", [128, 32], F32)
            negm = sbA("negm", [128, 16], F32)
            rec_t = sbA("rec_t", [128, 16], F32)
            mst = sbA("mst", [16, 2 * 128], BF16)
            esQ = ExitStack()
            psQ = esQ.enter_context(nc.psum_tensor("psQ", [128, 1024], F32))
            psTq = esQ.enter_context(nc.psum_tensor("psTq", [128, 2048], BF16))

            L0ASM = p.val("l0asm")

            L2B = ln_passB(NT, "l2")

            e["mark"]("LN2C")
            for i in range(NT):
                def sy_ld(sy, i=i):
                    p.wait(sy, "stx1%d" % (i % 2), (i // 2 + 1) * 16)
                    if i >= 4:
                        p.wait(sy, "l2apC", i - 3)
                    p.inc("ldr%d" % (i % 4), sy.dma_start(
                        out=xrot[:, (i % 4) * D:((i % 4) + 1) * D],
                        in_=x1_d[i * 128:(i + 1) * 128, :]), 16)
                block.sync(sy_ld)

                def v_ap(v, i=i):
                    p.wait(v, "ldr%d" % (i % 4), (i // 4 + 1) * 16)
                    p.wait(v, "l2B", L2B)
                    if i >= 2:
                        p.wait(v, "l2trT", i - 1)
                    if _STAGE == 3 and i >= 1:
                        p.wait(v, "stdb%d" % ((i - 1) % 2), ((i - 1) // 2 + 1) * 16)
                    ln_apply(v, xrot[:, (i % 4) * D:((i % 4) + 1) * D],
                             lnbf[:, (i % 2) * D:((i % 2) + 1) * D], i)
                    v.drain()
                    v.tensor_reduce(out=ln1[:, 0:1],
                                    in_=lnbf[:, (i % 2) * D:((i % 2) + 1) * D],
                                    axis=AX,
                                    op=OP.max, apply_absolute_value=True)
                    v.drain()
                    if i == 0:
                        v.tensor_copy(out=gmax[:], in_=ln1[:, 0:1])
                    else:
                        v.tensor_tensor(out=gmax[:], in0=gmax[:],
                                        in1=ln1[:, 0:1], op=OP.max)
                    if _STAGE == 3:
                        v.tensor_copy(out=lnsc[:],
                                      in_=lnbf[:, (i % 2) * D:((i % 2) + 1) * D])
                    p.inc("l2apC", v.engine_nop())
                block.vector(v_ap)

                def t_tr(t, i=i):
                    p.wait(t, "l2apC", i + 1)
                    if i >= 2:
                        p.wait(t, "l2trV", i - 1)
                    if i == 0:
                        p.wait(t, "l0asm", L0ASM)
                    last = None
                    for dk in range(ND):
                        last = t.transpose(
                            psTq[:, (i % 2) * 1024 + dk * 128:
                                 (i % 2) * 1024 + (dk + 1) * 128],
                            lnbf[:, (i % 2) * D + dk * 128:
                                 (i % 2) * D + (dk + 1) * 128], ident_b[:])
                    p.inc("l2trT", last)
                block.tensor(t_tr)

                def v_ev(v, i=i):
                    p.wait(v, "l2trT", i + 1)
                    for dk in range(ND):
                        v.tensor_copy(
                            out=xn2T[:, dk * T + i * 128:dk * T + (i + 1) * 128],
                            in_=psTq[:, (i % 2) * 1024 + dk * 128:
                                     (i % 2) * 1024 + (dk + 1) * 128])
                    p.inc("l2trV", v.engine_nop())
                block.vector(v_ev)

                if _STAGE == 3:
                    def sy_dbg(sy, i=i):
                        p.wait(sy, "l2trV", i + 1)
                        p.inc("stdb%d" % (i % 2), sy.dma_start(
                            out=d_out[i * 128:(i + 1) * 128, :],
                            in_=lnsc[:]), 16)
                    block.sync(sy_dbg)

            if _STAGE == 3:
                def sy_fin(sy):
                    p.wait(sy, "stdb0", p.val("stdb0"))
                    p.wait(sy, "stdb1", p.val("stdb1"))
                block.sync(sy_fin)
                return


            e["mark"]("QP")
            def g_wq(gg):
                p.wait(gg, "l0opT", p.val("l0opT"))
                p.inc("ldWq", gg.dma_start(out=wfull[:], in_=d_Wq), 16)
            block.gpsimd(g_wq)

            for tt in range(NT):
                def t_q(t, tt=tt):
                    p.wait(t, "ldWq", 16)
                    p.wait(t, "l2trV", tt + 1)
                    if tt == 0:
                        p.wait(t, "cst", CST)
                    if tt >= 1:
                        p.wait(t, "qevS", tt)
                    for dh in range(2):
                        last = None
                        for dk in range(ND):
                            last = t.matmul(
                                psQ[:, dh * 512:(dh + 1) * 512],
                                xn2T[:, dk * T + tt * 128:dk * T + (tt + 1) * 128],
                                wfull[:, dk * D + dh * 512:dk * D + (dh + 1) * 512],
                                start=(dk == 0), stop=False)
                        last = t.matmul(
                            psQ[:, dh * 512:(dh + 1) * 512],
                            ones_b[0:1, 0:128],
                            rows_b[0:1, R_BQ * D + dh * 512:R_BQ * D + (dh + 1) * 512],
                            start=False, stop=True)
                    p.inc("qmmT", last)
                block.tensor(t_q)

                def s_q(s, tt=tt):
                    p.wait(s, "qmmT", tt + 1)
                    if tt >= 2:
                        p.wait(s, "stq%d" % (tt % 2), (tt // 2) * 16)
                    p.inc("qevS", s.activation(
                        out=xorot[:, (tt % 2) * D:((tt % 2) + 1) * D],
                        in_=psQ[:, 0:1024], func=ACT.Identity))
                block.scalar(s_q)

                def v_q(v, tt=tt):
                    p.wait(v, "qevS", tt + 1)
                    ln_passA(v, xorot[:, (tt % 2) * D:((tt % 2) + 1) * D], tt)
                    p.inc("qlA", v.engine_nop())
                block.vector(v_q)

                def sy_q(sy, tt=tt):
                    p.wait(sy, "qlA", tt + 1)
                    p.inc("stq%d" % (tt % 2), sy.dma_start(
                        out=q_d[tt * 128:(tt + 1) * 128, :],
                        in_=xorot[:, (tt % 2) * D:((tt % 2) + 1) * D]), 16)
                block.sync(sy_q)

            QLB = ln_passB(NT, "ql")

            e["mark"]("VP")
            def g_wv(gg):
                p.wait(gg, "qmmT", NT)
                p.inc("ldWv", gg.dma_start(out=wfull[:], in_=d_Wv), 16)
            block.gpsimd(g_wv)

            def v_vz(v):
                p.wait(v, "cst", CST)
                v.memset(VAUG[:, 64:NT * H * 65:65], 1.0)
                p.inc("vzV", v.engine_nop())
            block.vector(v_vz)

            for tt in range(NT):
                def t_v(t, tt=tt):
                    p.wait(t, "ldWv", 16)
                    p.wait(t, "l2trV", NT)
                    if tt == 0:
                        p.wait(t, "qevS", NT)
                    if tt >= 1:
                        p.wait(t, "vevV", tt)
                    for dh in range(2):
                        last = None
                        for dk in range(ND):
                            last = t.matmul(
                                psQ[:, dh * 512:(dh + 1) * 512],
                                xn2T[:, dk * T + tt * 128:dk * T + (tt + 1) * 128],
                                wfull[:, dk * D + dh * 512:dk * D + (dh + 1) * 512],
                                start=(dk == 0), stop=False)
                        last = t.matmul(
                            psQ[:, dh * 512:(dh + 1) * 512],
                            ones_b[0:1, 0:128],
                            rows_b[0:1, R_BV * D + dh * 512:R_BV * D + (dh + 1) * 512],
                            start=False, stop=True)
                    p.inc("vmmT", last)
                block.tensor(t_v)

                def v_v(v, tt=tt):
                    p.wait(v, "vmmT", tt + 1)
                    p.wait(v, "vzV", 1)
                    for dh in range(2):
                        base = (tt * H + dh * 8) * 65
                        dst = VAUG[:, base:base + 8 * 65].rearrange(
                            "p (a b) -> p a b", b=65)[:, :, 0:64]
                        srcp = psQ[:, dh * 512:(dh + 1) * 512].rearrange(
                            "p (a b) -> p a b", b=64)
                        v.tensor_copy(out=dst, in_=srcp)
                    p.inc("vevV", v.engine_nop())
                block.vector(v_v)

            def t_gm(t):
                p.wait(t, "l2apC", NT)
                p.wait(t, "vevV", p.val("vevV"))
                p.inc("gmT", t.transpose(psQ[0:1, 0:128], gmax[:, 0:1],
                                         ident_f[:]))
            block.tensor(t_gm)

            def v_gm(v):
                p.wait(v, "gmT", 1)
                v.tensor_copy(out=armx[:], in_=psQ[0:1, 0:128])
                p.inc("gmV", v.engine_nop())
            block.vector(v_gm)

            def sy_gm(sy):
                p.wait(sy, "gmV", 1)
                p.inc("argi", sy.dma_start(out=ar_in, in_=armx[:]), 16)
            block.sync(sy_gm)

            def g_ar(gg):
                p.wait(gg, "argi", 16)
                p.inc("cc2", gg.collective_compute(
                    "AllReduce", OP.max, replica_groups=[list(range(NCORES))],
                    ins=[ar_in], outs=[ar_out]))
            block.gpsimd(g_ar)

            def sy_gm2(sy):
                p.wait(sy, "cc2", 1)
                p.inc("argo", sy.dma_start(out=armx[:], in_=ar_out), 16)
            block.sync(sy_gm2)

            def v_xs(v):
                p.wait(v, "argo", 16)
                v.tensor_reduce(out=xsp[0:1, 0:1], in_=armx[:], axis=AX,
                                op=OP.max)
                v.drain()
                v.tensor_scalar(out=xsp[0:1, 0:1], in0=xsp[0:1, 0:1],
                                scalar1=1.0 / 127.0, scalar2=1e-5,
                                op0=OP.mult, op1=OP.max)
                v.drain()
                v.reciprocal(out=xsp[0:1, 1:2], in_=xsp[0:1, 0:1])
                p.inc("xsV", v.engine_nop())
            block.vector(v_xs)

            def t_xs(t):
                p.wait(t, "xsV", 1)
                p.inc("xsT", t.matmul(psQ[:, 0:2], ones_f[0:1, 0:128],
                                      xsp[0:1, 0:2], start=True, stop=True))
            block.tensor(t_xs)

            def v_xs2(v):
                p.wait(v, "xsT", 1)
                v.tensor_copy(out=xsb[:], in_=psQ[:, 0:2])
                v.drain()
                v.tensor_scalar(out=col(C_SV), in0=col(C_WSC),
                                scalar1=xsb[:, 0:1], scalar2=None, op0=OP.mult)
                v.drain()
                p.inc("xsV2", v.engine_nop())
            block.vector(v_xs2)

            e["mark"]("QC")
            for i in range(NT):
                def sy_ld(sy, i=i):
                    p.wait(sy, "stq%d" % (i % 2), (i // 2 + 1) * 16)
                    if i >= 4:
                        p.wait(sy, "qcC", i - 3)
                    p.inc("ldq%d" % (i % 4), sy.dma_start(
                        out=xrot[:, (i % 4) * D:((i % 4) + 1) * D],
                        in_=q_d[i * 128:(i + 1) * 128, :]), 16)
                block.sync(sy_ld)

                def v_ap(v, i=i):
                    p.wait(v, "ldq%d" % (i % 4), (i // 4 + 1) * 16)
                    p.wait(v, "qlB", QLB)
                    if i >= 2:
                        p.wait(v, "qcT", i - 1)
                    ln_apply(v, xrot[:, (i % 4) * D:((i % 4) + 1) * D],
                             lnbf[:, (i % 2) * D:((i % 2) + 1) * D], i)
                    p.inc("qcC", v.engine_nop())
                block.vector(v_ap)

                def t_tr(t, i=i):
                    p.wait(t, "qcC", i + 1)
                    if i >= 2:
                        p.wait(t, "qcV", i - 1)
                    last = None
                    for dk in range(ND):
                        last = t.transpose(
                            psTq[:, (i % 2) * 1024 + dk * 128:
                                 (i % 2) * 1024 + (dk + 1) * 128],
                            lnbf[:, (i % 2) * D + dk * 128:
                                 (i % 2) * D + (dk + 1) * 128], ident_b[:])
                    p.inc("qcT", last)
                block.tensor(t_tr)

                def v_ev(v, i=i):
                    p.wait(v, "qcT", i + 1)
                    if i >= 2:
                        p.wait(v, "stqn%d" % (i % 2), (i // 2) * 16)
                    for dk in range(ND):
                        v.tensor_scalar(
                            out=qst[:, (i % 2) * D + dk * 128:
                                    (i % 2) * D + (dk + 1) * 128],
                            in0=psTq[:, (i % 2) * 1024 + dk * 128:
                                     (i % 2) * 1024 + (dk + 1) * 128],
                            scalar1=col(C_G)[:, dk:dk + 1],
                            scalar2=col(C_B)[:, dk:dk + 1],
                            op0=OP.mult, op1=OP.add)
                    p.inc("qcV", v.engine_nop())
                block.vector(v_ev)

                def sy_st(sy, i=i):
                    p.wait(sy, "qcV", i + 1)
                    dst = qln_d.rearrange("p (a b) -> p a b", a=ND)[
                        :, :, i * 128:(i + 1) * 128]
                    srcq = qst[:, (i % 2) * D:((i % 2) + 1) * D].rearrange(
                        "p (a b) -> p a b", b=128)
                    p.inc("stqn%d" % (i % 2), sy.dma_start(out=dst, in_=srcq), 16)
                block.gpsimd(sy_st)

            def g_wk(gg):
                p.wait(gg, "vmmT", NT)
                p.inc("ldWk", gg.dma_start(out=wfull[:], in_=d_Wk), 16)
            block.gpsimd(g_wk)

            def v_kbz(v):
                v.memset(KB[64:65, 0:T], 1.0)
                v.memset(KB[64:65, T:2 * T], 1.0)
                v.memset(auT[64:80, :], 0.0)
                p.inc("kbzV", v.engine_nop())
            block.vector(v_kbz)

            esQ.close()
            esB = ExitStack()
            with esB:
                psA = esB.enter_context(nc.psum_tensor("psA", [128, 1024], F32))
                psSS = esB.enter_context(nc.psum_tensor("psSS", [128, 2048], F32))
                psS0 = psSS
                psV = esB.enter_context(nc.psum_tensor("psV", [65, 1024], F32))

                e["mark"]("ATT")
                def kT_pieces(ha):
                    ps = []
                    for ts in range(4):
                        def f(t, ha=ha, ts=ts):
                            if ts == 0:
                                p.wait(t, "ldWk", 16)
                                if ha == 0:
                                    p.wait(t, "qcV", NT)
                                    p.wait(t, "vevV", NT)
                                    p.wait(t, "xsV2", 1)
                                else:
                                    p.wait(t, "mstV", ha)
                                    p.wait(t, "kbV", ha)
                                    p.wait(t, "kevS", ha * 4)
                            if ts >= 2:
                                p.wait(t, "kevS", ha * 4 + ts - 1)
                            last = None
                            for dk in range(ND):
                                last = t.matmul(
                                    psA[0:64, (ts % 2) * 512:((ts % 2) + 1) * 512],
                                    wfull[:, dk * D + ha * 64:dk * D + ha * 64 + 64],
                                    xn2T[:, dk * T + ts * 512:dk * T + (ts + 1) * 512],
                                    start=(dk == 0), stop=(dk == ND - 1))
                            p.inc("kmm2", last)
                        ps.append(f)
                    return ps

                def a_pieces(ha):
                    ps = []
                    for qt in range(NT):
                        for sub in range(2):
                            def f(t, ha=ha, qt=qt, sub=sub):
                                if qt + sub == 0:
                                    p.wait(t, "qmd%d" % (ha % 2),
                                           (ha // 2 + 1) * 16)
                                    p.wait(t, "kevS", (ha + 1) * 4)
                                p.wait(t, "rmV", ha * 32 + qt * 2 + sub)
                                for jj in range(2):
                                    last = t.matmul(
                                        psA[:, jj * 512:(jj + 1) * 512],
                                        QM[0:64, (ha % 2) * T + qt * 128:
                                           (ha % 2) * T + (qt + 1) * 128],
                                        ktp[0:64,
                                            sub * 1024 + jj * 512:
                                            sub * 1024 + (jj + 1) * 512],
                                        start=True, stop=True)
                                p.inc("aT", last)
                            ps.append(f)

                    def fn(t, ha=ha):
                        p.wait(t, "rmhV", ha + 1)
                        p.inc("nmT", t.transpose(psA[0:16, 0:128],
                                                 negm[:, 0:16], ident_f[:]))
                    ps.append(fn)
                    return ps

                for h in range(-1, H):
                    ha = h + 1
                    e["mark"]("slot%d" % h)

                    def t_slot(t, h=h, ha=ha):
                        pieces = []
                        if ha < H:
                            pieces = kT_pieces(ha)[2:] + a_pieces(ha)
                        if ha + 1 < H:
                            pieces = pieces + kT_pieces(ha + 1)[:2]
                        if h < 0:
                            for f in kT_pieces(0)[:2]:
                                f(t)
                            for f in pieces:
                                f(t)
                            return
                        p.wait(t, "qmr%d" % (h % 2), (h // 2 + 1) * 16)
                        p.wait(t, "kbV", h + 1)
                        pend = []

                        def emit_av(t, h, half, j):
                            g = h * 32 + half * 16 + j
                            p.wait(t, "expS", g + 1)
                            if j == 0:
                                p.wait(t, "uevS", h * 2 + half)
                            for qs in range(2):
                                last = t.matmul(
                                    psV[0:65, qs * 512:(qs + 1) * 512],
                                    VAUG[:, (j * H + h) * 65:(j * H + h) * 65 + 65],
                                    PT[:, (g % 2) * D + qs * 512:
                                       (g % 2) * D + (qs + 1) * 512],
                                    start=(j == 0), stop=(j == NT - 1))
                            p.inc("avT", last)

                        for half in range(2):
                            for j in range(NT):
                                g = h * 32 + half * 16 + j
                                if g >= 2:
                                    p.wait(t, "expS", g - 1)
                                last = None
                                for qs in range(2):
                                    last = t.matmul(
                                        psSS[:, (g % 2) * 1024 + qs * 512:
                                             (g % 2) * 1024 + (qs + 1) * 512],
                                        KB[:, (h % 2) * T + j * 128:
                                           (h % 2) * T + (j + 1) * 128],
                                        QM[:, (h % 2) * T + half * 1024 + qs * 512:
                                           (h % 2) * T + half * 1024 + (qs + 1) * 512],
                                        start=True, stop=True)
                                p.inc("stmT", last)
                                for f in pend:
                                    f(t)
                                pend = [lambda t, h=h, half=half, j=j:
                                        emit_av(t, h, half, j)]
                                if pieces:
                                    pieces.pop(0)(t)
                                if pieces and len(pieces) > 30 - (half * 16 + j):
                                    pieces.pop(0)(t)
                        for f in pend:
                            f(t)
                        for f in pieces:
                            f(t)
                    block.tensor(t_slot)

                    def s_slot(s, h=h, ha=ha):
                        def kev(ts):
                            p.wait(s, "kmm2", ha * 4 + ts + 1)
                            p.inc("kevS", s.activation(
                                out=ktp[0:64, ts * 512:(ts + 1) * 512],
                                in_=psA[0:64, (ts % 2) * 512:((ts % 2) + 1) * 512],
                                func=ACT.Identity,
                                bias=bkh[:, ha:ha + 1], scale=0.125))
                        if h < 0:
                            if ha < H:
                                for ts in range(4):
                                    kev(ts)
                            return
                        if ha < H:
                            kev(0)
                            kev(1)
                        if h == 0:
                            p.wait(s, "stqn0", p.val("stqn0"))
                            p.wait(s, "stqn1", p.val("stqn1"))
                        for half in range(2):
                            for j in range(NT):
                                g = h * 32 + half * 16 + j
                                p.wait(s, "stmT", g + 1)
                                if g >= 2:
                                    p.wait(s, "avT", g - 1)
                                p.inc("expS", s.activation(
                                    out=PT[:, (g % 2) * D:(g % 2) * D + D],
                                    in_=psSS[:, (g % 2) * 1024:(g % 2) * 1024 + 1024],
                                    func=ACT.Exp))
                                if half == 0 and j == 1 and ha < H:
                                    kev(2)
                                    kev(3)
                            p.wait(s, "avT", h * 32 + half * 16 + 16)
                            if h >= 2:
                                p.wait(s, "tpD", (h - 1) * 256)
                            if h == 0 and half == 0:
                                p.wait(s, "kbzV", 1)
                            p.inc("uevS", s.activation(
                                out=auT[0:65, (h % 2) * T + half * 1024:
                                        (h % 2) * T + half * 1024 + 1024],
                                                                in_=psV[0:65, 0:1024], func=ACT.Identity))
                    block.scalar(s_slot)

                    def v_slot(v, h=h, ha=ha):
                        if ha < H:
                            p.wait(v, "kevS", (ha + 1) * 4)
                            p.wait(v, "kbzV", 1)
                            if ha >= 2:
                                p.wait(v, "stmT", (ha - 1) * 32)
                            v.tensor_copy(out=KB[0:64, (ha % 2) * T:((ha % 2) + 1) * T],
                                          in_=ktp[0:64, 0:T])
                            p.inc("kbV", v.engine_nop())
                            for qt in range(NT):
                                for sub in range(2):
                                    p.wait(v, "aT", ha * 32 + qt * 2 + sub + 1)
                                    p.inc("rmV", v.tensor_reduce(
                                        out=nrmA[:, qt * 2 + sub:qt * 2 + sub + 1],
                                        in_=psA[:, 0:1024], axis=AX, op=OP.max,
                                        negate=True))
                            v.drain()
                            v.tensor_tensor(out=negm[:, 0:16], in0=nrmA[:, 0:32:2],
                                            in1=nrmA[:, 1:32:2], op=OP.min)
                            p.inc("rmhV", v.engine_nop())
                            p.wait(v, "nmT", ha + 1)
                            if ha >= 2:
                                p.wait(v, "stmA%d" % (ha % 2), (ha // 2) * 16)
                            v.tensor_copy(out=mst[:, (ha % 2) * 128:((ha % 2) + 1) * 128],
                                          in_=psA[0:16, 0:128])
                            p.inc("mstV", v.engine_nop())
                        hb = h - 1
                        if 0 <= hb:
                            p.wait(v, "tpD", (hb + 1) * 256)
                            b0 = (hb % 2) * 1280
                            v.drain()
                            v.tensor_scalar(out=rec_t[:, 0:16],
                                            in0=blk[:, b0 + 64:b0 + 16 * 80:80],
                                            scalar1=1e-8, scalar2=None, op0=OP.add)
                            v.drain()
                            v.reciprocal(out=rec_t[:, 0:16], in_=rec_t[:, 0:16])
                            v.drain()
                            for tt in range(NT):
                                v.tensor_scalar(
                                    out=attn_acc[:, tt * D + hb * 64:
                                                 tt * D + hb * 64 + 64],
                                    in0=blk[:, b0 + tt * 80:b0 + tt * 80 + 64],
                                    scalar1=rec_t[:, tt:tt + 1], scalar2=None,
                                    op0=OP.mult)
                            p.inc("blkV", v.engine_nop())
                    block.vector(v_slot)

                    def sy_slot(sy, h=h, ha=ha):
                        if ha < H:
                            p.wait(sy, "stqn0", p.val("stqn0"))
                            p.wait(sy, "stqn1", p.val("stqn1"))
                            if ha >= 2:
                                p.wait(sy, "stmT", (ha - 1) * 32)
                            p.inc("qmd%d" % (ha % 2), sy.dma_start(
                                out=QM[0:64, (ha % 2) * T:((ha % 2) + 1) * T],
                                in_=qln_d[64 * (ha % 2):64 * (ha % 2) + 64,
                                          (ha // 2) * T:(ha // 2 + 1) * T]), 16)
                        hb = h - 1
                        if hb >= 0:
                            p.wait(sy, "uevS", hb * 2 + 2)
                            if hb >= 1:
                                p.wait(sy, "tpD", hb * 256)
                            if hb >= 2:
                                p.wait(sy, "blkV", hb)
                            for tt in range(NT):
                                p.inc("tpD", sy.dma_start_transpose(
                                    out=blk[:, (hb % 2) * 1280 + tt * 80:
                                            (hb % 2) * 1280 + (tt + 1) * 80],
                                    in_=auT[0:80, (hb % 2) * T + tt * 128:
                                            (hb % 2) * T + (tt + 1) * 128]), 16)
                        if ha < H:
                            p.wait(sy, "mstV", ha + 1)
                            p.inc("stmA%d" % (ha % 2), sy.dma_start(
                                out=m_d[ha % 2:ha % 2 + 1, :],
                                in_=mst[:, (ha % 2) * 128:((ha % 2) + 1) * 128]), 16)
                            p.wait(sy, "stmA%d" % (ha % 2), (ha // 2 + 1) * 16)
                            p.inc("qmr%d" % (ha % 2), sy.dma_start(
                                out=QM[64:65, (ha % 2) * T:((ha % 2) + 1) * T],
                                in_=m_d[ha % 2:ha % 2 + 1, :]), 16)
                    block.sync(sy_slot)

                def sy_tp15(sy):
                    p.wait(sy, "uevS", 15 * 2 + 2)
                    p.wait(sy, "tpD", 15 * 256)
                    p.wait(sy, "blkV", 15)
                    for tt in range(NT):
                        p.inc("tpD", sy.dma_start_transpose(
                            out=blk[:, (15 % 2) * 1280 + tt * 80:
                                    (15 % 2) * 1280 + (tt + 1) * 80],
                            in_=auT[0:80, (15 % 2) * T + tt * 128:
                                    (15 % 2) * T + (tt + 1) * 128]), 16)
                block.sync(sy_tp15)

                def v_blk15(v):
                    p.wait(v, "tpD", H * 256)
                    b0 = (15 % 2) * 1280
                    v.drain()
                    v.tensor_scalar(out=rec_t[:, 0:16],
                                    in0=blk[:, b0 + 64:b0 + 16 * 80:80],
                                    scalar1=1e-8, scalar2=None, op0=OP.add)
                    v.drain()
                    v.reciprocal(out=rec_t[:, 0:16], in_=rec_t[:, 0:16])
                    v.drain()
                    for tt in range(NT):
                        v.tensor_scalar(
                            out=attn_acc[:, tt * D + 15 * 64:tt * D + 15 * 64 + 64],
                            in0=blk[:, b0 + tt * 80:b0 + tt * 80 + 64],
                            scalar1=rec_t[:, tt:tt + 1], scalar2=None,
                            op0=OP.mult)
                    p.inc("blkV", v.engine_nop())
                block.vector(v_blk15)

            e["mark"]("FFNWO")
            def g_wo(gg):
                p.wait(gg, "kmm2", 64)
                p.inc("ldWo", gg.dma_start(out=wfull[:], in_=d_Wo), 16)
            block.gpsimd(g_wo)

        esC = ExitStack()
        with esC:
            sbC = lambda *a: esC.enter_context(nc.sbuf_tensor(*a))
            w8b = sbC("w8b", [128, ND * D], BF16)
            ffnP = sbC("ffnP", [128, 2 * ND * 512], BF16)
            qbuf = sbC("qbuf", [128, ND * T], BF16)
            accT = sbC("accT", [128, 2 * D], BF16)
            wrow = sbC("wrow", [1, D], F32)
            psF = esC.enter_context(nc.psum_tensor("psF", [128, 1024], F32))
            psG = esC.enter_context(nc.psum_tensor("psG", [128, 2048], F32))
            psTc = esC.enter_context(nc.psum_tensor("psTc", [128, 1024], BF16))

            def g_w8(gg):
                p.wait(gg, "avT", 16 * 32)
                p.inc("ldW8", gg.dma_start(out=w8b[:], in_=d_W8T), 16)
            block.gpsimd(g_w8)

            def v_qz(v):
                p.wait(v, "xsV2", 1)
                p.wait(v, "stmT", 16 * 32)
                for dk in range(ND):
                    for hf2 in range(2):
                        sl = slice(dk * T + hf2 * 1024, dk * T + (hf2 + 1) * 1024)
                        v.tensor_scalar(out=lnsc[:], in0=xn2T[:, sl],
                                        scalar1=xsb[:, 1:2], scalar2=RBIG,
                                        op0=OP.mult, op1=OP.add)
                        v.drain()
                        v.tensor_scalar(out=xn2T[:, sl], in0=lnsc[:],
                                        scalar1=RBIG, scalar2=None,
                                        op0=OP.subtract)
                        v.drain()
                p.inc("qzV", v.engine_nop())
            block.vector(v_qz)

            def v_lnb(v):
                v.tensor_copy(out=lnbc_b[:], in_=col(C_LNB))
                p.inc("lnbV", v.engine_nop())
            block.vector(v_lnb)

            def t_wob(t):
                p.wait(t, "ldWo", 16)
                p.wait(t, "lnbV", 1)
                p.wait(t, "blkV", H)
                p.wait(t, "uevS", 32)
                last = None
                for oh in range(2):
                    for ch in range(ND):
                        last = t.matmul(
                            psF[0:1, oh * 512:(oh + 1) * 512],
                            lnbc_b[:, ch:ch + 1],
                            wfull[:, ch * D + oh * 512:ch * D + (oh + 1) * 512],
                            start=(ch == 0), stop=(ch == ND - 1))
                p.inc("wobT", last)
            block.tensor(t_wob)

            def v_wob(v):
                p.wait(v, "wobT", 1)
                v.tensor_tensor(out=wrow[:], in0=psF[0:1, 0:1024], in1=bo_f[:],
                                op=OP.add)
                v.drain()
                v.tensor_copy(out=rows_b[0:1, R_WOB * D:(R_WOB + 1) * D],
                              in_=wrow[:])
                p.inc("wobV", v.engine_nop())
            block.vector(v_wob)

            for ts in range(4):
                def t_f(t, ts=ts):
                    p.wait(t, "ldW8", 16)
                    p.wait(t, "qzV", 1)
                    p.wait(t, "wobV", 1)
                    if ts >= 2:
                        p.wait(t, "ffV", (ts - 1) * ND)
                    for nch in range(ND):
                        gi = ts * ND + nch
                        if gi >= 2:
                            p.wait(t, "ffV", gi - 1)
                        last = None
                        for dk in range(ND):
                            last = t.matmul(
                                psF[:, (nch % 2) * 512:((nch % 2) + 1) * 512],
                                w8b[:, dk * D + nch * 128:dk * D + (nch + 1) * 128],
                                xn2T[:, dk * T + ts * 512:dk * T + (ts + 1) * 512],
                                start=(dk == 0), stop=(dk == ND - 1))
                        p.inc("ffT", last)
                block.tensor(t_f)

                def v_f(v, ts=ts):
                    for nch in range(ND):
                        gi = ts * ND + nch
                        p.wait(v, "ffT", gi + 1)
                        v.tensor_scalar(
                            out=ffnP[:, (ts % 2) * ND * 512 + nch * 512:
                                     (ts % 2) * ND * 512 + (nch + 1) * 512],
                            in0=psF[:, (nch % 2) * 512:((nch % 2) + 1) * 512],
                            scalar1=col(C_SV)[:, nch:nch + 1],
                            scalar2=col(C_B1)[:, nch:nch + 1],
                            op0=OP.mult, op1=OP.add)
                        p.inc("ffV", v.engine_nop())
                block.vector(v_f)

                for sg in range(2):
                    tts = [ts * 4 + sg * 2, ts * 4 + sg * 2 + 1]

                    def t_acc(t, ts=ts, sg=sg, tts=tts):
                        p.wait(t, "blkV", H)
                        sgi = ts * 2 + sg
                        if sgi > 0:
                            p.wait(t, "finV", sgi * 2)
                        for k, tt in enumerate(tts):
                            if sgi * 2 + k > 0:
                                p.wait(t, "accV", sgi * 2 + k)
                            last = None
                            for ch in range(ND):
                                last = t.transpose(
                                    psTc[:, ch * 128:(ch + 1) * 128],
                                    attn_acc[:, tt * D + ch * 128:
                                             tt * D + (ch + 1) * 128],
                                    ident_b[:])
                            p.inc("accT2", last)
                    block.tensor(t_acc)

                    def v_acc(v, ts=ts, sg=sg, tts=tts):
                        for k, tt in enumerate(tts):
                            p.wait(v, "accT2", (ts * 2 + sg) * 2 + k + 1)
                            v.tensor_copy(out=accT[:, k * D:(k + 1) * D],
                                          in_=psTc[:, 0:1024])
                            p.inc("accV", v.engine_nop())
                    block.vector(v_acc)

                    if ts == 0 and sg == 0:
                        def sy_qb(sy):
                            p.wait(sy, "stqn0", p.val("stqn0"))
                            p.wait(sy, "stqn1", p.val("stqn1"))
                            p.wait(sy, "blkV", H)
                            p.inc("ldqb0", sy.dma_start(out=qbuf[:], in_=qln_d), 16)
                        block.sync(sy_qb)

                    def sy_x1(sy, ts=ts, sg=sg, tts=tts):
                        for k, tt in enumerate(tts):
                            if tt < 4:
                                p.wait(sy, "qcC", NT)
                            else:
                                p.wait(sy, "finV", tt - 3)
                            p.inc("ldf%d" % (tt % 4), sy.dma_start(
                                out=xrot[:, (tt % 4) * D:((tt % 4) + 1) * D],
                                in_=x1_d[tt * 128:(tt + 1) * 128, :]), 16)
                    block.sync(sy_x1)

                    def t_wo(t, ts=ts, sg=sg, tts=tts):
                        p.wait(t, "accV", (ts * 2 + sg) * 2 + 2)
                        p.wait(t, "wobV", 1)
                        if ts + sg > 0:
                            p.wait(t, "finV", (ts * 2 + sg) * 2)
                        for ch in range(ND):
                            gi = (ts * 2 + sg) * ND + ch
                            if ch == 0:
                                p.wait(t, "ldqb0", 16)
                            for k, tt in enumerate(tts):
                                for oh in range(2):
                                    t.matmul(
                                        psG[:, (k * 2 + oh) * 512:(k * 2 + oh + 1) * 512],
                                        accT[:, k * D + ch * 128:k * D + (ch + 1) * 128],
                                        wfull[:, ch * D + oh * 512:ch * D + (oh + 1) * 512],
                                        start=(ch == 0), stop=False)
                                    last = t.matmul(
                                        psG[:, (k * 2 + oh) * 512:(k * 2 + oh + 1) * 512],
                                        qbuf[:, ch * T + tt * 128:
                                             ch * T + (tt + 1) * 128],
                                        wfull[:, ch * D + oh * 512:ch * D + (oh + 1) * 512],
                                        start=False,
                                        stop=False)
                            p.inc("woT", last)
                        for k, tt in enumerate(tts):
                            for oh in range(2):
                                last = t.matmul(
                                    psG[:, (k * 2 + oh) * 512:(k * 2 + oh + 1) * 512],
                                    ones_b[0:1, 0:128],
                                    rows_b[0:1, R_BO * D + oh * 512:
                                           R_BO * D + (oh + 1) * 512],
                                    start=False, stop=True)
                        p.inc("woT2", last)
                    block.tensor(t_wo)

                    def t_ft(t, ts=ts, sg=sg, tts=tts):
                        p.wait(t, "woT2", (ts * 2 + sg) + 1)
                        p.wait(t, "ffV", (ts + 1) * ND)
                        for k, tt in enumerate(tts):
                            p.wait(t, "finV", (ts * 2 + sg) * 2 + k)
                            last = None
                            for nch in range(ND):
                                last = t.transpose(
                                    psTc[:, nch * 128:(nch + 1) * 128],
                                    ffnP[:, (ts % 2) * ND * 512 + nch * 512 +
                                         (tt % 4) * 128:
                                         (ts % 2) * ND * 512 + nch * 512 +
                                         (tt % 4) * 128 + 128],
                                    ident_b[:])
                            p.inc("ftT", last)
                    block.tensor(t_ft)

                    def v_fin(v, ts=ts, sg=sg, tts=tts):
                        for k, tt in enumerate(tts):
                            p.wait(v, "ldf%d" % (tt % 4), (tt // 4 + 1) * 16)
                            p.wait(v, "woT2", (ts * 2 + sg) + 1)
                            p.wait(v, "ftT", (ts * 2 + sg) * 2 + k + 1)
                            if tt >= 2:
                                p.wait(v, "stf%d" % (tt % 2), (tt // 2) * 16)
                            for oh in range(2):
                                v.tensor_tensor(
                                    out=xorot[:, (tt % 2) * D + oh * 512:
                                              (tt % 2) * D + (oh + 1) * 512],
                                    in0=psG[:, (k * 2 + oh) * 512:
                                            (k * 2 + oh + 1) * 512],
                                    in1=xrot[:, (tt % 4) * D + oh * 512:
                                             (tt % 4) * D + (oh + 1) * 512],
                                    op=OP.add)
                            v.drain()
                            if _STAGE >= 5:
                                v.tensor_copy(out=lnsc[:], in_=psTc[:, 0:1024])
                                v.drain()
                                v.tensor_tensor(
                                    out=xorot[:, (tt % 2) * D:((tt % 2) + 1) * D],
                                    in0=xorot[:, (tt % 2) * D:((tt % 2) + 1) * D],
                                    in1=lnsc[:],
                                    op=OP.add)
                                v.drain()
                            p.inc("finV", v.engine_nop())
                    block.vector(v_fin)

                    def sy_fo(sy, ts=ts, sg=sg, tts=tts):
                        for k, tt in enumerate(tts):
                            p.wait(sy, "finV", (ts * 2 + sg) * 2 + k + 1)
                            p.inc("stf%d" % (tt % 2), sy.dma_start(
                                out=d_out[tt * 128:(tt + 1) * 128, :],
                                in_=xorot[:, (tt % 2) * D:((tt % 2) + 1) * D]), 16)
                    block.sync(sy_fo)

            def sy_fin(sy):
                p.wait(sy, "stf0", p.val("stf0"))
                p.wait(sy, "stf1", p.val("stf1"))
            block.sync(sy_fin)



def kernel(**inputs):
    npad, per_core = _host_prep(inputs)
    nc = build(npad)
    res = run_bass_kernel_spmd(nc, per_core, core_ids=list(range(NCORES)))
    out = np.stack([res.results[i]["out"] for i in range(NCORES)], axis=0)
    return out.astype(np.float32)


# version 16
# speedup vs baseline: 1.1463x; 1.1463x over previous
import os
from contextlib import ExitStack
import numpy as np
import ml_dtypes
import concourse.bass as bass
import concourse.mybir as mybir
from concourse.bass import IndirectOffsetOnAxis
from concourse.bass_utils import run_bass_kernel_spmd

F32 = mybir.dt.float32
BF16 = mybir.dt.bfloat16
I32 = mybir.dt.int32
I8 = mybir.dt.int8
AX = mybir.AxisListType.X
OP = mybir.AluOpType
ACT = mybir.ActivationFunctionType

NCORES = 8
T = 2048
D = 1024
C = 2048
FF = 4096
H = 16
DH = 64
NT = T // 128
ND = D // 128
NCC = C // 128
NF = FF // 128
RBIG = 12582912.0
EPS = 1e-5

_STAGE = int(os.environ.get("MK_STAGE", "99"))



def _route_moe(expert_indices, expert_weights, x_flat):
    flat_idx = np.asarray(expert_indices).reshape(-1)
    flat_w = np.asarray(expert_weights).reshape(-1).astype(np.float32)
    order = np.argsort(flat_idx, kind="stable")
    tok = order // 2
    counts = np.bincount(flat_idx, minlength=NCORES)
    starts = np.concatenate([[0], np.cumsum(counts)])
    npad = max(512, int(np.ceil((counts.max() + 1) / 128.0) * 128))

    xs_l, fw_l = [], []
    for e in range(NCORES):
        n = int(counts[e])
        xs = np.zeros((npad, D), np.float32)
        fw = np.zeros((npad, 1), np.float32)
        xs[1:1 + n] = x_flat[tok[starts[e]:starts[e + 1]]]
        fw[1:1 + n, 0] = flat_w[order[starts[e]:starts[e + 1]]]
        xs_l.append(np.ascontiguousarray(xs))
        fw_l.append(np.ascontiguousarray(fw))

    expert_of_row = flat_idx[order]
    i_loc = 1 + (np.arange(len(order)) - starts[expert_of_row])
    chk = i_loc // 512
    c_k = np.minimum(512, npad - chk * 512)
    agrow = (NCORES * 512 * chk + expert_of_row * c_k + (i_loc - chk * 512))
    gidx_full = np.zeros((NCORES, 2, T), np.int64)
    occ = np.zeros(NCORES * T, np.int64)
    for r in range(len(order)):
        g = int(tok[r])
        gidx_full[g // T, occ[g], g % T] = agrow[r]
        occ[g] += 1
    gidx_l = []
    for c in range(NCORES):
        g2 = gidx_full[c].reshape(2, NT, 128).transpose(2, 0, 1).reshape(128, 32)
        gidx_l.append(np.ascontiguousarray(g2.astype(np.int32)))
    return npad, xs_l, fw_l, gidx_l


def _col8(v):
    return np.ascontiguousarray(np.asarray(v, np.float32).reshape(ND, 128).T)


def _lay(w):
    m = w.shape[1]
    return np.ascontiguousarray(
        np.asarray(w).reshape(ND, 128, m).transpose(1, 0, 2).reshape(128, ND * m))


def _host_prep(inputs):
    x = np.asarray(inputs["L0_attn_hidden_states"], np.float32)
    npad, xs_l, fw_l, gidx_l = _route_moe(
        inputs["L0_ffn_expert_indices"], inputs["L0_ffn_expert_weights"],
        x.reshape(-1, D))

    w_inT = np.asarray(inputs["L0_attn_in_proj_w"], np.float32)[:C].T
    w_outT = np.asarray(inputs["L0_attn_out_proj_w"], np.float32).T
    w_outT_l = np.ascontiguousarray(
        w_outT.reshape(NCC, 128, D).transpose(1, 0, 2).reshape(128, NCC * D))

    cw = np.asarray(inputs["L0_attn_conv_w"], np.float32)[:, 0, :]
    conv_diag = np.zeros((128, NCC * 4 * 128), np.float32)
    for cc in range(NCC):
        for k in range(4):
            conv_diag[:, (cc * 4 + k) * 128:(cc * 4 + k + 1) * 128] = \
                np.diag(cw[cc * 128:(cc + 1) * 128, k])

    w1 = np.asarray(inputs["L0_ffn_w1"], np.float32)
    w2 = np.asarray(inputs["L0_ffn_w2"], np.float32)
    w1_l = [np.ascontiguousarray(
        w1[e].reshape(ND, 128, NF, 128).transpose(2, 1, 0, 3).reshape(NF, 128, ND * 128))
        for e in range(NCORES)]
    w2_l = [np.ascontiguousarray(w2[e].reshape(NF, 128, D)) for e in range(NCORES)]

    shared = dict(
        w_inT=_lay(w_inT), w_outT=w_outT_l, conv_diag=conv_diag,
        conv_b=np.ascontiguousarray(np.asarray(inputs["L0_attn_conv_b"], np.float32).reshape(NCC, 128).T),
        bias0_row=np.ascontiguousarray(np.asarray(inputs["L0_ffn_bias"], np.float32)[None, :]),
        Wq=_lay(np.asarray(inputs["L1_attn_Wq"], np.float32)),
        Wk=_lay(np.asarray(inputs["L1_attn_Wk"], np.float32)),
        Wv=_lay(np.asarray(inputs["L1_attn_Wv"], np.float32)),
        Wo=_lay(np.asarray(inputs["L1_attn_Wo"], np.float32)),
        bq_row=np.ascontiguousarray(np.asarray(inputs["L1_attn_bq"], np.float32)[None, :]),
        bv_row=np.ascontiguousarray(np.asarray(inputs["L1_attn_bv"], np.float32)[None, :]),
        bo_row=np.ascontiguousarray(np.asarray(inputs["L1_attn_bo"], np.float32)[None, :]),
        lnb_col=_col8(inputs["L1_attn_ln_b"]),
        bk_h64=np.ascontiguousarray(
            np.asarray(inputs["L1_attn_bk"], np.float32).reshape(16, 64).T),
        g_col=_col8(inputs["L1_attn_ln_g"]),
        b_col=_col8(inputs["L1_attn_ln_b"]),
        W8T=np.ascontiguousarray(
            np.asarray(inputs["L1_ffn_weight_int8"]).T.reshape(ND, 128, D)
            .transpose(1, 0, 2).reshape(128, ND * D)),
        wscale_col=_col8(np.asarray(inputs["L1_ffn_weight_scale"], np.float32)[:, 0]),
        bias1_col=_col8(inputs["L1_ffn_bias"]),
        ident=np.eye(128, dtype=np.float32),
        ones_row=np.ones((1, T), np.float32),
    )
    per_core = []
    for i in range(NCORES):
        m = dict(shared)
        m["x"] = np.ascontiguousarray(x[i])
        m["xs"] = xs_l[i]
        m["fw"] = np.ascontiguousarray(fw_l[i].reshape(-1, 128).T)
        m["gidx"] = gidx_l[i]
        m["w1"] = w1_l[i]
        m["w2"] = w2_l[i]
        per_core.append(m)
    return npad, per_core



_ALIAS = {}
for _t, _names in {
    "mtrT": ["l0trT", "l2trT", "gmT", "xsT", "qmmT", "vmmT", "qcT"],
    "mlnA": ["l0A", "l2A", "qlA"],
    "init_v": ["haloz", "vzV", "kbzV", "lnbV", "gmV", "xsV", "xsV2"],
    "mlnBv": ["l0Bv", "l2Bv", "qlBv"],
    "mapC": ["l0apC", "l2apC", "qcC"],
    "mtrV": ["l0trV", "l2trV", "qcV"],
    "mlnB": ["l0B", "l2B", "qlB"],
    "qevS": ["kevS"],
    "dsl0": ["ldxsA0", "ldxsC0", "ldxA0", "ldxC0"],
    "dsl1": ["ldxsA1", "ldxsC1", "ldxA1", "ldxC1"],
    "dsl2": ["ldxA2", "ldxC2"],
    "dsl3": ["ldxA3", "ldxC3"],
    "dslB0": ["ldr0", "ldq0", "ldf0"],
    "dslB1": ["ldr1", "ldq1", "ldf1"],
    "dslB2": ["ldr2", "ldq2", "ldf2"],
    "dslB3": ["ldr3", "ldq3", "ldf3"],
    "stx10": ["stq0", "stf0", "stdb0"],
    "stx11": ["stq1", "stf1", "stdb1"],
    "qmA": ["qmd0", "ldqb0"],
    "qmB": ["qmd1", "ldqb1"],
    "ldWq": ["ldWv", "ldWk", "ldWo"],
    "argi": ["argo"],
    "cc": ["cc2"],
}.items():
    for _n in _names:
        _ALIAS[_n] = _t


class P:
    def __init__(self, nc):
        self.nc = nc
        self._phys = {}
        self._base = {}
        self._guards = []

    def _ph(self, name):
        ph = _ALIAS.get(name, name)
        if ph not in self._phys:
            g = self.nc.semaphore(ph)
            self._phys[ph] = [g.__enter__(), 0]
            self._guards.append(g)
        s = self._phys[ph]
        if name not in self._base:
            self._base[name] = s[1]
        return s, self._base[name]

    def _s(self, name):
        s, b = self._ph(name)
        return [s[0], s[1] - b]

    def inc(self, name, instr, n=1):
        s, b = self._ph(name)
        instr.then_inc(s[0], n)
        s[1] += n
        return s[1] - b

    def val(self, name):
        s, b = self._ph(name)
        return s[1] - b

    def wait(self, eng, name, v=None):
        s, b = self._ph(name)
        eng.wait_ge(s[0], b + ((s[1] - b) if v is None else v))

    def close(self):
        for g in reversed(self._guards):
            g.__exit__(None, None, None)



def build(npad):
    nc = bass.Bass("TRN2", target_bir_lowering=False)
    NPT = npad // 128
    NTG_M = (npad + 511) // 512
    tgw_M = [min(512, npad - 512 * t) for t in range(NTG_M)]

    def din(name, shape, dt=F32):
        return nc.dram_tensor(name, shape, dt, kind="ExternalInput").ap()

    d_x = din("x", [T, D])
    d_xs = din("xs", [npad, D])
    d_fw = din("fw", [128, NPT])
    d_gidx = din("gidx", [128, 32], I32)
    d_winT = din("w_inT", [128, ND * C])
    d_woutT = din("w_outT", [128, NCC * D])
    d_cdiag = din("conv_diag", [128, NCC * 4 * 128])
    d_cb = din("conv_b", [128, NCC])
    d_b0 = din("bias0_row", [1, D])
    d_w1 = din("w1", [NF, 128, ND * 128])
    d_w2 = din("w2", [NF, 128, D])
    d_Wq = din("Wq", [128, ND * D])
    d_Wk = din("Wk", [128, ND * D])
    d_Wv = din("Wv", [128, ND * D])
    d_Wo = din("Wo", [128, ND * D])
    d_bq = din("bq_row", [1, D])
    d_bv = din("bv_row", [1, D])
    d_bo = din("bo_row", [1, D])
    d_lnbc = din("lnb_col", [128, ND])
    d_bk = din("bk_h64", [64, 16])
    d_g = din("g_col", [128, ND])
    d_b = din("b_col", [128, ND])
    d_W8T = din("W8T", [128, ND * D], I8)
    d_wsc = din("wscale_col", [128, ND])
    d_b1 = din("bias1_col", [128, ND])
    d_id = din("ident", [128, 128])
    d_ones = din("ones_row", [1, T])
    d_out = nc.dram_tensor("out", [T, D], F32, kind="ExternalOutput").ap()

    ag_in = nc.dram_tensor("ag_in", [npad, D], F32).ap()
    ag_out = nc.dram_tensor("ag_out", [NCORES * npad, D], F32).ap()
    ar_in = nc.dram_tensor("ar_in", [1, 128], F32).ap()
    ar_out = nc.dram_tensor("ar_out", [1, 128], F32).ap()
    x1_d = nc.dram_tensor("x1_d", [T, D], F32).ap()
    q_d = nc.dram_tensor("q_d", [T, D], F32).ap()
    m_d = nc.dram_tensor("m_d", [2, 16 * 128], BF16).ap()
    qln_d = nc.dram_tensor("qln_d", [128, ND * T], BF16).ap()

    es = ExitStack()
    with es:
        _PH = globals().get("_PHASES", None)
        def mark(name):
            if _PH is not None:
                _PH.append((name, nc.n_instructions()))
        sb = lambda *a: es.enter_context(nc.sbuf_tensor(*a))
        xn2T = sb("xn2T", [128, ND * T], BF16)
        xrot = sb("xrot", [128, 4 * D], F32)
        xorot = sb("xorot", [128, 2 * D], F32)
        ident_f = sb("ident_f", [128, 128], F32)
        ident_b = sb("ident_b", [128, 128], BF16)
        ones_b = sb("ones_b", [1, 512], BF16)
        ones_f = sb("ones_f", [1, 128], F32)
        rows_b = sb("rows_b", [1, 6 * D], BF16)
        bo_f = sb("bo_f", [1, D], F32)
        cols_f = sb("cols_f", [128, 8 * ND], F32)
        cb_c = sb("cb_c", [128, NCC], F32)
        bkh = sb("bkh", [64, 16], F32)
        gmax = sb("gmax", [128, 1], F32)
        armx = sb("armx", [1, 128], F32)
        xsp = sb("xsp", [1, 2], F32)
        xsb = sb("xsb", [128, 2], F32)
        gidx_sb = sb("gidx_sb", [128, 32], I32)
        stats_b = sb("stats_b", [128, 12], F32)
        mva = sb("mva", [128, 2 * NT], F32)
        rsa = sb("rsa", [128, NT], F32)
        ln1 = sb("ln1", [128, NT], F32)
        lnsc = sb("lnsc", [128, D], F32)
        lnbf = sb("lnbf", [128, 2 * D], BF16)
        block = es.enter_context(nc.Block())
        p = P(nc)
        R_B0, R_BQ, R_BV, R_BO, R_LNB, R_WOB = range(6)

        def row(i):
            return rows_b[0:1, i * D:(i + 1) * D]

        C_BK8, C_G, C_B, C_WSC, C_B1, C_SV, C_LNB = range(7)

        def col(i):
            return cols_f[:, i * ND:(i + 1) * ND]

        def g_const(g):
            for dst, src in [(ident_f[:], d_id), (ident_b[:], d_id),
                             (ones_b[:], d_ones[0:1, 0:512]), (ones_f[:], d_ones[0:1, 0:128]),
                             (row(R_B0), d_b0), (row(R_BQ), d_bq),
                             (row(R_BV), d_bv), (row(R_BO), d_bo),
                             (bo_f[:], d_bo),
                             (bkh[:], d_bk), (col(C_G), d_g),
                             (col(C_B), d_b), (col(C_WSC), d_wsc),
                             (col(C_B1), d_b1), (col(C_LNB), d_lnbc),
                             (cb_c[:], d_cb), (gidx_sb[:], d_gidx)]:
                p.inc("cst", g.dma_start(out=dst, in_=src), 16)
        block.gpsimd(g_const)
        CST = p.val("cst")

        def v_init(v):
            p.wait(v, "cst", CST)
            v.tensor_scalar(out=bkh[:], in0=bkh[:], scalar1=0.125,
                            scalar2=None, op0=OP.mult)
            p.inc("init_v", v.engine_nop())
        block.vector(v_init)

        def ln_passA(v, xt, slot):
            v.bn_stats(out=stats_b[:, 0:6], in_=xt[:, 0:512])
            v.bn_stats(out=stats_b[:, 6:12], in_=xt[:, 512:1024])
            v.drain()
            v.bn_aggr(out=mva[:, 2 * slot:2 * slot + 2], in_=stats_b[:, 0:12])
            v.drain()

        def ln_passB(ntiles, tag):
            def fv(v):
                p.wait(v, tag + "A", ntiles)
                v.tensor_scalar(out=ln1[:, 0:ntiles], in0=mva[:, 1:2 * ntiles:2],
                                scalar1=EPS, scalar2=None, op0=OP.add)
                v.drain()
                v.reciprocal(out=rsa[:, 0:ntiles], in_=ln1[:, 0:ntiles])
                p.inc(tag + "Bv", v.engine_nop())
            block.vector(fv)

            def fs(s):
                p.wait(s, tag + "Bv")
                s.activation(out=rsa[:, 0:ntiles], in_=rsa[:, 0:ntiles],
                             func=ACT.Sqrt)
                p.inc(tag + "B", s.activation(out=ln1[:, 0:1], in_=ln1[:, 0:1],
                                              func=ACT.Identity))
            block.scalar(fs)
            return p.val(tag + "B")

        def ln_apply(v, xt, out, slot):
            v.tensor_scalar(out=out, in0=xt, scalar1=mva[:, 2 * slot:2 * slot + 1],
                            scalar2=rsa[:, slot:slot + 1], op0=OP.subtract,
                            op1=OP.mult)

        es0 = ExitStack()
        with es0:
            sb0 = lambda *a: es0.enter_context(nc.sbuf_tensor(*a))
            winT = sb0("winT", [128, ND * C], BF16)
            def g_l0w(g):
                p.inc("ldwin", g.dma_start(out=winT[:], in_=d_winT), 16)
            block.gpsimd(g_l0w)

            esM = ExitStack()
            with esM:
                sbM = lambda *a: esM.enter_context(nc.sbuf_tensor(*a))
                xs_t = sbM("xs_t", [128, 2 * D], F32)
                xsnT = sbM("xsnT", [128, ND * 512], BF16)
                hT = sbM("hT", [128, NF * 512], BF16)
                w1b = sbM("w1b", [128, 2 * ND * 128], BF16)
                w2b = sbM("w2b", [128, 2 * D], BF16)
                w1f = sbM("w1f", [128, 2 * ND * 128], F32)
                w2f = sbM("w2f", [128, 4 * D], F32)
                fw_sb = sbM("fw_sb", [128, NPT], F32)
                ys_sb = sbM("ys_sb", [128, 2 * D], F32)
                psM = esM.enter_context(nc.psum_tensor("psM", [128, 2048], F32))
                psTb = esM.enter_context(nc.psum_tensor("psTb", [128, 2048], BF16))
                def g_mld(g):
                    p.inc("ldfw", g.dma_start(out=fw_sb[:], in_=d_fw), 16)
                block.gpsimd(g_mld)

                mark("M_lnA")
                for i in range(NPT):
                    def sy_ld(sy, i=i):
                        if i >= 2:
                            p.wait(sy, "mlnA", i - 1)
                        p.inc("ldxsA%d" % (i % 2), sy.dma_start(
                            out=xs_t[:, (i % 2) * D:((i % 2) + 1) * D],
                            in_=d_xs[i * 128:(i + 1) * 128, :]), 16)
                    block.sync(sy_ld)

                    def v_st(v, i=i):
                        p.wait(v, "ldxsA%d" % (i % 2), (i // 2 + 1) * 16)
                        ln_passA(v, xs_t[:, (i % 2) * D:((i % 2) + 1) * D], i)
                        p.inc("mlnA", v.engine_nop())
                    block.vector(v_st)
                MLNB = ln_passB(NPT, "mln")

                mark("M_tg")
                XLN_K = [0]
                for tg in range(NTG_M):
                    tw = tgw_M[tg]
                    ntl = tw // 128
                    for ii in range(ntl):
                        i = tg * 4 + ii

                        def sy_ld(sy, i=i):
                            if i < 2:
                                p.wait(sy, "mlnA", NPT)
                            else:
                                p.wait(sy, "mtrV", i - 1)
                            if i >= 2:
                                p.wait(sy, "mapC", i - 1)
                            p.inc("ldxsC%d" % (i % 2), sy.dma_start(
                                out=xs_t[:, (i % 2) * D:((i % 2) + 1) * D],
                                in_=d_xs[i * 128:(i + 1) * 128, :]), 16)
                        block.sync(sy_ld)

                        def v_ap(v, i=i):
                            p.wait(v, "ldxsC%d" % (i % 2), (i // 2 + 1) * 16)
                            p.wait(v, "mlnB", MLNB)
                            if i >= 2:
                                p.wait(v, "mtrT", i - 1)
                            ln_apply(v, xs_t[:, (i % 2) * D:((i % 2) + 1) * D],
                                     lnbf[:, (i % 2) * D:((i % 2) + 1) * D], i)
                            p.inc("mapC", v.engine_nop())
                        block.vector(v_ap)

                        def t_tr(t, i=i):
                            p.wait(t, "mapC", i + 1)
                            p.wait(t, "cst", CST)
                            if i >= 2:
                                p.wait(t, "mtrV", i - 1)
                            if i >= 4:
                                p.wait(t, "mhT", (i // 4) * NF)
                            last = None
                            for dk in range(ND):
                                last = t.transpose(
                                    psTb[:, (i % 2) * 1024 + dk * 128:
                                         (i % 2) * 1024 + (dk + 1) * 128],
                                    lnbf[:, (i % 2) * D + dk * 128:
                                         (i % 2) * D + (dk + 1) * 128], ident_b[:])
                            p.inc("mtrT", last)
                        block.tensor(t_tr)

                        def v_ev(v, i=i, ii=ii):
                            p.wait(v, "mtrT", i + 1)
                            for dk in range(ND):
                                v.tensor_copy(
                                    out=xsnT[:, dk * 512 + ii * 128:
                                             dk * 512 + (ii + 1) * 128],
                                    in_=psTb[:, (i % 2) * 1024 + dk * 128:
                                             (i % 2) * 1024 + (dk + 1) * 128])
                            p.inc("mtrV", v.engine_nop())
                        block.vector(v_ev)

                    def sy_w1(sy, tg=tg):
                        for fc in range(NF):
                            gi = tg * NF + fc
                            if gi >= 2:
                                p.wait(sy, "w1cV", gi - 1)
                            p.inc("ldw1%d" % (fc % 2), sy.dma_start(
                                out=w1f[:, (fc % 2) * 1024:((fc % 2) + 1) * 1024],
                                in_=d_w1[fc]), 16)
                    block.sync(sy_w1)

                    def w1cast(s, gi):
                        fc2 = gi % NF
                        p.wait(s, "ldw1%d" % (fc2 % 2), (gi // 2 + 1) * 16)
                        if gi >= 2:
                            p.wait(s, "mhT", gi - 1)
                        p.inc("w1cV", s.copy(
                            out=w1b[:, (fc2 % 2) * 1024:((fc2 % 2) + 1) * 1024],
                            in_=w1f[:, (fc2 % 2) * 1024:((fc2 % 2) + 1) * 1024]))

                    def t_h(t, tg=tg, tw=tw):
                        p.wait(t, "mtrV", tg * 4 + tw // 128)
                        p.wait(t, "myV", p.val("myV"))
                        for fc in range(NF):
                            gi = tg * NF + fc
                            p.wait(t, "w1cV", gi + 1)
                            if gi >= 2:
                                p.wait(t, "mhS", gi - 1)
                            last = None
                            for dk in range(ND):
                                last = t.matmul(
                                    psM[:, (fc % 2) * 512:(fc % 2) * 512 + tw],
                                    w1b[:, (fc % 2) * 1024 + dk * 128:
                                        (fc % 2) * 1024 + (dk + 1) * 128],
                                    xsnT[:, dk * 512:dk * 512 + tw],
                                    start=(dk == 0), stop=(dk == ND - 1))
                            p.inc("mhT", last)
                    block.tensor(t_h)

                    def s_h(s, tg=tg, tw=tw):
                        if tg == 0:
                            w1cast(s, 0)
                            w1cast(s, 1)
                        if tg >= 1:
                            p.wait(s, "myT", tg * 2 * NF)
                        for fc in range(NF):
                            gi = tg * NF + fc
                            p.wait(s, "mhT", gi + 1)
                            p.inc("mhS", s.activation(
                                out=hT[:, fc * 512:fc * 512 + tw],
                                in_=psM[:, (fc % 2) * 512:(fc % 2) * 512 + tw],
                                func=ACT.Gelu))
                            if fc + 2 < NF:
                                w1cast(s, gi + 2)
                    block.scalar(s_h)

                    nhalf = (ntl + 1) // 2
                    for hf in range(nhalf):
                        tis = [hf * 2 + k for k in range(2) if hf * 2 + k < ntl]

                        def sy_w2(sy, tg=tg, hf=hf):
                            for fb in range(NF // 2):
                                gi = (tg * 2 + hf) * NF // 2 + fb
                                if gi >= 2:
                                    p.wait(sy, "w2cV", (gi - 1) * 2)
                                p.inc("ldw2%d" % (fb % 2), sy.dma_start(
                                    out=w2f[:, (fb % 2) * 2048:((fb % 2) + 1) * 2048
                                            ].rearrange("p (a m) -> p a m", a=2),
                                    in_=d_w2[fb * 2:fb * 2 + 2].rearrange(
                                        "a p m -> p a m")), 16)
                        block.sync(sy_w2)

                        def v_w2(v, tg=tg, hf=hf):
                            for fc in range(NF):
                                fb = fc // 2
                                gi = (tg * 2 + hf) * NF + fc
                                p.wait(v, "ldw2%d" % (fb % 2),
                                       ((tg * 2 + hf) * 8 + fb // 2 + 1) * 16)
                                if gi >= 2:
                                    p.wait(v, "myT", gi - 1)
                                p.inc("w2cV", v.tensor_copy(
                                    out=w2b[:, (fc % 2) * D:((fc % 2) + 1) * D],
                                    in_=w2f[:, (fb % 2) * 2048 + (fc % 2) * D:
                                            (fb % 2) * 2048 + ((fc % 2) + 1) * D]))
                        block.vector(v_w2)

                        def s_w2(s, tg=tg, hf=hf):
                            if hf == nhalf - 1 and tg + 1 < NTG_M:
                                w1cast(s, (tg + 1) * NF)
                                w1cast(s, (tg + 1) * NF + 1)
                        block.scalar(s_w2)

                        def t_y(t, tg=tg, hf=hf, tis=tis):
                            p.wait(t, "mhS", (tg + 1) * NF)
                            p.wait(t, "myV", p.val("myV"))
                            for fc in range(NF):
                                gi = (tg * 2 + hf) * NF + fc
                                p.wait(t, "w2cV", gi + 1)
                                last = None
                                for ti in tis:
                                    for oh in range(2):
                                        last = t.matmul(
                                            psM[:, ((ti % 2) * 2 + oh) * 512:
                                                ((ti % 2) * 2 + oh + 1) * 512],
                                            hT[:, fc * 512 + ti * 128:
                                               fc * 512 + (ti + 1) * 128],
                                            w2b[:, (fc % 2) * D + oh * 512:
                                                (fc % 2) * D + (oh + 1) * 512],
                                            start=(fc == 0), stop=(fc == NF - 1))
                                p.inc("myT", last)
                        block.tensor(t_y)

                        def v_y(v, tg=tg, tis=tis):
                            p.wait(v, "ldfw", 16)
                            p.wait(v, "stys", p.val("stys"))
                            p.wait(v, "myT")
                            for ti in tis:
                                for oh in range(2):
                                    v.tensor_scalar(
                                        out=ys_sb[:, (ti % 2) * D + oh * 512:
                                                  (ti % 2) * D + (oh + 1) * 512],
                                        in0=psM[:, ((ti % 2) * 2 + oh) * 512:
                                                ((ti % 2) * 2 + oh + 1) * 512],
                                        scalar1=fw_sb[:, tg * 4 + ti:tg * 4 + ti + 1],
                                        scalar2=None, op0=OP.mult)
                                p.inc("myV", v.engine_nop())
                        block.vector(v_y)

                        def sy_y(sy, tg=tg, tis=tis):
                            p.wait(sy, "myV")
                            for ti in tis:
                                r0 = tg * 512 + ti * 128
                                p.inc("stys", sy.dma_start(
                                    out=ag_in[r0:r0 + 128, :],
                                    in_=ys_sb[:, (ti % 2) * D:((ti % 2) + 1) * D]),
                                    16)
                        block.sync(sy_y)

                    xln_sched = [i for i in range(NT)
                                 if (i // 4 if i < NPT else i % NTG_M) == tg]
                    for i in xln_sched:
                        k = XLN_K[0]
                        XLN_K[0] += 1

                        def g_xld(g, i=i, k=k):
                            if k >= 4:
                                p.wait(g, "l0A", k - 3)
                            p.inc("xla%d" % (k % 4), g.dma_start(
                                out=xrot[:, (k % 4) * D:((k % 4) + 1) * D],
                                in_=d_x[i * 128:(i + 1) * 128, :]), 16)
                        block.gpsimd(g_xld)

                        def v_xst(v, i=i, k=k):
                            p.wait(v, "xla%d" % (k % 4), (k // 4 + 1) * 16)
                            ln_passA(v, xrot[:, (k % 4) * D:((k % 4) + 1) * D], i)
                            p.inc("l0A", v.engine_nop())
                        block.vector(v_xst)

                mark("M_ag")
                def g_ag(g):
                    for k in range(NTG_M):
                        c0 = k * 512
                        ck = min(512, npad - c0)
                        p.wait(g, "stys", ((c0 + ck) // 128) * 16)
                        p.inc("cc", g.collective_compute(
                            "AllGather", OP.bypass,
                            replica_groups=[list(range(NCORES))],
                            ins=[ag_in[c0:c0 + ck, :]],
                            outs=[ag_out[NCORES * c0:NCORES * (c0 + ck), :]]))
                block.gpsimd(g_ag)

            esL = ExitStack()
            with esL:
                sbL = lambda *a: esL.enter_context(nc.sbuf_tensor(*a))
                xnT = sbL("xnT", [128, ND * 512], BF16)
                woutT = sbL("woutT", [128, NCC * D], BF16)
                cdiag = sbL("cdiag", [128, NCC * 4 * 128], BF16)
                y_sb = sbL("y_sb", [128, NCC * 512], BF16)
                xz_sb = sbL("xz_sb", [128, 2 * 520], BF16)
                halo = sbL("halo", [128, NCC * 4], BF16)
                plane = sbL("plane", [128, 4 * D], F32)
                psZ = esL.enter_context(nc.psum_tensor("psZ", [128, 1024], F32))
                psC2 = esL.enter_context(nc.psum_tensor("psC2", [128, 1024], F32))
                psO = esL.enter_context(nc.psum_tensor("psO", [128, 1024], F32))
                psT0 = esL.enter_context(nc.psum_tensor("psT0", [128, 2048], BF16))
                def g_l0w2(g):
                    p.wait(g, "myV", p.val("myV"))
                    p.wait(g, "stys", p.val("stys"))
                    p.inc("ldcd", g.dma_start(out=cdiag[:], in_=d_cdiag), 16)
                    p.inc("ldwo", g.dma_start(out=woutT[:], in_=d_woutT), 16)
                block.gpsimd(g_l0w2)

                mark("L0_lnA")
                L0B = ln_passB(NT, "l0")

                def v_hz(v):
                    p.wait(v, "cst", CST)
                    p.wait(v, "stys", p.val("stys"))
                    p.wait(v, "myV", p.val("myV"))
                    v.memset(halo[:], 0.0)
                    p.inc("haloz", v.engine_nop())
                block.vector(v_hz)

                mark("L0_tg")
                for tg in range(4):
                    for ii in range(4):
                        i = tg * 4 + ii

                        def sy_ld(sy, i=i):
                            if i < 4:
                                p.wait(sy, "l0A", 13 + i)
                            else:
                                p.wait(sy, "l0asm", i - 3)
                            p.inc("ldxC%d" % (i % 4), sy.dma_start(
                                out=xrot[:, (i % 4) * D:((i % 4) + 1) * D],
                                in_=d_x[i * 128:(i + 1) * 128, :]), 16)
                        block.sync(sy_ld)

                        def v_ap(v, i=i):
                            p.wait(v, "ldxC%d" % (i % 4), (i // 4 + 1) * 16)
                            p.wait(v, "l0B", L0B)
                            if i >= 2:
                                p.wait(v, "l0trT", i - 1)
                            if _STAGE == 0 and i >= 2:
                                p.wait(v, "stdbg%d" % (i % 2), (i // 2) * 16)
                            ln_apply(v, xrot[:, (i % 4) * D:((i % 4) + 1) * D],
                                     lnbf[:, (i % 2) * D:((i % 2) + 1) * D]
                                     if _STAGE != 0 else
                                     xorot[:, (i % 2) * D:((i % 2) + 1) * D], i)
                            if _STAGE == 0:
                                v.drain()
                                v.tensor_copy(out=lnbf[:, (i % 2) * D:
                                                       ((i % 2) + 1) * D],
                                              in_=xorot[:, (i % 2) * D:((i % 2) + 1) * D])
                            p.inc("l0apC", v.engine_nop())
                        block.vector(v_ap)

                        if _STAGE == 0:
                            def sy_dbg(sy, i=i):
                                p.wait(sy, "l0apC", i + 1)
                                p.inc("stdbg%d" % (i % 2), sy.dma_start(
                                    out=d_out[i * 128:(i + 1) * 128, :],
                                    in_=xorot[:, (i % 2) * D:((i % 2) + 1) * D]), 16)
                            block.sync(sy_dbg)

                        def t_tr(t, i=i):
                            p.wait(t, "l0apC", i + 1)
                            if i >= 2:
                                p.wait(t, "l0trV", i - 1)
                            if i == 0:
                                p.wait(t, "myV", p.val("myV"))
                            if i >= 4:
                                p.wait(t, "l0zT", (i // 4) * NCC)
                            last = None
                            for dk in range(ND):
                                last = t.transpose(
                                    psT0[:, (i % 2) * 1024 + dk * 128:
                                         (i % 2) * 1024 + (dk + 1) * 128],
                                    lnbf[:, (i % 2) * D + dk * 128:
                                         (i % 2) * D + (dk + 1) * 128], ident_b[:])
                            p.inc("l0trT", last)
                        block.tensor(t_tr)

                        def v_ev(v, i=i, ii=ii):
                            p.wait(v, "l0trT", i + 1)
                            for dk in range(ND):
                                v.tensor_copy(
                                    out=xnT[:, dk * 512 + ii * 128:
                                            dk * 512 + (ii + 1) * 128],
                                    in_=psT0[:, (i % 2) * 1024 + dk * 128:
                                             (i % 2) * 1024 + (dk + 1) * 128])
                            p.inc("l0trV", v.engine_nop())
                        block.vector(v_ev)

                    def t_yp(t, tg=tg):
                        p.wait(t, "l0trV", tg * 4 + 4)
                        if tg == 0:
                            p.wait(t, "ldwin", 16)
                            p.wait(t, "ldcd", 16)
                        for cc in range(NCC):
                            gi = tg * NCC + cc
                            if gi >= 2:
                                p.wait(t, "l0xz", 2 * (gi - 1))
                            last = None
                            for dk in range(ND):
                                last = t.matmul(
                                    psZ[:, (cc % 2) * 512:((cc % 2) + 1) * 512],
                                    winT[:, dk * C + cc * 128:dk * C + (cc + 1) * 128],
                                    xnT[:, dk * 512:(dk + 1) * 512],
                                    start=(dk == 0), stop=(dk == ND - 1))
                            p.inc("l0zT", last)
                            p.wait(t, "l0xz", 2 * gi + 2)
                            if gi >= 2:
                                p.wait(t, "l0sil", gi - 1)
                            last = None
                            for k in range(4):
                                last = t.matmul(
                                    psC2[:, (cc % 2) * 512:((cc % 2) + 1) * 512],
                                    cdiag[:, (cc * 4 + k) * 128:(cc * 4 + k + 1) * 128],
                                    xz_sb[:, (cc % 2) * 520 + k:(cc % 2) * 520 + k + 512],
                                    start=(k == 0), stop=(k == 3))
                            p.inc("l0cvT", last)
                    block.tensor(t_yp)

                    def s_yp(s, tg=tg):
                        for cc in range(NCC):
                            gi = tg * NCC + cc
                            p.wait(s, "l0zT", gi + 1)
                            if gi >= 2:
                                p.wait(s, "l0cvT", gi - 1)
                            if tg == 0 and cc == 0:
                                p.wait(s, "haloz", 1)
                            if tg >= 1 and cc == 0:
                                p.wait(s, "l0opT", tg * 8)
                            s.activation(out=xz_sb[:, (cc % 2) * 520 + 3:
                                                   (cc % 2) * 520 + 515],
                                         in_=psZ[:, (cc % 2) * 512:((cc % 2) + 1) * 512],
                                         func=ACT.Identity)
                            s.copy(out=xz_sb[:, (cc % 2) * 520:(cc % 2) * 520 + 3],
                                   in_=halo[:, cc * 4:cc * 4 + 3])
                            s.drain()
                            p.inc("l0xz", s.copy(
                                out=halo[:, cc * 4:cc * 4 + 3],
                                in_=xz_sb[:, (cc % 2) * 520 + 512:
                                          (cc % 2) * 520 + 515]), 2)
                            p.wait(s, "l0cvT", gi + 1)
                            p.inc("l0sil", s.activation(
                                out=y_sb[:, cc * 512:(cc + 1) * 512],
                                in_=psC2[:, (cc % 2) * 512:((cc % 2) + 1) * 512],
                                func=ACT.Silu, bias=cb_c[:, cc:cc + 1], scale=1.0))
                    block.scalar(s_yp)

                    def g_pl(g, tg=tg):
                        if _STAGE >= 2:
                            for ii in range(4):
                                tt = tg * 4 + ii
                                p.wait(g, "cc", NTG_M)
                                if tt >= 2:
                                    p.wait(g, "l0asm", tt - 1)
                                for pl in range(2):
                                    p.inc("gpl%d" % (tt % 2), g.indirect_dma_start(
                                        out=plane[:, ((tt % 2) * 2 + pl) * D:
                                                  ((tt % 2) * 2 + pl + 1) * D],
                                        out_offset=None,
                                        in_=ag_out,
                                        in_offset=IndirectOffsetOnAxis(
                                            gidx_sb[:, pl * 16 + tt:pl * 16 + tt + 1],
                                            0)), 16)
                    block.gpsimd(g_pl)

                    def t_op(t, tg=tg):
                        p.wait(t, "l0sil", (tg + 1) * NCC)
                        if tg == 0:
                            p.wait(t, "ldwo", 16)
                            p.wait(t, "cst", CST)
                        for ii in range(4):
                            tt = tg * 4 + ii
                            for oh in range(2):
                                if tt >= 1:
                                    p.wait(t, "l0asV", 2 * (tt - 1) + oh + 1)
                                last = None
                                for cc in range(NCC):
                                    last = t.matmul(
                                        psO[:, oh * 512:(oh + 1) * 512],
                                        y_sb[:, cc * 512 + ii * 128:
                                             cc * 512 + (ii + 1) * 128],
                                        woutT[:, cc * D + oh * 512:
                                              cc * D + (oh + 1) * 512],
                                        start=(cc == 0), stop=False)
                                last = t.matmul(
                                    psO[:, oh * 512:(oh + 1) * 512],
                                    ones_b[0:1, 0:128],
                                    rows_b[0:1, R_B0 * D + oh * 512:
                                           R_B0 * D + (oh + 1) * 512],
                                    start=False, stop=True)
                                p.inc("l0opT", last)
                    block.tensor(t_op)

                    def v_asm(v, tg=tg):
                        for ii in range(4):
                            tt = tg * 4 + ii
                            p.wait(v, "ldxC%d" % (tt % 4), (tt // 4 + 1) * 16)
                            if tt >= 2:
                                p.wait(v, "stx1%d" % (tt % 2), (tt // 2) * 16)
                            for oh in range(2):
                                p.wait(v, "l0opT", 2 * tt + oh + 1)
                                v.tensor_tensor(
                                    out=xorot[:, (tt % 2) * D + oh * 512:
                                              (tt % 2) * D + (oh + 1) * 512],
                                    in0=psO[:, oh * 512:(oh + 1) * 512],
                                    in1=xrot[:, (tt % 4) * D + oh * 512:
                                             (tt % 4) * D + (oh + 1) * 512],
                                    op=OP.add)
                                p.inc("l0asV", v.engine_nop())
                            v.drain()
                            if _STAGE >= 2:
                                p.wait(v, "gpl%d" % (tt % 2), (tt // 2 + 1) * 32)
                                for pl in range(2):
                                    v.tensor_tensor(
                                        out=xorot[:, (tt % 2) * D:((tt % 2) + 1) * D],
                                        in0=xorot[:, (tt % 2) * D:((tt % 2) + 1) * D],
                                        in1=plane[:, ((tt % 2) * 2 + pl) * D:
                                                  ((tt % 2) * 2 + pl + 1) * D],
                                        op=OP.add)
                                    v.drain()
                            if _STAGE >= 3:
                                ln_passA(v, xorot[:, (tt % 2) * D:((tt % 2) + 1) * D],
                                         tt)
                                p.inc("l2A", v.engine_nop())
                            p.inc("l0asm", v.engine_nop())
                    block.vector(v_asm)

                    def sy_st(sy, tg=tg):
                        for ii in range(4):
                            tt = tg * 4 + ii
                            p.wait(sy, "l0asm", tt + 1)
                            dst = d_out if _STAGE in (1, 2) else x1_d
                            p.inc("stx1%d" % (tt % 2), sy.dma_start(
                                out=dst[tt * 128:(tt + 1) * 128, :],
                                in_=xorot[:, (tt % 2) * D:((tt % 2) + 1) * D]), 16)
                    block.sync(sy_st)

        if _STAGE <= 2:
            def sy_fin(sy):
                p.wait(sy, "stx10", p.val("stx10"))
                p.wait(sy, "stx11", p.val("stx11"))
            block.sync(sy_fin)
            p.close()
            return nc

        mark("L1")
        _build_l1(nc, p, block, dict(
            d_x=d_x, d_Wq=d_Wq, d_Wk=d_Wk, d_Wv=d_Wv, d_Wo=d_Wo, d_W8T=d_W8T,
            d_out=d_out, x1_d=x1_d, q_d=q_d, qln_d=qln_d,
            ar_in=ar_in, ar_out=ar_out, m_d=m_d,
            xn2T=xn2T, xrot=xrot, xorot=xorot, ident_f=ident_f, ident_b=ident_b,
            ones_b=ones_b, ones_f=ones_f, rows_b=rows_b, bo_f=bo_f,
            cols_f=cols_f, gmax=gmax, armx=armx, xsp=xsp, xsb=xsb,
            stats_b=stats_b, mva=mva, rsa=rsa, ln1=ln1, lnsc=lnsc, lnbf=lnbf,
            bkh=bkh,
            row=row, col=col, CST=CST,
            R_B0=R_B0, R_BQ=R_BQ, R_BV=R_BV, R_BO=R_BO, R_LNB=R_LNB,
            R_WOB=R_WOB, C_BK8=C_BK8, C_G=C_G, C_B=C_B, C_WSC=C_WSC,
            C_B1=C_B1, C_SV=C_SV, C_LNB=C_LNB,
            ln_passA=ln_passA, ln_passB=ln_passB, ln_apply=ln_apply,
            mark=mark))
        p.close()
    return nc


def _build_l1(nc, p, block, e):
    g = dict(e)
    d_Wq, d_Wk, d_Wv, d_Wo, d_W8T = g["d_Wq"], g["d_Wk"], g["d_Wv"], g["d_Wo"], g["d_W8T"]
    d_out, x1_d, q_d, qln_d = g["d_out"], g["x1_d"], g["q_d"], g["qln_d"]
    ar_in, ar_out = g["ar_in"], g["ar_out"]
    m_d = g["m_d"]
    xn2T, xrot, xorot = g["xn2T"], g["xrot"], g["xorot"]
    ident_f, ident_b, ones_b, ones_f = g["ident_f"], g["ident_b"], g["ones_b"], g["ones_f"]
    rows_b, bo_f, cols_f = g["rows_b"], g["bo_f"], g["cols_f"]
    gmax, armx, xsp, xsb = g["gmax"], g["armx"], g["xsp"], g["xsb"]
    mva, rsa, ln1, lnsc, lnbf = g["mva"], g["rsa"], g["ln1"], g["lnsc"], g["lnbf"]
    bkh = g["bkh"]
    row, col, CST = g["row"], g["col"], g["CST"]
    R_BQ, R_BV, R_BO, R_WOB = g["R_BQ"], g["R_BV"], g["R_BO"], g["R_WOB"]
    C_BK8, C_G, C_B, C_WSC, C_B1, C_SV, C_LNB = (
        g["C_BK8"], g["C_G"], g["C_B"], g["C_WSC"], g["C_B1"], g["C_SV"], g["C_LNB"])
    ln_passA, ln_passB, ln_apply = g["ln_passA"], g["ln_passB"], g["ln_apply"]

    es0 = ExitStack()
    with es0:
        attn_acc = es0.enter_context(
            nc.sbuf_tensor("attn_acc", [128, NT * D], BF16))
        wfull = es0.enter_context(
            nc.sbuf_tensor("wfull", [128, ND * D], BF16))
        lnbc_b = es0.enter_context(nc.sbuf_tensor("lnbc_b", [128, ND], BF16))

        esA = ExitStack()
        with esA:
            sbA = lambda *a: esA.enter_context(nc.sbuf_tensor(*a))
            ktp = sbA("ktp", [64, T], BF16)
            VAUG = sbA("VAUG", [128, NT * H * 65], BF16)
            QM = sbA("QM", [65, 2 * T], BF16)
            KB = sbA("KB", [65, 2 * T], BF16)
            PT = sbA("PT", [128, 4 * D], BF16)
            qst = PT
            auT = sbA("auT", [80, 2 * T], BF16)
            blk = sbA("blk", [128, 2 * 16 * 80], BF16)

            nrmA = sbA("nrmA", [128, 32], F32)
            negm = sbA("negm", [128, 16], F32)
            rec_t = sbA("rec_t", [128, 16], F32)
            mst = sbA("mst", [16, 2 * 128], BF16)
            esQ = ExitStack()
            psQ = esQ.enter_context(nc.psum_tensor("psQ", [128, 1024], F32))
            psTq = esQ.enter_context(nc.psum_tensor("psTq", [128, 2048], BF16))

            L0ASM = p.val("l0asm")

            L2B = ln_passB(NT, "l2")

            e["mark"]("LN2C")
            for i in range(NT):
                def sy_ld(sy, i=i):
                    p.wait(sy, "stx1%d" % (i % 2), (i // 2 + 1) * 16)
                    if i >= 4:
                        p.wait(sy, "l2apC", i - 3)
                    p.inc("ldr%d" % (i % 4), sy.dma_start(
                        out=xrot[:, (i % 4) * D:((i % 4) + 1) * D],
                        in_=x1_d[i * 128:(i + 1) * 128, :]), 16)
                block.sync(sy_ld)

                def v_ap(v, i=i):
                    p.wait(v, "ldr%d" % (i % 4), (i // 4 + 1) * 16)
                    p.wait(v, "l2B", L2B)
                    if i >= 2:
                        p.wait(v, "l2trT", i - 1)
                    if _STAGE == 3 and i >= 1:
                        p.wait(v, "stdb%d" % ((i - 1) % 2), ((i - 1) // 2 + 1) * 16)
                    ln_apply(v, xrot[:, (i % 4) * D:((i % 4) + 1) * D],
                             lnbf[:, (i % 2) * D:((i % 2) + 1) * D], i)
                    v.drain()
                    v.tensor_reduce(out=ln1[:, 0:1],
                                    in_=lnbf[:, (i % 2) * D:((i % 2) + 1) * D],
                                    axis=AX,
                                    op=OP.max, apply_absolute_value=True)
                    v.drain()
                    if i == 0:
                        v.tensor_copy(out=gmax[:], in_=ln1[:, 0:1])
                    else:
                        v.tensor_tensor(out=gmax[:], in0=gmax[:],
                                        in1=ln1[:, 0:1], op=OP.max)
                    if _STAGE == 3:
                        v.tensor_copy(out=lnsc[:],
                                      in_=lnbf[:, (i % 2) * D:((i % 2) + 1) * D])
                    p.inc("l2apC", v.engine_nop())
                block.vector(v_ap)

                def t_tr(t, i=i):
                    p.wait(t, "l2apC", i + 1)
                    if i >= 2:
                        p.wait(t, "l2trV", i - 1)
                    if i == 0:
                        p.wait(t, "l0asm", L0ASM)
                    last = None
                    for dk in range(ND):
                        last = t.transpose(
                            psTq[:, (i % 2) * 1024 + dk * 128:
                                 (i % 2) * 1024 + (dk + 1) * 128],
                            lnbf[:, (i % 2) * D + dk * 128:
                                 (i % 2) * D + (dk + 1) * 128], ident_b[:])
                    p.inc("l2trT", last)
                block.tensor(t_tr)

                def v_ev(v, i=i):
                    p.wait(v, "l2trT", i + 1)
                    for dk in range(ND):
                        v.tensor_copy(
                            out=xn2T[:, dk * T + i * 128:dk * T + (i + 1) * 128],
                            in_=psTq[:, (i % 2) * 1024 + dk * 128:
                                     (i % 2) * 1024 + (dk + 1) * 128])
                    p.inc("l2trV", v.engine_nop())
                block.vector(v_ev)

                if _STAGE == 3:
                    def sy_dbg(sy, i=i):
                        p.wait(sy, "l2trV", i + 1)
                        p.inc("stdb%d" % (i % 2), sy.dma_start(
                            out=d_out[i * 128:(i + 1) * 128, :],
                            in_=lnsc[:]), 16)
                    block.sync(sy_dbg)

            if _STAGE == 3:
                def sy_fin(sy):
                    p.wait(sy, "stdb0", p.val("stdb0"))
                    p.wait(sy, "stdb1", p.val("stdb1"))
                block.sync(sy_fin)
                return


            e["mark"]("QP")
            def g_wq(gg):
                p.wait(gg, "l0opT", p.val("l0opT"))
                p.inc("ldWq", gg.dma_start(out=wfull[:], in_=d_Wq), 16)
            block.gpsimd(g_wq)

            for tt in range(NT):
                def t_q(t, tt=tt):
                    p.wait(t, "ldWq", 16)
                    p.wait(t, "l2trV", tt + 1)
                    if tt == 0:
                        p.wait(t, "cst", CST)
                    if tt >= 1:
                        p.wait(t, "qevS", tt)
                    for dh in range(2):
                        last = None
                        for dk in range(ND):
                            last = t.matmul(
                                psQ[:, dh * 512:(dh + 1) * 512],
                                xn2T[:, dk * T + tt * 128:dk * T + (tt + 1) * 128],
                                wfull[:, dk * D + dh * 512:dk * D + (dh + 1) * 512],
                                start=(dk == 0), stop=False)
                        last = t.matmul(
                            psQ[:, dh * 512:(dh + 1) * 512],
                            ones_b[0:1, 0:128],
                            rows_b[0:1, R_BQ * D + dh * 512:R_BQ * D + (dh + 1) * 512],
                            start=False, stop=True)
                    p.inc("qmmT", last)
                block.tensor(t_q)

                def s_q(s, tt=tt):
                    p.wait(s, "qmmT", tt + 1)
                    if tt >= 2:
                        p.wait(s, "stq%d" % (tt % 2), (tt // 2) * 16)
                    p.inc("qevS", s.activation(
                        out=xorot[:, (tt % 2) * D:((tt % 2) + 1) * D],
                        in_=psQ[:, 0:1024], func=ACT.Identity))
                block.scalar(s_q)

                def v_q(v, tt=tt):
                    p.wait(v, "qevS", tt + 1)
                    ln_passA(v, xorot[:, (tt % 2) * D:((tt % 2) + 1) * D], tt)
                    p.inc("qlA", v.engine_nop())
                block.vector(v_q)

                def sy_q(sy, tt=tt):
                    p.wait(sy, "qlA", tt + 1)
                    p.inc("stq%d" % (tt % 2), sy.dma_start(
                        out=q_d[tt * 128:(tt + 1) * 128, :],
                        in_=xorot[:, (tt % 2) * D:((tt % 2) + 1) * D]), 16)
                block.sync(sy_q)

            QLB = ln_passB(NT, "ql")

            e["mark"]("VP")
            def g_wv(gg):
                p.wait(gg, "qmmT", NT)
                p.inc("ldWv", gg.dma_start(out=wfull[:], in_=d_Wv), 16)
            block.gpsimd(g_wv)

            def v_vz(v):
                p.wait(v, "cst", CST)
                v.memset(VAUG[:, 64:NT * H * 65:65], 1.0)
                p.inc("vzV", v.engine_nop())
            block.vector(v_vz)

            for tt in range(NT):
                def t_v(t, tt=tt):
                    p.wait(t, "ldWv", 16)
                    p.wait(t, "l2trV", NT)
                    if tt == 0:
                        p.wait(t, "qevS", NT)
                    if tt >= 1:
                        p.wait(t, "vevV", tt)
                    for dh in range(2):
                        last = None
                        for dk in range(ND):
                            last = t.matmul(
                                psQ[:, dh * 512:(dh + 1) * 512],
                                xn2T[:, dk * T + tt * 128:dk * T + (tt + 1) * 128],
                                wfull[:, dk * D + dh * 512:dk * D + (dh + 1) * 512],
                                start=(dk == 0), stop=False)
                        last = t.matmul(
                            psQ[:, dh * 512:(dh + 1) * 512],
                            ones_b[0:1, 0:128],
                            rows_b[0:1, R_BV * D + dh * 512:R_BV * D + (dh + 1) * 512],
                            start=False, stop=True)
                    p.inc("vmmT", last)
                block.tensor(t_v)

                def v_v(v, tt=tt):
                    p.wait(v, "vmmT", tt + 1)
                    p.wait(v, "vzV", 1)
                    for dh in range(2):
                        base = (tt * H + dh * 8) * 65
                        dst = VAUG[:, base:base + 8 * 65].rearrange(
                            "p (a b) -> p a b", b=65)[:, :, 0:64]
                        srcp = psQ[:, dh * 512:(dh + 1) * 512].rearrange(
                            "p (a b) -> p a b", b=64)
                        v.tensor_copy(out=dst, in_=srcp)
                    p.inc("vevV", v.engine_nop())
                block.vector(v_v)

            def t_gm(t):
                p.wait(t, "l2apC", NT)
                p.wait(t, "vevV", p.val("vevV"))
                p.inc("gmT", t.transpose(psQ[0:1, 0:128], gmax[:, 0:1],
                                         ident_f[:]))
            block.tensor(t_gm)

            def v_gm(v):
                p.wait(v, "gmT", 1)
                v.tensor_copy(out=armx[:], in_=psQ[0:1, 0:128])
                p.inc("gmV", v.engine_nop())
            block.vector(v_gm)

            def sy_gm(sy):
                p.wait(sy, "gmV", 1)
                p.inc("argi", sy.dma_start(out=ar_in, in_=armx[:]), 16)
            block.sync(sy_gm)

            def g_ar(gg):
                p.wait(gg, "argi", 16)
                p.inc("cc2", gg.collective_compute(
                    "AllReduce", OP.max, replica_groups=[list(range(NCORES))],
                    ins=[ar_in], outs=[ar_out]))
            block.gpsimd(g_ar)

            def sy_gm2(sy):
                p.wait(sy, "cc2", 1)
                p.inc("argo", sy.dma_start(out=armx[:], in_=ar_out), 16)
            block.sync(sy_gm2)

            def v_xs(v):
                p.wait(v, "argo", 16)
                v.tensor_reduce(out=xsp[0:1, 0:1], in_=armx[:], axis=AX,
                                op=OP.max)
                v.drain()
                v.tensor_scalar(out=xsp[0:1, 0:1], in0=xsp[0:1, 0:1],
                                scalar1=1.0 / 127.0, scalar2=1e-5,
                                op0=OP.mult, op1=OP.max)
                v.drain()
                v.reciprocal(out=xsp[0:1, 1:2], in_=xsp[0:1, 0:1])
                p.inc("xsV", v.engine_nop())
            block.vector(v_xs)


            e["mark"]("QC")
            for i in range(NT):
                def sy_ld(sy, i=i):
                    p.wait(sy, "stq%d" % (i % 2), (i // 2 + 1) * 16)
                    if i >= 4:
                        p.wait(sy, "qcC", i - 3)
                    p.inc("ldq%d" % (i % 4), sy.dma_start(
                        out=xrot[:, (i % 4) * D:((i % 4) + 1) * D],
                        in_=q_d[i * 128:(i + 1) * 128, :]), 16)
                block.sync(sy_ld)

                def v_ap(v, i=i):
                    p.wait(v, "ldq%d" % (i % 4), (i // 4 + 1) * 16)
                    p.wait(v, "qlB", QLB)
                    if i >= 2:
                        p.wait(v, "qcT", i - 1)
                    ln_apply(v, xrot[:, (i % 4) * D:((i % 4) + 1) * D],
                             lnbf[:, (i % 2) * D:((i % 2) + 1) * D], i)
                    p.inc("qcC", v.engine_nop())
                block.vector(v_ap)

                def t_tr(t, i=i):
                    p.wait(t, "qcC", i + 1)
                    if i >= 2:
                        p.wait(t, "qcV", i - 1)
                    last = None
                    for dk in range(ND):
                        last = t.transpose(
                            psTq[:, (i % 2) * 1024 + dk * 128:
                                 (i % 2) * 1024 + (dk + 1) * 128],
                            lnbf[:, (i % 2) * D + dk * 128:
                                 (i % 2) * D + (dk + 1) * 128], ident_b[:])
                    p.inc("qcT", last)
                block.tensor(t_tr)

                def v_ev(v, i=i):
                    p.wait(v, "qcT", i + 1)
                    if i >= 2:
                        p.wait(v, "stqn%d" % (i % 2), (i // 2) * 16)
                    for dk in range(ND):
                        v.tensor_scalar(
                            out=qst[:, (i % 2) * D + dk * 128:
                                    (i % 2) * D + (dk + 1) * 128],
                            in0=psTq[:, (i % 2) * 1024 + dk * 128:
                                     (i % 2) * 1024 + (dk + 1) * 128],
                            scalar1=col(C_G)[:, dk:dk + 1],
                            scalar2=col(C_B)[:, dk:dk + 1],
                            op0=OP.mult, op1=OP.add)
                    p.inc("qcV", v.engine_nop())
                block.vector(v_ev)

                def sy_st(sy, i=i):
                    p.wait(sy, "qcV", i + 1)
                    dst = qln_d.rearrange("p (a b) -> p a b", a=ND)[
                        :, :, i * 128:(i + 1) * 128]
                    srcq = qst[:, (i % 2) * D:((i % 2) + 1) * D].rearrange(
                        "p (a b) -> p a b", b=128)
                    p.inc("stqn%d" % (i % 2), sy.dma_start(out=dst, in_=srcq), 16)
                block.gpsimd(sy_st)

            def t_xs(t):
                p.wait(t, "xsV", 1)
                p.inc("xsT", t.matmul(psQ[:, 0:2], ones_f[0:1, 0:128],
                                      xsp[0:1, 0:2], start=True, stop=True))
            block.tensor(t_xs)

            def v_xs2(v):
                p.wait(v, "xsT", 1)
                v.tensor_copy(out=xsb[:], in_=psQ[:, 0:2])
                v.drain()
                v.tensor_scalar(out=col(C_SV), in0=col(C_WSC),
                                scalar1=xsb[:, 0:1], scalar2=None, op0=OP.mult)
                v.drain()
                p.inc("xsV2", v.engine_nop())
            block.vector(v_xs2)

            def g_wk(gg):
                p.wait(gg, "vmmT", NT)
                p.inc("ldWk", gg.dma_start(out=wfull[:], in_=d_Wk), 16)
            block.gpsimd(g_wk)

            def v_kbz(v):
                v.memset(KB[64:65, 0:T], 1.0)
                v.memset(KB[64:65, T:2 * T], 1.0)
                v.memset(auT[64:80, :], 0.0)
                p.inc("kbzV", v.engine_nop())
            block.vector(v_kbz)

            esQ.close()
            esB = ExitStack()
            with esB:
                psA = esB.enter_context(nc.psum_tensor("psA", [128, 1024], F32))
                psSS = esB.enter_context(nc.psum_tensor("psSS", [128, 2048], F32))
                psS0 = psSS
                psV = esB.enter_context(nc.psum_tensor("psV", [65, 1024], F32))

                e["mark"]("ATT")
                def kT_pieces(ha):
                    ps = []
                    for ts in range(4):
                        def f(t, ha=ha, ts=ts):
                            if ts == 0:
                                p.wait(t, "ldWk", 16)
                                if ha == 0:
                                    p.wait(t, "qcV", NT)
                                    p.wait(t, "vevV", NT)
                                    p.wait(t, "xsV2", 1)
                                else:
                                    p.wait(t, "mstV", ha)
                                    p.wait(t, "kbV", ha)
                                    p.wait(t, "kevS", ha * 4)
                            if ts >= 2:
                                p.wait(t, "kevS", ha * 4 + ts - 1)
                            last = None
                            for dk in range(ND):
                                last = t.matmul(
                                    psA[0:64, (ts % 2) * 512:((ts % 2) + 1) * 512],
                                    wfull[:, dk * D + ha * 64:dk * D + ha * 64 + 64],
                                    xn2T[:, dk * T + ts * 512:dk * T + (ts + 1) * 512],
                                    start=(dk == 0), stop=(dk == ND - 1))
                            p.inc("kmm2", last)
                        ps.append(f)
                    return ps

                def a_pieces(ha):
                    ps = []
                    for qt in range(NT):
                        for sub in range(2):
                            def f(t, ha=ha, qt=qt, sub=sub):
                                if qt + sub == 0:
                                    p.wait(t, "qmd%d" % (ha % 2),
                                           (ha // 2 + 1) * 16)
                                    p.wait(t, "kevS", (ha + 1) * 4)
                                p.wait(t, "rmV", ha * 32 + qt * 2 + sub)
                                for jj in range(2):
                                    last = t.matmul(
                                        psA[:, jj * 512:(jj + 1) * 512],
                                        QM[0:64, (ha % 2) * T + qt * 128:
                                           (ha % 2) * T + (qt + 1) * 128],
                                        ktp[0:64,
                                            sub * 1024 + jj * 512:
                                            sub * 1024 + (jj + 1) * 512],
                                        start=True, stop=True)
                                p.inc("aT", last)
                            ps.append(f)

                    def fn(t, ha=ha):
                        p.wait(t, "rmhV", ha + 1)
                        p.inc("nmT", t.transpose(psA[0:16, 0:128],
                                                 negm[:, 0:16], ident_f[:]))
                    ps.append(fn)
                    return ps

                for h in range(-1, H):
                    ha = h + 1
                    e["mark"]("slot%d" % h)

                    def t_slot(t, h=h, ha=ha):
                        pieces = []
                        if ha < H:
                            pieces = kT_pieces(ha)[2:] + a_pieces(ha)
                        if ha + 1 < H:
                            pieces = pieces + kT_pieces(ha + 1)[:2]
                        if h < 0:
                            for f in kT_pieces(0)[:2]:
                                f(t)
                            for f in pieces:
                                f(t)
                            return
                        p.wait(t, "qmr%d" % (h % 2), (h // 2 + 1) * 16)
                        p.wait(t, "kbV", h + 1)
                        pend = []

                        def emit_av(t, h, half, j):
                            g = h * 32 + half * 16 + j
                            p.wait(t, "expS", g + 1)
                            if j == 0:
                                p.wait(t, "uevS", h * 2 + half)
                            for qs in range(2):
                                last = t.matmul(
                                    psV[0:65, qs * 512:(qs + 1) * 512],
                                    VAUG[:, (j * H + h) * 65:(j * H + h) * 65 + 65],
                                    PT[:, (g % 2) * D + qs * 512:
                                       (g % 2) * D + (qs + 1) * 512],
                                    start=(j == 0), stop=(j == NT - 1))
                            p.inc("avT", last)

                        for half in range(2):
                            for j in range(NT):
                                g = h * 32 + half * 16 + j
                                if g >= 2:
                                    p.wait(t, "expS", g - 1)
                                last = None
                                for qs in range(2):
                                    last = t.matmul(
                                        psSS[:, (g % 2) * 1024 + qs * 512:
                                             (g % 2) * 1024 + (qs + 1) * 512],
                                        KB[:, (h % 2) * T + j * 128:
                                           (h % 2) * T + (j + 1) * 128],
                                        QM[:, (h % 2) * T + half * 1024 + qs * 512:
                                           (h % 2) * T + half * 1024 + (qs + 1) * 512],
                                        start=True, stop=True)
                                p.inc("stmT", last)
                                for f in pend:
                                    f(t)
                                pend = [lambda t, h=h, half=half, j=j:
                                        emit_av(t, h, half, j)]
                                if pieces:
                                    pieces.pop(0)(t)
                                if pieces and len(pieces) > 30 - (half * 16 + j):
                                    pieces.pop(0)(t)
                        for f in pend:
                            f(t)
                        for f in pieces:
                            f(t)
                    block.tensor(t_slot)

                    def s_slot(s, h=h, ha=ha):
                        def kev(ts):
                            p.wait(s, "kmm2", ha * 4 + ts + 1)
                            p.inc("kevS", s.activation(
                                out=ktp[0:64, ts * 512:(ts + 1) * 512],
                                in_=psA[0:64, (ts % 2) * 512:((ts % 2) + 1) * 512],
                                func=ACT.Identity,
                                bias=bkh[:, ha:ha + 1], scale=0.125))
                        if h < 0:
                            if ha < H:
                                for ts in range(4):
                                    kev(ts)
                            return
                        if ha < H:
                            kev(0)
                            kev(1)
                        if h == 0:
                            p.wait(s, "stqn0", p.val("stqn0"))
                            p.wait(s, "stqn1", p.val("stqn1"))
                        for half in range(2):
                            for j in range(NT):
                                g = h * 32 + half * 16 + j
                                p.wait(s, "stmT", g + 1)
                                if g >= 2:
                                    p.wait(s, "avT", g - 1)
                                p.inc("expS", s.activation(
                                    out=PT[:, (g % 2) * D:(g % 2) * D + D],
                                    in_=psSS[:, (g % 2) * 1024:(g % 2) * 1024 + 1024],
                                    func=ACT.Exp))
                                if half == 0 and j == 1 and ha < H:
                                    kev(2)
                                    kev(3)
                            p.wait(s, "avT", h * 32 + half * 16 + 16)
                            if h >= 2:
                                p.wait(s, "tpD", (h - 1) * 256)
                            if h == 0 and half == 0:
                                p.wait(s, "kbzV", 1)
                            p.inc("uevS", s.activation(
                                out=auT[0:65, (h % 2) * T + half * 1024:
                                        (h % 2) * T + half * 1024 + 1024],
                                                                in_=psV[0:65, 0:1024], func=ACT.Identity))
                    block.scalar(s_slot)

                    def v_slot(v, h=h, ha=ha):
                        if ha < H:
                            p.wait(v, "kevS", (ha + 1) * 4)
                            p.wait(v, "kbzV", 1)
                            if ha >= 2:
                                p.wait(v, "stmT", (ha - 1) * 32)
                            v.tensor_copy(out=KB[0:64, (ha % 2) * T:((ha % 2) + 1) * T],
                                          in_=ktp[0:64, 0:T])
                            p.inc("kbV", v.engine_nop())
                            for qt in range(NT):
                                for sub in range(2):
                                    p.wait(v, "aT", ha * 32 + qt * 2 + sub + 1)
                                    p.inc("rmV", v.tensor_reduce(
                                        out=nrmA[:, qt * 2 + sub:qt * 2 + sub + 1],
                                        in_=psA[:, 0:1024], axis=AX, op=OP.max,
                                        negate=True))
                            v.drain()
                            v.tensor_tensor(out=negm[:, 0:16], in0=nrmA[:, 0:32:2],
                                            in1=nrmA[:, 1:32:2], op=OP.min)
                            p.inc("rmhV", v.engine_nop())
                            p.wait(v, "nmT", ha + 1)
                            if ha >= 2:
                                p.wait(v, "stmA%d" % (ha % 2), (ha // 2) * 16)
                            v.tensor_copy(out=mst[:, (ha % 2) * 128:((ha % 2) + 1) * 128],
                                          in_=psA[0:16, 0:128])
                            p.inc("mstV", v.engine_nop())
                        hb = h - 1
                        if 0 <= hb:
                            p.wait(v, "tpD", (hb + 1) * 256)
                            b0 = (hb % 2) * 1280
                            v.drain()
                            v.tensor_scalar(out=rec_t[:, 0:16],
                                            in0=blk[:, b0 + 64:b0 + 16 * 80:80],
                                            scalar1=1e-8, scalar2=None, op0=OP.add)
                            v.drain()
                            v.reciprocal(out=rec_t[:, 0:16], in_=rec_t[:, 0:16])
                            v.drain()
                            for tt in range(NT):
                                v.tensor_scalar(
                                    out=attn_acc[:, tt * D + hb * 64:
                                                 tt * D + hb * 64 + 64],
                                    in0=blk[:, b0 + tt * 80:b0 + tt * 80 + 64],
                                    scalar1=rec_t[:, tt:tt + 1], scalar2=None,
                                    op0=OP.mult)
                            p.inc("blkV", v.engine_nop())
                    block.vector(v_slot)

                    def sy_slot(sy, h=h, ha=ha):
                        if ha < H:
                            p.wait(sy, "stqn0", p.val("stqn0"))
                            p.wait(sy, "stqn1", p.val("stqn1"))
                            if ha >= 2:
                                p.wait(sy, "stmT", (ha - 1) * 32)
                            p.inc("qmd%d" % (ha % 2), sy.dma_start(
                                out=QM[0:64, (ha % 2) * T:((ha % 2) + 1) * T],
                                in_=qln_d[64 * (ha % 2):64 * (ha % 2) + 64,
                                          (ha // 2) * T:(ha // 2 + 1) * T]), 16)
                        hb = h - 1
                        if hb >= 0:
                            p.wait(sy, "uevS", hb * 2 + 2)
                            if hb >= 1:
                                p.wait(sy, "tpD", hb * 256)
                            if hb >= 2:
                                p.wait(sy, "blkV", hb)
                            for tt in range(NT):
                                p.inc("tpD", sy.dma_start_transpose(
                                    out=blk[:, (hb % 2) * 1280 + tt * 80:
                                            (hb % 2) * 1280 + (tt + 1) * 80],
                                    in_=auT[0:80, (hb % 2) * T + tt * 128:
                                            (hb % 2) * T + (tt + 1) * 128]), 16)
                        if ha < H:
                            p.wait(sy, "mstV", ha + 1)
                            p.inc("stmA%d" % (ha % 2), sy.dma_start(
                                out=m_d[ha % 2:ha % 2 + 1, :],
                                in_=mst[:, (ha % 2) * 128:((ha % 2) + 1) * 128]), 16)
                            p.wait(sy, "stmA%d" % (ha % 2), (ha // 2 + 1) * 16)
                            p.inc("qmr%d" % (ha % 2), sy.dma_start(
                                out=QM[64:65, (ha % 2) * T:((ha % 2) + 1) * T],
                                in_=m_d[ha % 2:ha % 2 + 1, :]), 16)
                    block.sync(sy_slot)

                def sy_tp15(sy):
                    p.wait(sy, "uevS", 15 * 2 + 2)
                    p.wait(sy, "tpD", 15 * 256)
                    p.wait(sy, "blkV", 15)
                    for tt in range(NT):
                        p.inc("tpD", sy.dma_start_transpose(
                            out=blk[:, (15 % 2) * 1280 + tt * 80:
                                    (15 % 2) * 1280 + (tt + 1) * 80],
                            in_=auT[0:80, (15 % 2) * T + tt * 128:
                                    (15 % 2) * T + (tt + 1) * 128]), 16)
                block.sync(sy_tp15)

                def v_qz(v):
                    p.wait(v, "xsV2", 1)
                    p.wait(v, "kmm2", 64)
                    for dk in range(ND):
                        for hf2 in range(2):
                            sl = slice(dk * T + hf2 * 1024, dk * T + (hf2 + 1) * 1024)
                            v.tensor_scalar(out=lnsc[:], in0=xn2T[:, sl],
                                            scalar1=xsb[:, 1:2], scalar2=RBIG,
                                            op0=OP.mult, op1=OP.add)
                            v.drain()
                            v.tensor_scalar(out=xn2T[:, sl], in0=lnsc[:],
                                            scalar1=RBIG, scalar2=None,
                                            op0=OP.subtract)
                            v.drain()
                    p.inc("qzV", v.engine_nop())
                block.vector(v_qz)

                def v_blk15(v):
                    p.wait(v, "tpD", H * 256)
                    b0 = (15 % 2) * 1280
                    v.drain()
                    v.tensor_scalar(out=rec_t[:, 0:16],
                                    in0=blk[:, b0 + 64:b0 + 16 * 80:80],
                                    scalar1=1e-8, scalar2=None, op0=OP.add)
                    v.drain()
                    v.reciprocal(out=rec_t[:, 0:16], in_=rec_t[:, 0:16])
                    v.drain()
                    for tt in range(NT):
                        v.tensor_scalar(
                            out=attn_acc[:, tt * D + 15 * 64:tt * D + 15 * 64 + 64],
                            in0=blk[:, b0 + tt * 80:b0 + tt * 80 + 64],
                            scalar1=rec_t[:, tt:tt + 1], scalar2=None,
                            op0=OP.mult)
                    p.inc("blkV", v.engine_nop())
                block.vector(v_blk15)

            e["mark"]("FFNWO")
            def g_wo(gg):
                p.wait(gg, "kmm2", 64)
                p.inc("ldWo", gg.dma_start(out=wfull[:], in_=d_Wo), 16)
            block.gpsimd(g_wo)

        esC = ExitStack()
        with esC:
            sbC = lambda *a: esC.enter_context(nc.sbuf_tensor(*a))
            w8b = sbC("w8b", [128, ND * D], BF16)
            ffnP = sbC("ffnP", [128, 2 * ND * 512], BF16)
            qbuf = sbC("qbuf", [128, ND * T], BF16)
            accT = sbC("accT", [128, 2 * D], BF16)
            wrow = sbC("wrow", [1, D], F32)
            psF = esC.enter_context(nc.psum_tensor("psF", [128, 1024], F32))
            psG = esC.enter_context(nc.psum_tensor("psG", [128, 2048], F32))
            psTc = esC.enter_context(nc.psum_tensor("psTc", [128, 1024], BF16))

            def g_w8(gg):
                p.wait(gg, "avT", 16 * 32)
                p.inc("ldW8", gg.dma_start(out=w8b[:], in_=d_W8T), 16)
            block.gpsimd(g_w8)


            def v_lnb(v):
                v.tensor_copy(out=lnbc_b[:], in_=col(C_LNB))
                p.inc("lnbV", v.engine_nop())
            block.vector(v_lnb)

            def t_wob(t):
                p.wait(t, "ldWo", 16)
                p.wait(t, "lnbV", 1)
                p.wait(t, "blkV", H)
                p.wait(t, "uevS", 32)
                last = None
                for oh in range(2):
                    for ch in range(ND):
                        last = t.matmul(
                            psF[0:1, oh * 512:(oh + 1) * 512],
                            lnbc_b[:, ch:ch + 1],
                            wfull[:, ch * D + oh * 512:ch * D + (oh + 1) * 512],
                            start=(ch == 0), stop=(ch == ND - 1))
                p.inc("wobT", last)
            block.tensor(t_wob)

            def v_wob(v):
                p.wait(v, "wobT", 1)
                v.tensor_tensor(out=wrow[:], in0=psF[0:1, 0:1024], in1=bo_f[:],
                                op=OP.add)
                v.drain()
                v.tensor_copy(out=rows_b[0:1, R_WOB * D:(R_WOB + 1) * D],
                              in_=wrow[:])
                p.inc("wobV", v.engine_nop())
            block.vector(v_wob)

            for ts in range(4):
                def t_f(t, ts=ts):
                    p.wait(t, "ldW8", 16)
                    p.wait(t, "qzV", 1)
                    p.wait(t, "wobV", 1)
                    if ts >= 2:
                        p.wait(t, "ffV", (ts - 1) * ND)
                    for nch in range(ND):
                        gi = ts * ND + nch
                        if gi >= 2:
                            p.wait(t, "ffV", gi - 1)
                        last = None
                        for dk in range(ND):
                            last = t.matmul(
                                psF[:, (nch % 2) * 512:((nch % 2) + 1) * 512],
                                w8b[:, dk * D + nch * 128:dk * D + (nch + 1) * 128],
                                xn2T[:, dk * T + ts * 512:dk * T + (ts + 1) * 512],
                                start=(dk == 0), stop=(dk == ND - 1))
                        p.inc("ffT", last)
                block.tensor(t_f)

                def v_f(v, ts=ts):
                    for nch in range(ND):
                        gi = ts * ND + nch
                        p.wait(v, "ffT", gi + 1)
                        v.tensor_scalar(
                            out=ffnP[:, (ts % 2) * ND * 512 + nch * 512:
                                     (ts % 2) * ND * 512 + (nch + 1) * 512],
                            in0=psF[:, (nch % 2) * 512:((nch % 2) + 1) * 512],
                            scalar1=col(C_SV)[:, nch:nch + 1],
                            scalar2=col(C_B1)[:, nch:nch + 1],
                            op0=OP.mult, op1=OP.add)
                        p.inc("ffV", v.engine_nop())
                block.vector(v_f)

                for sg in range(2):
                    tts = [ts * 4 + sg * 2, ts * 4 + sg * 2 + 1]

                    def t_acc(t, ts=ts, sg=sg, tts=tts):
                        p.wait(t, "blkV", H)
                        sgi = ts * 2 + sg
                        if sgi > 0:
                            p.wait(t, "finV", sgi * 2)
                        for k, tt in enumerate(tts):
                            if sgi * 2 + k > 0:
                                p.wait(t, "accV", sgi * 2 + k)
                            last = None
                            for ch in range(ND):
                                last = t.transpose(
                                    psTc[:, ch * 128:(ch + 1) * 128],
                                    attn_acc[:, tt * D + ch * 128:
                                             tt * D + (ch + 1) * 128],
                                    ident_b[:])
                            p.inc("accT2", last)
                    block.tensor(t_acc)

                    def v_acc(v, ts=ts, sg=sg, tts=tts):
                        for k, tt in enumerate(tts):
                            p.wait(v, "accT2", (ts * 2 + sg) * 2 + k + 1)
                            v.tensor_copy(out=accT[:, k * D:(k + 1) * D],
                                          in_=psTc[:, 0:1024])
                            p.inc("accV", v.engine_nop())
                    block.vector(v_acc)

                    if ts == 0 and sg == 0:
                        def sy_qb(sy):
                            p.wait(sy, "stqn0", p.val("stqn0"))
                            p.wait(sy, "stqn1", p.val("stqn1"))
                            p.wait(sy, "blkV", H)
                            p.inc("ldqb0", sy.dma_start(out=qbuf[:], in_=qln_d), 16)
                        block.sync(sy_qb)

                    def sy_x1(sy, ts=ts, sg=sg, tts=tts):
                        for k, tt in enumerate(tts):
                            if tt < 4:
                                p.wait(sy, "qcC", NT)
                            else:
                                p.wait(sy, "finV", tt - 3)
                            p.inc("ldf%d" % (tt % 4), sy.dma_start(
                                out=xrot[:, (tt % 4) * D:((tt % 4) + 1) * D],
                                in_=x1_d[tt * 128:(tt + 1) * 128, :]), 16)
                    block.sync(sy_x1)

                    def t_wo(t, ts=ts, sg=sg, tts=tts):
                        p.wait(t, "accV", (ts * 2 + sg) * 2 + 2)
                        p.wait(t, "wobV", 1)
                        if ts + sg > 0:
                            p.wait(t, "finV", (ts * 2 + sg) * 2)
                        for ch in range(ND):
                            gi = (ts * 2 + sg) * ND + ch
                            if ch == 0:
                                p.wait(t, "ldqb0", 16)
                            for k, tt in enumerate(tts):
                                for oh in range(2):
                                    t.matmul(
                                        psG[:, (k * 2 + oh) * 512:(k * 2 + oh + 1) * 512],
                                        accT[:, k * D + ch * 128:k * D + (ch + 1) * 128],
                                        wfull[:, ch * D + oh * 512:ch * D + (oh + 1) * 512],
                                        start=(ch == 0), stop=False)
                                    last = t.matmul(
                                        psG[:, (k * 2 + oh) * 512:(k * 2 + oh + 1) * 512],
                                        qbuf[:, ch * T + tt * 128:
                                             ch * T + (tt + 1) * 128],
                                        wfull[:, ch * D + oh * 512:ch * D + (oh + 1) * 512],
                                        start=False,
                                        stop=False)
                            p.inc("woT", last)
                        for k, tt in enumerate(tts):
                            for oh in range(2):
                                last = t.matmul(
                                    psG[:, (k * 2 + oh) * 512:(k * 2 + oh + 1) * 512],
                                    ones_b[0:1, 0:128],
                                    rows_b[0:1, R_BO * D + oh * 512:
                                           R_BO * D + (oh + 1) * 512],
                                    start=False, stop=True)
                        p.inc("woT2", last)
                    block.tensor(t_wo)

                    def t_ft(t, ts=ts, sg=sg, tts=tts):
                        p.wait(t, "woT2", (ts * 2 + sg) + 1)
                        p.wait(t, "ffV", (ts + 1) * ND)
                        for k, tt in enumerate(tts):
                            p.wait(t, "finV", (ts * 2 + sg) * 2 + k)
                            last = None
                            for nch in range(ND):
                                last = t.transpose(
                                    psTc[:, nch * 128:(nch + 1) * 128],
                                    ffnP[:, (ts % 2) * ND * 512 + nch * 512 +
                                         (tt % 4) * 128:
                                         (ts % 2) * ND * 512 + nch * 512 +
                                         (tt % 4) * 128 + 128],
                                    ident_b[:])
                            p.inc("ftT", last)
                    block.tensor(t_ft)

                    def v_fin(v, ts=ts, sg=sg, tts=tts):
                        for k, tt in enumerate(tts):
                            p.wait(v, "ldf%d" % (tt % 4), (tt // 4 + 1) * 16)
                            p.wait(v, "woT2", (ts * 2 + sg) + 1)
                            p.wait(v, "ftT", (ts * 2 + sg) * 2 + k + 1)
                            if tt >= 2:
                                p.wait(v, "stf%d" % (tt % 2), (tt // 2) * 16)
                            for oh in range(2):
                                v.tensor_tensor(
                                    out=xorot[:, (tt % 2) * D + oh * 512:
                                              (tt % 2) * D + (oh + 1) * 512],
                                    in0=psG[:, (k * 2 + oh) * 512:
                                            (k * 2 + oh + 1) * 512],
                                    in1=xrot[:, (tt % 4) * D + oh * 512:
                                             (tt % 4) * D + (oh + 1) * 512],
                                    op=OP.add)
                            v.drain()
                            if _STAGE >= 5:
                                v.tensor_copy(out=lnsc[:], in_=psTc[:, 0:1024])
                                v.drain()
                                v.tensor_tensor(
                                    out=xorot[:, (tt % 2) * D:((tt % 2) + 1) * D],
                                    in0=xorot[:, (tt % 2) * D:((tt % 2) + 1) * D],
                                    in1=lnsc[:],
                                    op=OP.add)
                                v.drain()
                            p.inc("finV", v.engine_nop())
                    block.vector(v_fin)

                    def sy_fo(sy, ts=ts, sg=sg, tts=tts):
                        for k, tt in enumerate(tts):
                            p.wait(sy, "finV", (ts * 2 + sg) * 2 + k + 1)
                            p.inc("stf%d" % (tt % 2), sy.dma_start(
                                out=d_out[tt * 128:(tt + 1) * 128, :],
                                in_=xorot[:, (tt % 2) * D:((tt % 2) + 1) * D]), 16)
                    block.sync(sy_fo)

            def sy_fin(sy):
                p.wait(sy, "stf0", p.val("stf0"))
                p.wait(sy, "stf1", p.val("stf1"))
            block.sync(sy_fin)



def kernel(**inputs):
    npad, per_core = _host_prep(inputs)
    nc = build(npad)
    res = run_bass_kernel_spmd(nc, per_core, core_ids=list(range(NCORES)))
    out = np.stack([res.results[i]["out"] for i in range(NCORES)], axis=0)
    return out.astype(np.float32)


# version 17
# speedup vs baseline: 1.1670x; 1.0180x over previous
import os
from contextlib import ExitStack
import numpy as np
import ml_dtypes
import concourse.bass as bass
import concourse.mybir as mybir
from concourse.bass import IndirectOffsetOnAxis
from concourse.bass_utils import run_bass_kernel_spmd

F32 = mybir.dt.float32
BF16 = mybir.dt.bfloat16
I32 = mybir.dt.int32
I8 = mybir.dt.int8
AX = mybir.AxisListType.X
OP = mybir.AluOpType
ACT = mybir.ActivationFunctionType

NCORES = 8
T = 2048
D = 1024
C = 2048
FF = 4096
H = 16
DH = 64
NT = T // 128
ND = D // 128
NCC = C // 128
NF = FF // 128
RBIG = 12582912.0
EPS = 1e-5

_STAGE = int(os.environ.get("MK_STAGE", "99"))



def _route_moe(expert_indices, expert_weights, x_flat):
    flat_idx = np.asarray(expert_indices).reshape(-1)
    flat_w = np.asarray(expert_weights).reshape(-1).astype(np.float32)
    order = np.argsort(flat_idx, kind="stable")
    tok = order // 2
    counts = np.bincount(flat_idx, minlength=NCORES)
    starts = np.concatenate([[0], np.cumsum(counts)])
    npad = max(512, int(np.ceil((counts.max() + 1) / 128.0) * 128))

    xs_l, fw_l = [], []
    for e in range(NCORES):
        n = int(counts[e])
        xs = np.zeros((npad, D), np.float32)
        fw = np.zeros((npad, 1), np.float32)
        xs[1:1 + n] = x_flat[tok[starts[e]:starts[e + 1]]]
        fw[1:1 + n, 0] = flat_w[order[starts[e]:starts[e + 1]]]
        xs_l.append(np.ascontiguousarray(xs))
        fw_l.append(np.ascontiguousarray(fw))

    expert_of_row = flat_idx[order]
    i_loc = 1 + (np.arange(len(order)) - starts[expert_of_row])
    chk = i_loc // 512
    c_k = np.minimum(512, npad - chk * 512)
    agrow = (NCORES * 512 * chk + expert_of_row * c_k + (i_loc - chk * 512))
    gidx_full = np.zeros((NCORES, 2, T), np.int64)
    occ = np.zeros(NCORES * T, np.int64)
    for r in range(len(order)):
        g = int(tok[r])
        gidx_full[g // T, occ[g], g % T] = agrow[r]
        occ[g] += 1
    gidx_l = []
    for c in range(NCORES):
        g2 = gidx_full[c].reshape(2, NT, 128).transpose(2, 0, 1).reshape(128, 32)
        gidx_l.append(np.ascontiguousarray(g2.astype(np.int32)))
    return npad, xs_l, fw_l, gidx_l


def _col8(v):
    return np.ascontiguousarray(np.asarray(v, np.float32).reshape(ND, 128).T)


def _lay(w):
    m = w.shape[1]
    return np.ascontiguousarray(
        np.asarray(w).reshape(ND, 128, m).transpose(1, 0, 2).reshape(128, ND * m))


def _host_prep(inputs):
    x = np.asarray(inputs["L0_attn_hidden_states"], np.float32)
    npad, xs_l, fw_l, gidx_l = _route_moe(
        inputs["L0_ffn_expert_indices"], inputs["L0_ffn_expert_weights"],
        x.reshape(-1, D))

    w_inT = np.asarray(inputs["L0_attn_in_proj_w"], np.float32)[:C].T
    w_outT = np.asarray(inputs["L0_attn_out_proj_w"], np.float32).T
    w_outT_l = np.ascontiguousarray(
        w_outT.reshape(NCC, 128, D).transpose(1, 0, 2).reshape(128, NCC * D))

    cw = np.asarray(inputs["L0_attn_conv_w"], np.float32)[:, 0, :]
    conv_diag = np.zeros((128, NCC * 4 * 128), np.float32)
    for cc in range(NCC):
        for k in range(4):
            conv_diag[:, (cc * 4 + k) * 128:(cc * 4 + k + 1) * 128] = \
                np.diag(cw[cc * 128:(cc + 1) * 128, k])

    w1 = np.asarray(inputs["L0_ffn_w1"], np.float32)
    w2 = np.asarray(inputs["L0_ffn_w2"], np.float32)
    w1_l = [np.ascontiguousarray(
        w1[e].reshape(ND, 128, NF, 128).transpose(2, 1, 0, 3).reshape(NF, 128, ND * 128))
        for e in range(NCORES)]
    w2_l = [np.ascontiguousarray(w2[e].reshape(NF, 128, D)) for e in range(NCORES)]

    shared = dict(
        w_inT=_lay(w_inT), w_outT=w_outT_l, conv_diag=conv_diag,
        conv_b=np.ascontiguousarray(np.asarray(inputs["L0_attn_conv_b"], np.float32).reshape(NCC, 128).T),
        bias0_row=np.ascontiguousarray(np.asarray(inputs["L0_ffn_bias"], np.float32)[None, :]),
        Wq=_lay(np.asarray(inputs["L1_attn_Wq"], np.float32)),
        Wk=_lay(np.asarray(inputs["L1_attn_Wk"], np.float32)),
        Wv=_lay(np.asarray(inputs["L1_attn_Wv"], np.float32)),
        Wo=_lay(np.asarray(inputs["L1_attn_Wo"], np.float32)),
        bq_row=np.ascontiguousarray(np.asarray(inputs["L1_attn_bq"], np.float32)[None, :]),
        bv_row=np.ascontiguousarray(np.asarray(inputs["L1_attn_bv"], np.float32)[None, :]),
        bo_row=np.ascontiguousarray(np.asarray(inputs["L1_attn_bo"], np.float32)[None, :]),
        lnb_col=_col8(inputs["L1_attn_ln_b"]),
        bk_h64=np.ascontiguousarray(
            np.asarray(inputs["L1_attn_bk"], np.float32).reshape(16, 64).T),
        g_col=_col8(inputs["L1_attn_ln_g"]),
        b_col=_col8(inputs["L1_attn_ln_b"]),
        W8T=np.ascontiguousarray(
            np.asarray(inputs["L1_ffn_weight_int8"]).T.reshape(ND, 128, D)
            .transpose(1, 0, 2).reshape(128, ND * D)),
        wscale_col=_col8(np.asarray(inputs["L1_ffn_weight_scale"], np.float32)[:, 0]),
        bias1_col=_col8(inputs["L1_ffn_bias"]),
        ident=np.eye(128, dtype=np.float32),
        ones_row=np.ones((1, T), np.float32),
    )
    per_core = []
    for i in range(NCORES):
        m = dict(shared)
        m["x"] = np.ascontiguousarray(x[i])
        m["xs"] = xs_l[i]
        m["fw"] = np.ascontiguousarray(fw_l[i].reshape(-1, 128).T)
        m["gidx"] = gidx_l[i]
        m["w1"] = w1_l[i]
        m["w2"] = w2_l[i]
        per_core.append(m)
    return npad, per_core



_ALIAS = {}
for _t, _names in {
    "mtrT": ["l0trT", "l2trT", "gmT", "xsT", "qmmT", "vmmT", "qcT"],
    "mlnA": ["l0A", "l2A", "qlA"],
    "init_v": ["haloz", "vzV", "kbzV", "lnbV", "gmV", "xsV", "xsV2"],
    "mlnBv": ["l0Bv", "l2Bv", "qlBv"],
    "mapC": ["l0apC", "l2apC", "qcC"],
    "mtrV": ["l0trV", "l2trV", "qcV"],
    "mlnB": ["l0B", "l2B", "qlB"],
    "qevS": ["kevS"],
    "dsl0": ["ldxsA0", "ldxsC0", "ldxA0", "ldxC0"],
    "dsl1": ["ldxsA1", "ldxsC1", "ldxA1", "ldxC1"],
    "dsl2": ["ldxA2", "ldxC2"],
    "dsl3": ["ldxA3", "ldxC3"],
    "dslB0": ["ldr0", "ldq0", "ldf0"],
    "dslB1": ["ldr1", "ldq1", "ldf1"],
    "dslB2": ["ldr2", "ldq2", "ldf2"],
    "dslB3": ["ldr3", "ldq3", "ldf3"],
    "stx10": ["stq0", "stf0", "stdb0"],
    "stx11": ["stq1", "stf1", "stdb1"],
    "qmA": ["qmd0", "ldqb0"],
    "qmB": ["qmd1", "ldqb1"],
    "ldWq": ["ldWv", "ldWk", "ldWo"],
    "argi": ["argo"],
    "cc": ["cc2"],
}.items():
    for _n in _names:
        _ALIAS[_n] = _t


class P:
    def __init__(self, nc):
        self.nc = nc
        self._phys = {}
        self._base = {}
        self._guards = []

    def _ph(self, name):
        ph = _ALIAS.get(name, name)
        if ph not in self._phys:
            g = self.nc.semaphore(ph)
            self._phys[ph] = [g.__enter__(), 0]
            self._guards.append(g)
        s = self._phys[ph]
        if name not in self._base:
            self._base[name] = s[1]
        return s, self._base[name]

    def _s(self, name):
        s, b = self._ph(name)
        return [s[0], s[1] - b]

    def inc(self, name, instr, n=1):
        s, b = self._ph(name)
        instr.then_inc(s[0], n)
        s[1] += n
        return s[1] - b

    def val(self, name):
        s, b = self._ph(name)
        return s[1] - b

    def wait(self, eng, name, v=None):
        s, b = self._ph(name)
        eng.wait_ge(s[0], b + ((s[1] - b) if v is None else v))

    def close(self):
        for g in reversed(self._guards):
            g.__exit__(None, None, None)



def build(npad):
    nc = bass.Bass("TRN2", target_bir_lowering=False)
    NPT = npad // 128
    NTG_M = (npad + 511) // 512
    tgw_M = [min(512, npad - 512 * t) for t in range(NTG_M)]

    def din(name, shape, dt=F32):
        return nc.dram_tensor(name, shape, dt, kind="ExternalInput").ap()

    d_x = din("x", [T, D])
    d_xs = din("xs", [npad, D])
    d_fw = din("fw", [128, NPT])
    d_gidx = din("gidx", [128, 32], I32)
    d_winT = din("w_inT", [128, ND * C])
    d_woutT = din("w_outT", [128, NCC * D])
    d_cdiag = din("conv_diag", [128, NCC * 4 * 128])
    d_cb = din("conv_b", [128, NCC])
    d_b0 = din("bias0_row", [1, D])
    d_w1 = din("w1", [NF, 128, ND * 128])
    d_w2 = din("w2", [NF, 128, D])
    d_Wq = din("Wq", [128, ND * D])
    d_Wk = din("Wk", [128, ND * D])
    d_Wv = din("Wv", [128, ND * D])
    d_Wo = din("Wo", [128, ND * D])
    d_bq = din("bq_row", [1, D])
    d_bv = din("bv_row", [1, D])
    d_bo = din("bo_row", [1, D])
    d_lnbc = din("lnb_col", [128, ND])
    d_bk = din("bk_h64", [64, 16])
    d_g = din("g_col", [128, ND])
    d_b = din("b_col", [128, ND])
    d_W8T = din("W8T", [128, ND * D], I8)
    d_wsc = din("wscale_col", [128, ND])
    d_b1 = din("bias1_col", [128, ND])
    d_id = din("ident", [128, 128])
    d_ones = din("ones_row", [1, T])
    d_out = nc.dram_tensor("out", [T, D], F32, kind="ExternalOutput").ap()

    ag_in = nc.dram_tensor("ag_in", [npad, D], F32).ap()
    ag_out = nc.dram_tensor("ag_out", [NCORES * npad, D], F32).ap()
    ar_in = nc.dram_tensor("ar_in", [1, 128], F32).ap()
    ar_out = nc.dram_tensor("ar_out", [1, 128], F32).ap()
    x1_d = nc.dram_tensor("x1_d", [T, D], F32).ap()
    q_d = nc.dram_tensor("q_d", [T, D], F32).ap()
    m_d = nc.dram_tensor("m_d", [2, 16 * 128], BF16).ap()
    qln_d = nc.dram_tensor("qln_d", [128, ND * T], BF16).ap()

    es = ExitStack()
    with es:
        _PH = globals().get("_PHASES", None)
        def mark(name):
            if _PH is not None:
                _PH.append((name, nc.n_instructions()))
        sb = lambda *a: es.enter_context(nc.sbuf_tensor(*a))
        xn2T = sb("xn2T", [128, ND * T], BF16)
        xrot = sb("xrot", [128, 4 * D], F32)
        xorot = sb("xorot", [128, 2 * D], F32)
        ident_f = sb("ident_f", [128, 128], F32)
        ident_b = sb("ident_b", [128, 128], BF16)
        ones_b = sb("ones_b", [1, 512], BF16)
        ones_f = sb("ones_f", [1, 128], F32)
        rows_b = sb("rows_b", [1, 6 * D], BF16)
        bo_f = sb("bo_f", [1, D], F32)
        cols_f = sb("cols_f", [128, 8 * ND], F32)
        cb_c = sb("cb_c", [128, NCC], F32)
        bkh = sb("bkh", [64, 16], F32)
        gmax = sb("gmax", [128, 1], F32)
        armx = sb("armx", [1, 128], F32)
        xsp = sb("xsp", [1, 2], F32)
        xsb = sb("xsb", [128, 2], F32)
        gidx_sb = sb("gidx_sb", [128, 32], I32)
        stats_b = sb("stats_b", [128, 12], F32)
        mva = sb("mva", [128, 2 * NT], F32)
        rsa = sb("rsa", [128, NT], F32)
        ln1 = sb("ln1", [128, NT], F32)
        lnsc = sb("lnsc", [128, D], F32)
        lnbf = sb("lnbf", [128, 2 * D], BF16)
        block = es.enter_context(nc.Block())
        p = P(nc)
        R_B0, R_BQ, R_BV, R_BO, R_LNB, R_WOB = range(6)

        def row(i):
            return rows_b[0:1, i * D:(i + 1) * D]

        C_BK8, C_G, C_B, C_WSC, C_B1, C_SV, C_LNB = range(7)

        def col(i):
            return cols_f[:, i * ND:(i + 1) * ND]

        def g_const(g):
            for dst, src in [(ident_f[:], d_id), (ident_b[:], d_id),
                             (ones_b[:], d_ones[0:1, 0:512]), (ones_f[:], d_ones[0:1, 0:128]),
                             (row(R_B0), d_b0), (row(R_BQ), d_bq),
                             (row(R_BV), d_bv), (row(R_BO), d_bo),
                             (bo_f[:], d_bo),
                             (bkh[:], d_bk), (col(C_G), d_g),
                             (col(C_B), d_b), (col(C_WSC), d_wsc),
                             (col(C_B1), d_b1), (col(C_LNB), d_lnbc),
                             (cb_c[:], d_cb), (gidx_sb[:], d_gidx)]:
                p.inc("cst", g.dma_start(out=dst, in_=src), 16)
        block.gpsimd(g_const)
        CST = p.val("cst")

        def v_init(v):
            p.wait(v, "cst", CST)
            v.tensor_scalar(out=bkh[:], in0=bkh[:], scalar1=0.125,
                            scalar2=None, op0=OP.mult)
            p.inc("init_v", v.engine_nop())
        block.vector(v_init)

        def ln_passA(v, xt, slot):
            v.bn_stats(out=stats_b[:, 0:6], in_=xt[:, 0:512])
            v.bn_stats(out=stats_b[:, 6:12], in_=xt[:, 512:1024])
            v.drain()
            v.bn_aggr(out=mva[:, 2 * slot:2 * slot + 2], in_=stats_b[:, 0:12])
            v.drain()

        def ln_passB(ntiles, tag):
            def fv(v):
                p.wait(v, tag + "A", ntiles)
                v.tensor_scalar(out=ln1[:, 0:ntiles], in0=mva[:, 1:2 * ntiles:2],
                                scalar1=EPS, scalar2=None, op0=OP.add)
                v.drain()
                v.reciprocal(out=rsa[:, 0:ntiles], in_=ln1[:, 0:ntiles])
                p.inc(tag + "Bv", v.engine_nop())
            block.vector(fv)

            def fs(s):
                p.wait(s, tag + "Bv")
                s.activation(out=rsa[:, 0:ntiles], in_=rsa[:, 0:ntiles],
                             func=ACT.Sqrt)
                p.inc(tag + "B", s.activation(out=ln1[:, 0:1], in_=ln1[:, 0:1],
                                              func=ACT.Identity))
            block.scalar(fs)
            return p.val(tag + "B")

        def ln_apply(v, xt, out, slot):
            v.tensor_scalar(out=out, in0=xt, scalar1=mva[:, 2 * slot:2 * slot + 1],
                            scalar2=rsa[:, slot:slot + 1], op0=OP.subtract,
                            op1=OP.mult)

        es0 = ExitStack()
        with es0:
            sb0 = lambda *a: es0.enter_context(nc.sbuf_tensor(*a))
            winT = sb0("winT", [128, ND * C], BF16)
            def g_l0w(g):
                p.inc("ldwin", g.dma_start(out=winT[:], in_=d_winT), 16)
            block.gpsimd(g_l0w)

            esM = ExitStack()
            with esM:
                sbM = lambda *a: esM.enter_context(nc.sbuf_tensor(*a))
                xs_t = sbM("xs_t", [128, 2 * D], F32)
                xsnT = sbM("xsnT", [128, ND * 512], BF16)
                hT = sbM("hT", [128, NF * 512], BF16)
                w1b = sbM("w1b", [128, 2 * ND * 128], BF16)
                w2b = sbM("w2b", [128, 2 * D], BF16)
                w1f = sbM("w1f", [128, 2 * ND * 128], F32)
                w2f = sbM("w2f", [128, 4 * D], F32)
                fw_sb = sbM("fw_sb", [128, NPT], F32)
                ys_sb = sbM("ys_sb", [128, 2 * D], F32)
                psM = esM.enter_context(nc.psum_tensor("psM", [128, 2048], F32))
                psTb = esM.enter_context(nc.psum_tensor("psTb", [128, 2048], BF16))
                def g_mld(g):
                    p.inc("ldfw", g.dma_start(out=fw_sb[:], in_=d_fw), 16)
                block.gpsimd(g_mld)

                mark("M_lnA")
                for i in range(NPT):
                    def sy_ld(sy, i=i):
                        if i >= 2:
                            p.wait(sy, "mlnA", i - 1)
                        p.inc("ldxsA%d" % (i % 2), sy.dma_start(
                            out=xs_t[:, (i % 2) * D:((i % 2) + 1) * D],
                            in_=d_xs[i * 128:(i + 1) * 128, :]), 16)
                    block.sync(sy_ld)

                    def v_st(v, i=i):
                        p.wait(v, "ldxsA%d" % (i % 2), (i // 2 + 1) * 16)
                        ln_passA(v, xs_t[:, (i % 2) * D:((i % 2) + 1) * D], i)
                        p.inc("mlnA", v.engine_nop())
                    block.vector(v_st)
                MLNB = ln_passB(NPT, "mln")

                mark("M_tg")
                XLN_K = [0]
                for tg in range(NTG_M):
                    tw = tgw_M[tg]
                    ntl = tw // 128
                    for ii in range(ntl):
                        i = tg * 4 + ii

                        def sy_ld(sy, i=i):
                            if i < 2:
                                p.wait(sy, "mlnA", NPT)
                            else:
                                p.wait(sy, "mtrV", i - 1)
                            if i >= 2:
                                p.wait(sy, "mapC", i - 1)
                            p.inc("ldxsC%d" % (i % 2), sy.dma_start(
                                out=xs_t[:, (i % 2) * D:((i % 2) + 1) * D],
                                in_=d_xs[i * 128:(i + 1) * 128, :]), 16)
                        block.sync(sy_ld)

                        def v_ap(v, i=i):
                            p.wait(v, "ldxsC%d" % (i % 2), (i // 2 + 1) * 16)
                            p.wait(v, "mlnB", MLNB)
                            if i >= 2:
                                p.wait(v, "mtrT", i - 1)
                            ln_apply(v, xs_t[:, (i % 2) * D:((i % 2) + 1) * D],
                                     lnbf[:, (i % 2) * D:((i % 2) + 1) * D], i)
                            p.inc("mapC", v.engine_nop())
                        block.vector(v_ap)

                        def t_tr(t, i=i):
                            p.wait(t, "mapC", i + 1)
                            p.wait(t, "cst", CST)
                            if i >= 2:
                                p.wait(t, "mtrV", i - 1)
                            if i >= 4:
                                p.wait(t, "mhT", (i // 4) * NF)
                            last = None
                            for dk in range(ND):
                                last = t.transpose(
                                    psTb[:, (i % 2) * 1024 + dk * 128:
                                         (i % 2) * 1024 + (dk + 1) * 128],
                                    lnbf[:, (i % 2) * D + dk * 128:
                                         (i % 2) * D + (dk + 1) * 128], ident_b[:])
                            p.inc("mtrT", last)
                        block.tensor(t_tr)

                        def v_ev(v, i=i, ii=ii):
                            p.wait(v, "mtrT", i + 1)
                            for dk in range(ND):
                                v.tensor_copy(
                                    out=xsnT[:, dk * 512 + ii * 128:
                                             dk * 512 + (ii + 1) * 128],
                                    in_=psTb[:, (i % 2) * 1024 + dk * 128:
                                             (i % 2) * 1024 + (dk + 1) * 128])
                            p.inc("mtrV", v.engine_nop())
                        block.vector(v_ev)

                    def sy_w1(sy, tg=tg):
                        for fc in range(NF):
                            gi = tg * NF + fc
                            if gi >= 2:
                                p.wait(sy, "w1cV", gi - 1)
                            p.inc("ldw1%d" % (fc % 2), sy.dma_start(
                                out=w1f[:, (fc % 2) * 1024:((fc % 2) + 1) * 1024],
                                in_=d_w1[fc]), 16)
                    block.sync(sy_w1)

                    def w1cast(s, gi):
                        fc2 = gi % NF
                        p.wait(s, "ldw1%d" % (fc2 % 2), (gi // 2 + 1) * 16)
                        if gi >= 2:
                            p.wait(s, "mhT", gi - 1)
                        p.inc("w1cV", s.copy(
                            out=w1b[:, (fc2 % 2) * 1024:((fc2 % 2) + 1) * 1024],
                            in_=w1f[:, (fc2 % 2) * 1024:((fc2 % 2) + 1) * 1024]))

                    def t_h(t, tg=tg, tw=tw):
                        p.wait(t, "mtrV", tg * 4 + tw // 128)
                        p.wait(t, "myV", p.val("myV"))
                        for fc in range(NF):
                            gi = tg * NF + fc
                            p.wait(t, "w1cV", gi + 1)
                            if gi >= 2:
                                p.wait(t, "mhS", gi - 1)
                            last = None
                            for dk in range(ND):
                                last = t.matmul(
                                    psM[:, (fc % 2) * 512:(fc % 2) * 512 + tw],
                                    w1b[:, (fc % 2) * 1024 + dk * 128:
                                        (fc % 2) * 1024 + (dk + 1) * 128],
                                    xsnT[:, dk * 512:dk * 512 + tw],
                                    start=(dk == 0), stop=(dk == ND - 1))
                            p.inc("mhT", last)
                    block.tensor(t_h)

                    def s_h(s, tg=tg, tw=tw):
                        if tg == 0:
                            w1cast(s, 0)
                            w1cast(s, 1)
                        if tg >= 1:
                            p.wait(s, "myT", tg * 2 * NF)
                        for fc in range(NF):
                            gi = tg * NF + fc
                            p.wait(s, "mhT", gi + 1)
                            p.inc("mhS", s.activation(
                                out=hT[:, fc * 512:fc * 512 + tw],
                                in_=psM[:, (fc % 2) * 512:(fc % 2) * 512 + tw],
                                func=ACT.Gelu))
                            if fc + 2 < NF:
                                w1cast(s, gi + 2)
                    block.scalar(s_h)

                    nhalf = (ntl + 1) // 2
                    for hf in range(nhalf):
                        tis = [hf * 2 + k for k in range(2) if hf * 2 + k < ntl]

                        def sy_w2(sy, tg=tg, hf=hf):
                            for fb in range(NF // 2):
                                gi = (tg * 2 + hf) * NF // 2 + fb
                                if gi >= 2:
                                    p.wait(sy, "w2cV", (gi - 1) * 2)
                                p.inc("ldw2%d" % (fb % 2), sy.dma_start(
                                    out=w2f[:, (fb % 2) * 2048:((fb % 2) + 1) * 2048
                                            ].rearrange("p (a m) -> p a m", a=2),
                                    in_=d_w2[fb * 2:fb * 2 + 2].rearrange(
                                        "a p m -> p a m")), 16)
                        block.sync(sy_w2)

                        def v_w2(v, tg=tg, hf=hf):
                            for fc in range(NF):
                                fb = fc // 2
                                gi = (tg * 2 + hf) * NF + fc
                                p.wait(v, "ldw2%d" % (fb % 2),
                                       ((tg * 2 + hf) * 8 + fb // 2 + 1) * 16)
                                if gi >= 2:
                                    p.wait(v, "myT", gi - 1)
                                p.inc("w2cV", v.tensor_copy(
                                    out=w2b[:, (fc % 2) * D:((fc % 2) + 1) * D],
                                    in_=w2f[:, (fb % 2) * 2048 + (fc % 2) * D:
                                            (fb % 2) * 2048 + ((fc % 2) + 1) * D]))
                        block.vector(v_w2)

                        def s_w2(s, tg=tg, hf=hf):
                            if hf == nhalf - 1 and tg + 1 < NTG_M:
                                w1cast(s, (tg + 1) * NF)
                                w1cast(s, (tg + 1) * NF + 1)
                        block.scalar(s_w2)

                        def t_y(t, tg=tg, hf=hf, tis=tis):
                            p.wait(t, "mhS", (tg + 1) * NF)
                            p.wait(t, "myV", p.val("myV"))
                            for fc in range(NF):
                                gi = (tg * 2 + hf) * NF + fc
                                p.wait(t, "w2cV", gi + 1)
                                last = None
                                for ti in tis:
                                    for oh in range(2):
                                        last = t.matmul(
                                            psM[:, ((ti % 2) * 2 + oh) * 512:
                                                ((ti % 2) * 2 + oh + 1) * 512],
                                            hT[:, fc * 512 + ti * 128:
                                               fc * 512 + (ti + 1) * 128],
                                            w2b[:, (fc % 2) * D + oh * 512:
                                                (fc % 2) * D + (oh + 1) * 512],
                                            start=(fc == 0), stop=(fc == NF - 1))
                                p.inc("myT", last)
                        block.tensor(t_y)

                        def v_y(v, tg=tg, tis=tis):
                            p.wait(v, "ldfw", 16)
                            p.wait(v, "stys", p.val("stys"))
                            p.wait(v, "myT")
                            for ti in tis:
                                for oh in range(2):
                                    v.tensor_scalar(
                                        out=ys_sb[:, (ti % 2) * D + oh * 512:
                                                  (ti % 2) * D + (oh + 1) * 512],
                                        in0=psM[:, ((ti % 2) * 2 + oh) * 512:
                                                ((ti % 2) * 2 + oh + 1) * 512],
                                        scalar1=fw_sb[:, tg * 4 + ti:tg * 4 + ti + 1],
                                        scalar2=None, op0=OP.mult)
                                p.inc("myV", v.engine_nop())
                        block.vector(v_y)

                        def sy_y(sy, tg=tg, tis=tis):
                            p.wait(sy, "myV")
                            for ti in tis:
                                r0 = tg * 512 + ti * 128
                                p.inc("stys", sy.dma_start(
                                    out=ag_in[r0:r0 + 128, :],
                                    in_=ys_sb[:, (ti % 2) * D:((ti % 2) + 1) * D]),
                                    16)
                        block.sync(sy_y)

                    xln_sched = [i for i in range(NT)
                                 if (i // 4 if i < NPT else i % NTG_M) == tg]
                    for i in xln_sched:
                        k = XLN_K[0]
                        XLN_K[0] += 1

                        def g_xld(g, i=i, k=k):
                            if k >= 4:
                                p.wait(g, "l0A", k - 3)
                            p.inc("xla%d" % (k % 4), g.dma_start(
                                out=xrot[:, (k % 4) * D:((k % 4) + 1) * D],
                                in_=d_x[i * 128:(i + 1) * 128, :]), 16)
                        block.gpsimd(g_xld)

                        def v_xst(v, i=i, k=k):
                            p.wait(v, "xla%d" % (k % 4), (k // 4 + 1) * 16)
                            ln_passA(v, xrot[:, (k % 4) * D:((k % 4) + 1) * D], i)
                            p.inc("l0A", v.engine_nop())
                        block.vector(v_xst)

                mark("M_ag")
                def g_ag(g):
                    for k in range(NTG_M):
                        c0 = k * 512
                        ck = min(512, npad - c0)
                        p.wait(g, "stys", ((c0 + ck) // 128) * 16)
                        p.inc("cc", g.collective_compute(
                            "AllGather", OP.bypass,
                            replica_groups=[list(range(NCORES))],
                            ins=[ag_in[c0:c0 + ck, :]],
                            outs=[ag_out[NCORES * c0:NCORES * (c0 + ck), :]]))
                block.gpsimd(g_ag)

            esL = ExitStack()
            with esL:
                sbL = lambda *a: esL.enter_context(nc.sbuf_tensor(*a))
                xnT = sbL("xnT", [128, ND * 512], BF16)
                woutT = sbL("woutT", [128, NCC * D], BF16)
                cdiag = sbL("cdiag", [128, NCC * 4 * 128], BF16)
                y_sb = sbL("y_sb", [128, NCC * 512], BF16)
                xz_sb = sbL("xz_sb", [128, 2 * 520], BF16)
                halo = sbL("halo", [128, NCC * 4], BF16)
                plane = sbL("plane", [128, 4 * D], F32)
                psZ = esL.enter_context(nc.psum_tensor("psZ", [128, 1024], F32))
                psC2 = esL.enter_context(nc.psum_tensor("psC2", [128, 1024], F32))
                psO = esL.enter_context(nc.psum_tensor("psO", [128, 1024], F32))
                psT0 = esL.enter_context(nc.psum_tensor("psT0", [128, 2048], BF16))
                def g_l0w2(g):
                    p.wait(g, "myV", p.val("myV"))
                    p.wait(g, "stys", p.val("stys"))
                    p.inc("ldcd", g.dma_start(out=cdiag[:], in_=d_cdiag), 16)
                    p.inc("ldwo", g.dma_start(out=woutT[:], in_=d_woutT), 16)
                block.gpsimd(g_l0w2)

                mark("L0_lnA")
                L0B = ln_passB(NT, "l0")

                def v_hz(v):
                    p.wait(v, "cst", CST)
                    p.wait(v, "stys", p.val("stys"))
                    p.wait(v, "myV", p.val("myV"))
                    v.memset(halo[:], 0.0)
                    p.inc("haloz", v.engine_nop())
                block.vector(v_hz)

                mark("L0_tg")
                for tg in range(4):
                    for ii in range(4):
                        i = tg * 4 + ii

                        def sy_ld(sy, i=i):
                            if i < 4:
                                p.wait(sy, "l0A", 13 + i)
                            else:
                                p.wait(sy, "l0asm", i - 3)
                            p.inc("ldxC%d" % (i % 4), sy.dma_start(
                                out=xrot[:, (i % 4) * D:((i % 4) + 1) * D],
                                in_=d_x[i * 128:(i + 1) * 128, :]), 16)
                        block.sync(sy_ld)

                        def v_ap(v, i=i):
                            p.wait(v, "ldxC%d" % (i % 4), (i // 4 + 1) * 16)
                            p.wait(v, "l0B", L0B)
                            if i >= 2:
                                p.wait(v, "l0trT", i - 1)
                            if _STAGE == 0 and i >= 2:
                                p.wait(v, "stdbg%d" % (i % 2), (i // 2) * 16)
                            ln_apply(v, xrot[:, (i % 4) * D:((i % 4) + 1) * D],
                                     lnbf[:, (i % 2) * D:((i % 2) + 1) * D]
                                     if _STAGE != 0 else
                                     xorot[:, (i % 2) * D:((i % 2) + 1) * D], i)
                            if _STAGE == 0:
                                v.drain()
                                v.tensor_copy(out=lnbf[:, (i % 2) * D:
                                                       ((i % 2) + 1) * D],
                                              in_=xorot[:, (i % 2) * D:((i % 2) + 1) * D])
                            p.inc("l0apC", v.engine_nop())
                        block.vector(v_ap)

                        if _STAGE == 0:
                            def sy_dbg(sy, i=i):
                                p.wait(sy, "l0apC", i + 1)
                                p.inc("stdbg%d" % (i % 2), sy.dma_start(
                                    out=d_out[i * 128:(i + 1) * 128, :],
                                    in_=xorot[:, (i % 2) * D:((i % 2) + 1) * D]), 16)
                            block.sync(sy_dbg)

                        def t_tr(t, i=i):
                            p.wait(t, "l0apC", i + 1)
                            if i >= 2:
                                p.wait(t, "l0trV", i - 1)
                            if i == 0:
                                p.wait(t, "myV", p.val("myV"))
                            if i >= 4:
                                p.wait(t, "l0zT", (i // 4) * NCC)
                            last = None
                            for dk in range(ND):
                                last = t.transpose(
                                    psT0[:, (i % 2) * 1024 + dk * 128:
                                         (i % 2) * 1024 + (dk + 1) * 128],
                                    lnbf[:, (i % 2) * D + dk * 128:
                                         (i % 2) * D + (dk + 1) * 128], ident_b[:])
                            p.inc("l0trT", last)
                        block.tensor(t_tr)

                        def v_ev(v, i=i, ii=ii):
                            p.wait(v, "l0trT", i + 1)
                            for dk in range(ND):
                                v.tensor_copy(
                                    out=xnT[:, dk * 512 + ii * 128:
                                            dk * 512 + (ii + 1) * 128],
                                    in_=psT0[:, (i % 2) * 1024 + dk * 128:
                                             (i % 2) * 1024 + (dk + 1) * 128])
                            p.inc("l0trV", v.engine_nop())
                        block.vector(v_ev)

                    def t_yp(t, tg=tg):
                        p.wait(t, "l0trV", tg * 4 + 4)
                        if tg == 0:
                            p.wait(t, "ldwin", 16)
                            p.wait(t, "ldcd", 16)
                        for cc in range(NCC):
                            gi = tg * NCC + cc
                            if gi >= 2:
                                p.wait(t, "l0xz", 2 * (gi - 1))
                            last = None
                            for dk in range(ND):
                                last = t.matmul(
                                    psZ[:, (cc % 2) * 512:((cc % 2) + 1) * 512],
                                    winT[:, dk * C + cc * 128:dk * C + (cc + 1) * 128],
                                    xnT[:, dk * 512:(dk + 1) * 512],
                                    start=(dk == 0), stop=(dk == ND - 1))
                            p.inc("l0zT", last)
                            p.wait(t, "l0xz", 2 * gi + 2)
                            if gi >= 2:
                                p.wait(t, "l0sil", gi - 1)
                            last = None
                            for k in range(4):
                                last = t.matmul(
                                    psC2[:, (cc % 2) * 512:((cc % 2) + 1) * 512],
                                    cdiag[:, (cc * 4 + k) * 128:(cc * 4 + k + 1) * 128],
                                    xz_sb[:, (cc % 2) * 520 + k:(cc % 2) * 520 + k + 512],
                                    start=(k == 0), stop=(k == 3))
                            p.inc("l0cvT", last)
                    block.tensor(t_yp)

                    def s_yp(s, tg=tg):
                        for cc in range(NCC):
                            gi = tg * NCC + cc
                            p.wait(s, "l0zT", gi + 1)
                            if gi >= 2:
                                p.wait(s, "l0cvT", gi - 1)
                            if tg == 0 and cc == 0:
                                p.wait(s, "haloz", 1)
                            if tg >= 1 and cc == 0:
                                p.wait(s, "l0opT", tg * 8)
                            s.activation(out=xz_sb[:, (cc % 2) * 520 + 3:
                                                   (cc % 2) * 520 + 515],
                                         in_=psZ[:, (cc % 2) * 512:((cc % 2) + 1) * 512],
                                         func=ACT.Identity)
                            s.copy(out=xz_sb[:, (cc % 2) * 520:(cc % 2) * 520 + 3],
                                   in_=halo[:, cc * 4:cc * 4 + 3])
                            s.drain()
                            p.inc("l0xz", s.copy(
                                out=halo[:, cc * 4:cc * 4 + 3],
                                in_=xz_sb[:, (cc % 2) * 520 + 512:
                                          (cc % 2) * 520 + 515]), 2)
                            p.wait(s, "l0cvT", gi + 1)
                            p.inc("l0sil", s.activation(
                                out=y_sb[:, cc * 512:(cc + 1) * 512],
                                in_=psC2[:, (cc % 2) * 512:((cc % 2) + 1) * 512],
                                func=ACT.Silu, bias=cb_c[:, cc:cc + 1], scale=1.0))
                    block.scalar(s_yp)

                    def g_pl(g, tg=tg):
                        if _STAGE >= 2:
                            for ii in range(4):
                                tt = tg * 4 + ii
                                p.wait(g, "cc", NTG_M)
                                if tt >= 2:
                                    p.wait(g, "l0asm", tt - 1)
                                for pl in range(2):
                                    p.inc("gpl%d" % (tt % 2), g.indirect_dma_start(
                                        out=plane[:, ((tt % 2) * 2 + pl) * D:
                                                  ((tt % 2) * 2 + pl + 1) * D],
                                        out_offset=None,
                                        in_=ag_out,
                                        in_offset=IndirectOffsetOnAxis(
                                            gidx_sb[:, pl * 16 + tt:pl * 16 + tt + 1],
                                            0)), 16)
                    block.gpsimd(g_pl)

                    def t_op(t, tg=tg):
                        p.wait(t, "l0sil", (tg + 1) * NCC)
                        if tg == 0:
                            p.wait(t, "ldwo", 16)
                            p.wait(t, "cst", CST)
                        for ii in range(4):
                            tt = tg * 4 + ii
                            for oh in range(2):
                                if tt >= 1:
                                    p.wait(t, "l0asV", 2 * (tt - 1) + oh + 1)
                                last = None
                                for cc in range(NCC):
                                    last = t.matmul(
                                        psO[:, oh * 512:(oh + 1) * 512],
                                        y_sb[:, cc * 512 + ii * 128:
                                             cc * 512 + (ii + 1) * 128],
                                        woutT[:, cc * D + oh * 512:
                                              cc * D + (oh + 1) * 512],
                                        start=(cc == 0), stop=False)
                                last = t.matmul(
                                    psO[:, oh * 512:(oh + 1) * 512],
                                    ones_b[0:1, 0:128],
                                    rows_b[0:1, R_B0 * D + oh * 512:
                                           R_B0 * D + (oh + 1) * 512],
                                    start=False, stop=True)
                                p.inc("l0opT", last)
                    block.tensor(t_op)

                    def v_asm(v, tg=tg):
                        for ii in range(4):
                            tt = tg * 4 + ii
                            p.wait(v, "ldxC%d" % (tt % 4), (tt // 4 + 1) * 16)
                            if tt >= 2:
                                p.wait(v, "stx1%d" % (tt % 2), (tt // 2) * 16)
                            for oh in range(2):
                                p.wait(v, "l0opT", 2 * tt + oh + 1)
                                v.tensor_tensor(
                                    out=xorot[:, (tt % 2) * D + oh * 512:
                                              (tt % 2) * D + (oh + 1) * 512],
                                    in0=psO[:, oh * 512:(oh + 1) * 512],
                                    in1=xrot[:, (tt % 4) * D + oh * 512:
                                             (tt % 4) * D + (oh + 1) * 512],
                                    op=OP.add)
                                p.inc("l0asV", v.engine_nop())
                            v.drain()
                            if _STAGE >= 2:
                                p.wait(v, "gpl%d" % (tt % 2), (tt // 2 + 1) * 32)
                                for pl in range(2):
                                    v.tensor_tensor(
                                        out=xorot[:, (tt % 2) * D:((tt % 2) + 1) * D],
                                        in0=xorot[:, (tt % 2) * D:((tt % 2) + 1) * D],
                                        in1=plane[:, ((tt % 2) * 2 + pl) * D:
                                                  ((tt % 2) * 2 + pl + 1) * D],
                                        op=OP.add)
                                    v.drain()
                            if _STAGE >= 3:
                                ln_passA(v, xorot[:, (tt % 2) * D:((tt % 2) + 1) * D],
                                         tt)
                                p.inc("l2A", v.engine_nop())
                            p.inc("l0asm", v.engine_nop())
                    block.vector(v_asm)

                    def sy_st(sy, tg=tg):
                        for ii in range(4):
                            tt = tg * 4 + ii
                            p.wait(sy, "l0asm", tt + 1)
                            dst = d_out if _STAGE in (1, 2) else x1_d
                            p.inc("stx1%d" % (tt % 2), sy.dma_start(
                                out=dst[tt * 128:(tt + 1) * 128, :],
                                in_=xorot[:, (tt % 2) * D:((tt % 2) + 1) * D]), 16)
                    block.sync(sy_st)

        if _STAGE <= 2:
            def sy_fin(sy):
                p.wait(sy, "stx10", p.val("stx10"))
                p.wait(sy, "stx11", p.val("stx11"))
            block.sync(sy_fin)
            p.close()
            return nc

        mark("L1")
        _build_l1(nc, p, block, dict(
            d_x=d_x, d_Wq=d_Wq, d_Wk=d_Wk, d_Wv=d_Wv, d_Wo=d_Wo, d_W8T=d_W8T,
            d_out=d_out, x1_d=x1_d, q_d=q_d, qln_d=qln_d,
            ar_in=ar_in, ar_out=ar_out, m_d=m_d,
            xn2T=xn2T, xrot=xrot, xorot=xorot, ident_f=ident_f, ident_b=ident_b,
            ones_b=ones_b, ones_f=ones_f, rows_b=rows_b, bo_f=bo_f,
            cols_f=cols_f, gmax=gmax, armx=armx, xsp=xsp, xsb=xsb,
            stats_b=stats_b, mva=mva, rsa=rsa, ln1=ln1, lnsc=lnsc, lnbf=lnbf,
            bkh=bkh,
            row=row, col=col, CST=CST,
            R_B0=R_B0, R_BQ=R_BQ, R_BV=R_BV, R_BO=R_BO, R_LNB=R_LNB,
            R_WOB=R_WOB, C_BK8=C_BK8, C_G=C_G, C_B=C_B, C_WSC=C_WSC,
            C_B1=C_B1, C_SV=C_SV, C_LNB=C_LNB,
            ln_passA=ln_passA, ln_passB=ln_passB, ln_apply=ln_apply,
            mark=mark))
        p.close()
    return nc


def _build_l1(nc, p, block, e):
    g = dict(e)
    d_Wq, d_Wk, d_Wv, d_Wo, d_W8T = g["d_Wq"], g["d_Wk"], g["d_Wv"], g["d_Wo"], g["d_W8T"]
    d_out, x1_d, q_d, qln_d = g["d_out"], g["x1_d"], g["q_d"], g["qln_d"]
    ar_in, ar_out = g["ar_in"], g["ar_out"]
    m_d = g["m_d"]
    xn2T, xrot, xorot = g["xn2T"], g["xrot"], g["xorot"]
    ident_f, ident_b, ones_b, ones_f = g["ident_f"], g["ident_b"], g["ones_b"], g["ones_f"]
    rows_b, bo_f, cols_f = g["rows_b"], g["bo_f"], g["cols_f"]
    gmax, armx, xsp, xsb = g["gmax"], g["armx"], g["xsp"], g["xsb"]
    mva, rsa, ln1, lnsc, lnbf = g["mva"], g["rsa"], g["ln1"], g["lnsc"], g["lnbf"]
    bkh = g["bkh"]
    row, col, CST = g["row"], g["col"], g["CST"]
    R_BQ, R_BV, R_BO, R_WOB = g["R_BQ"], g["R_BV"], g["R_BO"], g["R_WOB"]
    C_BK8, C_G, C_B, C_WSC, C_B1, C_SV, C_LNB = (
        g["C_BK8"], g["C_G"], g["C_B"], g["C_WSC"], g["C_B1"], g["C_SV"], g["C_LNB"])
    ln_passA, ln_passB, ln_apply = g["ln_passA"], g["ln_passB"], g["ln_apply"]

    es0 = ExitStack()
    with es0:
        attn_acc = es0.enter_context(
            nc.sbuf_tensor("attn_acc", [128, NT * D], BF16))
        wfull = es0.enter_context(
            nc.sbuf_tensor("wfull", [128, ND * D], BF16))
        lnbc_b = es0.enter_context(nc.sbuf_tensor("lnbc_b", [128, ND], BF16))

        esA = ExitStack()
        with esA:
            sbA = lambda *a: esA.enter_context(nc.sbuf_tensor(*a))
            ktp = sbA("ktp", [64, T], BF16)
            VAUG = sbA("VAUG", [128, NT * H * 65], BF16)
            QM = sbA("QM", [65, 2 * T], BF16)
            KB = sbA("KB", [65, 2 * T], BF16)
            PT = sbA("PT", [128, 4 * D], BF16)
            qst = PT
            auT = sbA("auT", [80, 2 * T], BF16)
            blk = sbA("blk", [128, 2 * 16 * 80], BF16)

            nrmA = sbA("nrmA", [128, 32], F32)
            negm = sbA("negm", [128, 16], F32)
            rec_t = sbA("rec_t", [128, 16], F32)
            mst = sbA("mst", [16, 2 * 128], BF16)
            esQ = ExitStack()
            psQ = esQ.enter_context(nc.psum_tensor("psQ", [128, 2048], F32))
            psTq = esQ.enter_context(nc.psum_tensor("psTq", [128, 2048], BF16))

            L0ASM = p.val("l0asm")

            L2B = ln_passB(NT, "l2")

            e["mark"]("LN2C")
            for i in range(NT):
                def sy_ld(sy, i=i):
                    p.wait(sy, "stx1%d" % (i % 2), (i // 2 + 1) * 16)
                    if i >= 4:
                        p.wait(sy, "l2apC", i - 3)
                    p.inc("ldr%d" % (i % 4), sy.dma_start(
                        out=xrot[:, (i % 4) * D:((i % 4) + 1) * D],
                        in_=x1_d[i * 128:(i + 1) * 128, :]), 16)
                block.sync(sy_ld)

                def v_ap(v, i=i):
                    p.wait(v, "ldr%d" % (i % 4), (i // 4 + 1) * 16)
                    p.wait(v, "l2B", L2B)
                    if i >= 2:
                        p.wait(v, "l2trT", i - 1)
                    if _STAGE == 3 and i >= 1:
                        p.wait(v, "stdb%d" % ((i - 1) % 2), ((i - 1) // 2 + 1) * 16)
                    ln_apply(v, xrot[:, (i % 4) * D:((i % 4) + 1) * D],
                             lnbf[:, (i % 2) * D:((i % 2) + 1) * D], i)
                    v.drain()
                    v.tensor_reduce(out=ln1[:, 0:1],
                                    in_=lnbf[:, (i % 2) * D:((i % 2) + 1) * D],
                                    axis=AX,
                                    op=OP.max, apply_absolute_value=True)
                    v.drain()
                    if i == 0:
                        v.tensor_copy(out=gmax[:], in_=ln1[:, 0:1])
                    else:
                        v.tensor_tensor(out=gmax[:], in0=gmax[:],
                                        in1=ln1[:, 0:1], op=OP.max)
                    if _STAGE == 3:
                        v.tensor_copy(out=lnsc[:],
                                      in_=lnbf[:, (i % 2) * D:((i % 2) + 1) * D])
                    p.inc("l2apC", v.engine_nop())
                block.vector(v_ap)

                def t_tr(t, i=i):
                    p.wait(t, "l2apC", i + 1)
                    if i >= 2:
                        p.wait(t, "l2trV", i - 1)
                    if i == 0:
                        p.wait(t, "l0asm", L0ASM)
                    last = None
                    for dk in range(ND):
                        last = t.transpose(
                            psTq[:, (i % 2) * 1024 + dk * 128:
                                 (i % 2) * 1024 + (dk + 1) * 128],
                            lnbf[:, (i % 2) * D + dk * 128:
                                 (i % 2) * D + (dk + 1) * 128], ident_b[:])
                    p.inc("l2trT", last)
                block.tensor(t_tr)

                def v_ev(v, i=i):
                    p.wait(v, "l2trT", i + 1)
                    for dk in range(ND):
                        v.tensor_copy(
                            out=xn2T[:, dk * T + i * 128:dk * T + (i + 1) * 128],
                            in_=psTq[:, (i % 2) * 1024 + dk * 128:
                                     (i % 2) * 1024 + (dk + 1) * 128])
                    p.inc("l2trV", v.engine_nop())
                block.vector(v_ev)

                if _STAGE == 3:
                    def sy_dbg(sy, i=i):
                        p.wait(sy, "l2trV", i + 1)
                        p.inc("stdb%d" % (i % 2), sy.dma_start(
                            out=d_out[i * 128:(i + 1) * 128, :],
                            in_=lnsc[:]), 16)
                    block.sync(sy_dbg)

            if _STAGE == 3:
                def sy_fin(sy):
                    p.wait(sy, "stdb0", p.val("stdb0"))
                    p.wait(sy, "stdb1", p.val("stdb1"))
                block.sync(sy_fin)
                return


            e["mark"]("QP")
            def g_wq(gg):
                p.wait(gg, "l0opT", p.val("l0opT"))
                p.inc("ldWq", gg.dma_start(out=wfull[:], in_=d_Wq), 16)
            block.gpsimd(g_wq)

            for tt in range(NT):
                def t_q(t, tt=tt):
                    p.wait(t, "ldWq", 16)
                    p.wait(t, "l2trV", tt + 1)
                    if tt == 0:
                        p.wait(t, "cst", CST)
                    if tt >= 2:
                        p.wait(t, "qevS", tt - 1)
                    for dh in range(2):
                        last = None
                        for dk in range(ND):
                            last = t.matmul(
                                psQ[:, (tt % 2) * 1024 + dh * 512:
                                    (tt % 2) * 1024 + (dh + 1) * 512],
                                xn2T[:, dk * T + tt * 128:dk * T + (tt + 1) * 128],
                                wfull[:, dk * D + dh * 512:dk * D + (dh + 1) * 512],
                                start=(dk == 0), stop=False)
                        last = t.matmul(
                            psQ[:, (tt % 2) * 1024 + dh * 512:
                                (tt % 2) * 1024 + (dh + 1) * 512],
                            ones_b[0:1, 0:128],
                            rows_b[0:1, R_BQ * D + dh * 512:R_BQ * D + (dh + 1) * 512],
                            start=False, stop=True)
                    p.inc("qmmT", last)
                block.tensor(t_q)

                def s_q(s, tt=tt):
                    p.wait(s, "qmmT", tt + 1)
                    if tt >= 2:
                        p.wait(s, "stq%d" % (tt % 2), (tt // 2) * 16)
                    p.inc("qevS", s.activation(
                        out=xorot[:, (tt % 2) * D:((tt % 2) + 1) * D],
                        in_=psQ[:, (tt % 2) * 1024:(tt % 2) * 1024 + 1024],
                        func=ACT.Identity))
                block.scalar(s_q)

                def v_q(v, tt=tt):
                    p.wait(v, "qevS", tt + 1)
                    ln_passA(v, xorot[:, (tt % 2) * D:((tt % 2) + 1) * D], tt)
                    p.inc("qlA", v.engine_nop())
                block.vector(v_q)

                def sy_q(sy, tt=tt):
                    p.wait(sy, "qlA", tt + 1)
                    p.inc("stq%d" % (tt % 2), sy.dma_start(
                        out=q_d[tt * 128:(tt + 1) * 128, :],
                        in_=xorot[:, (tt % 2) * D:((tt % 2) + 1) * D]), 16)
                block.sync(sy_q)

            QLB = ln_passB(NT, "ql")

            e["mark"]("VP")
            def g_wv(gg):
                p.wait(gg, "qmmT", NT)
                p.inc("ldWv", gg.dma_start(out=wfull[:], in_=d_Wv), 16)
            block.gpsimd(g_wv)

            def v_vz(v):
                p.wait(v, "cst", CST)
                v.memset(VAUG[:, 64:NT * H * 65:65], 1.0)
                p.inc("vzV", v.engine_nop())
            block.vector(v_vz)

            for tt in range(NT):
                def t_v(t, tt=tt):
                    p.wait(t, "ldWv", 16)
                    p.wait(t, "l2trV", NT)
                    if tt == 0:
                        p.wait(t, "qevS", NT)
                    if tt >= 1:
                        p.wait(t, "vevV", tt)
                    for dh in range(2):
                        last = None
                        for dk in range(ND):
                            last = t.matmul(
                                psQ[:, dh * 512:(dh + 1) * 512],
                                xn2T[:, dk * T + tt * 128:dk * T + (tt + 1) * 128],
                                wfull[:, dk * D + dh * 512:dk * D + (dh + 1) * 512],
                                start=(dk == 0), stop=False)
                        last = t.matmul(
                            psQ[:, dh * 512:(dh + 1) * 512],
                            ones_b[0:1, 0:128],
                            rows_b[0:1, R_BV * D + dh * 512:R_BV * D + (dh + 1) * 512],
                            start=False, stop=True)
                    p.inc("vmmT", last)
                block.tensor(t_v)

                def v_v(v, tt=tt):
                    p.wait(v, "vmmT", tt + 1)
                    p.wait(v, "vzV", 1)
                    for dh in range(2):
                        base = (tt * H + dh * 8) * 65
                        dst = VAUG[:, base:base + 8 * 65].rearrange(
                            "p (a b) -> p a b", b=65)[:, :, 0:64]
                        srcp = psQ[:, dh * 512:(dh + 1) * 512].rearrange(
                            "p (a b) -> p a b", b=64)
                        v.tensor_copy(out=dst, in_=srcp)
                    p.inc("vevV", v.engine_nop())
                block.vector(v_v)

            def t_gm(t):
                p.wait(t, "l2apC", NT)
                p.wait(t, "vevV", p.val("vevV"))
                p.inc("gmT", t.transpose(psQ[0:1, 0:128], gmax[:, 0:1],
                                         ident_f[:]))
            block.tensor(t_gm)

            def v_gm(v):
                p.wait(v, "gmT", 1)
                v.tensor_copy(out=armx[:], in_=psQ[0:1, 0:128])
                p.inc("gmV", v.engine_nop())
            block.vector(v_gm)

            def sy_gm(sy):
                p.wait(sy, "gmV", 1)
                p.inc("argi", sy.dma_start(out=ar_in, in_=armx[:]), 16)
            block.sync(sy_gm)

            def g_ar(gg):
                p.wait(gg, "argi", 16)
                p.inc("cc2", gg.collective_compute(
                    "AllReduce", OP.max, replica_groups=[list(range(NCORES))],
                    ins=[ar_in], outs=[ar_out]))
            block.gpsimd(g_ar)

            def sy_gm2(sy):
                p.wait(sy, "cc2", 1)
                p.inc("argo", sy.dma_start(out=armx[:], in_=ar_out), 16)
            block.sync(sy_gm2)

            def v_xs(v):
                p.wait(v, "argo", 16)
                v.tensor_reduce(out=xsp[0:1, 0:1], in_=armx[:], axis=AX,
                                op=OP.max)
                v.drain()
                v.tensor_scalar(out=xsp[0:1, 0:1], in0=xsp[0:1, 0:1],
                                scalar1=1.0 / 127.0, scalar2=1e-5,
                                op0=OP.mult, op1=OP.max)
                v.drain()
                v.reciprocal(out=xsp[0:1, 1:2], in_=xsp[0:1, 0:1])
                p.inc("xsV", v.engine_nop())
            block.vector(v_xs)


            e["mark"]("QC")
            for i in range(NT):
                def sy_ld(sy, i=i):
                    p.wait(sy, "stq%d" % (i % 2), (i // 2 + 1) * 16)
                    if i >= 4:
                        p.wait(sy, "qcC", i - 3)
                    p.inc("ldq%d" % (i % 4), sy.dma_start(
                        out=xrot[:, (i % 4) * D:((i % 4) + 1) * D],
                        in_=q_d[i * 128:(i + 1) * 128, :]), 16)
                block.sync(sy_ld)

                def v_ap(v, i=i):
                    p.wait(v, "ldq%d" % (i % 4), (i // 4 + 1) * 16)
                    p.wait(v, "qlB", QLB)
                    if i >= 2:
                        p.wait(v, "qcT", i - 1)
                    ln_apply(v, xrot[:, (i % 4) * D:((i % 4) + 1) * D],
                             lnbf[:, (i % 2) * D:((i % 2) + 1) * D], i)
                    p.inc("qcC", v.engine_nop())
                block.vector(v_ap)

                def t_tr(t, i=i):
                    p.wait(t, "qcC", i + 1)
                    if i >= 2:
                        p.wait(t, "qcV", i - 1)
                    last = None
                    for dk in range(ND):
                        last = t.transpose(
                            psTq[:, (i % 2) * 1024 + dk * 128:
                                 (i % 2) * 1024 + (dk + 1) * 128],
                            lnbf[:, (i % 2) * D + dk * 128:
                                 (i % 2) * D + (dk + 1) * 128], ident_b[:])
                    p.inc("qcT", last)
                block.tensor(t_tr)

                def v_ev(v, i=i):
                    p.wait(v, "qcT", i + 1)
                    if i >= 2:
                        p.wait(v, "stqn%d" % (i % 2), (i // 2) * 16)
                    for dk in range(ND):
                        v.tensor_scalar(
                            out=qst[:, (i % 2) * D + dk * 128:
                                    (i % 2) * D + (dk + 1) * 128],
                            in0=psTq[:, (i % 2) * 1024 + dk * 128:
                                     (i % 2) * 1024 + (dk + 1) * 128],
                            scalar1=col(C_G)[:, dk:dk + 1],
                            scalar2=col(C_B)[:, dk:dk + 1],
                            op0=OP.mult, op1=OP.add)
                    p.inc("qcV", v.engine_nop())
                block.vector(v_ev)

                def sy_st(sy, i=i):
                    p.wait(sy, "qcV", i + 1)
                    dst = qln_d.rearrange("p (a b) -> p a b", a=ND)[
                        :, :, i * 128:(i + 1) * 128]
                    srcq = qst[:, (i % 2) * D:((i % 2) + 1) * D].rearrange(
                        "p (a b) -> p a b", b=128)
                    p.inc("stqn%d" % (i % 2), sy.dma_start(out=dst, in_=srcq), 16)
                block.gpsimd(sy_st)

            def t_xs(t):
                p.wait(t, "xsV", 1)
                p.inc("xsT", t.matmul(psQ[:, 0:2], ones_f[0:1, 0:128],
                                      xsp[0:1, 0:2], start=True, stop=True))
            block.tensor(t_xs)

            def v_xs2(v):
                p.wait(v, "xsT", 1)
                v.tensor_copy(out=xsb[:], in_=psQ[:, 0:2])
                v.drain()
                v.tensor_scalar(out=col(C_SV), in0=col(C_WSC),
                                scalar1=xsb[:, 0:1], scalar2=None, op0=OP.mult)
                v.drain()
                p.inc("xsV2", v.engine_nop())
            block.vector(v_xs2)

            def g_wk(gg):
                p.wait(gg, "vmmT", NT)
                p.inc("ldWk", gg.dma_start(out=wfull[:], in_=d_Wk), 16)
            block.gpsimd(g_wk)

            def v_kbz(v):
                v.memset(KB[64:65, 0:T], 1.0)
                v.memset(KB[64:65, T:2 * T], 1.0)
                v.memset(auT[64:80, :], 0.0)
                p.inc("kbzV", v.engine_nop())
            block.vector(v_kbz)

            esQ.close()
            esB = ExitStack()
            with esB:
                psA = esB.enter_context(nc.psum_tensor("psA", [128, 1024], F32))
                psSS = esB.enter_context(nc.psum_tensor("psSS", [128, 2048], F32))
                psS0 = psSS
                psV = esB.enter_context(nc.psum_tensor("psV", [65, 1024], F32))

                e["mark"]("ATT")
                def kT_pieces(ha):
                    ps = []
                    for ts in range(4):
                        def f(t, ha=ha, ts=ts):
                            if ts == 0:
                                p.wait(t, "ldWk", 16)
                                if ha == 0:
                                    p.wait(t, "qcV", NT)
                                    p.wait(t, "vevV", NT)
                                    p.wait(t, "xsV2", 1)
                                else:
                                    p.wait(t, "mstV", ha)
                                    p.wait(t, "kbV", ha)
                                    p.wait(t, "kevS", ha * 4)
                            if ts >= 2:
                                p.wait(t, "kevS", ha * 4 + ts - 1)
                            last = None
                            for dk in range(ND):
                                last = t.matmul(
                                    psA[0:64, (ts % 2) * 512:((ts % 2) + 1) * 512],
                                    wfull[:, dk * D + ha * 64:dk * D + ha * 64 + 64],
                                    xn2T[:, dk * T + ts * 512:dk * T + (ts + 1) * 512],
                                    start=(dk == 0), stop=(dk == ND - 1))
                            p.inc("kmm2", last)
                        ps.append(f)
                    return ps

                def a_pieces(ha):
                    ps = []
                    for qt in range(NT):
                        for sub in range(2):
                            def f(t, ha=ha, qt=qt, sub=sub):
                                if qt + sub == 0:
                                    p.wait(t, "qmd%d" % (ha % 2),
                                           (ha // 2 + 1) * 16)
                                    p.wait(t, "kevS", (ha + 1) * 4)
                                p.wait(t, "rmV", ha * 32 + qt * 2 + sub)
                                for jj in range(2):
                                    last = t.matmul(
                                        psA[:, jj * 512:(jj + 1) * 512],
                                        QM[0:64, (ha % 2) * T + qt * 128:
                                           (ha % 2) * T + (qt + 1) * 128],
                                        ktp[0:64,
                                            sub * 1024 + jj * 512:
                                            sub * 1024 + (jj + 1) * 512],
                                        start=True, stop=True)
                                p.inc("aT", last)
                            ps.append(f)

                    def fn(t, ha=ha):
                        p.wait(t, "rmhV", ha + 1)
                        p.inc("nmT", t.transpose(psA[0:16, 0:128],
                                                 negm[:, 0:16], ident_f[:]))
                    ps.append(fn)
                    return ps

                for h in range(-1, H):
                    ha = h + 1
                    e["mark"]("slot%d" % h)

                    def t_slot(t, h=h, ha=ha):
                        pieces = []
                        if ha < H:
                            pieces = kT_pieces(ha)[2:] + a_pieces(ha)
                        if ha + 1 < H:
                            pieces = pieces + kT_pieces(ha + 1)[:2]
                        if h < 0:
                            for f in kT_pieces(0)[:2]:
                                f(t)
                            for f in pieces:
                                f(t)
                            return
                        p.wait(t, "qmr%d" % (h % 2), (h // 2 + 1) * 16)
                        p.wait(t, "kbV", h + 1)
                        pend = []

                        def emit_av(t, h, half, j):
                            g = h * 32 + half * 16 + j
                            p.wait(t, "expS", g + 1)
                            if j == 0:
                                p.wait(t, "uevS", h * 2 + half)
                            for qs in range(2):
                                last = t.matmul(
                                    psV[0:65, qs * 512:(qs + 1) * 512],
                                    VAUG[:, (j * H + h) * 65:(j * H + h) * 65 + 65],
                                    PT[:, (g % 2) * D + qs * 512:
                                       (g % 2) * D + (qs + 1) * 512],
                                    start=(j == 0), stop=(j == NT - 1))
                            p.inc("avT", last)

                        for half in range(2):
                            for j in range(NT):
                                g = h * 32 + half * 16 + j
                                if g >= 2:
                                    p.wait(t, "expS", g - 1)
                                last = None
                                for qs in range(2):
                                    last = t.matmul(
                                        psSS[:, (g % 2) * 1024 + qs * 512:
                                             (g % 2) * 1024 + (qs + 1) * 512],
                                        KB[:, (h % 2) * T + j * 128:
                                           (h % 2) * T + (j + 1) * 128],
                                        QM[:, (h % 2) * T + half * 1024 + qs * 512:
                                           (h % 2) * T + half * 1024 + (qs + 1) * 512],
                                        start=True, stop=True)
                                p.inc("stmT", last)
                                for f in pend:
                                    f(t)
                                pend = [lambda t, h=h, half=half, j=j:
                                        emit_av(t, h, half, j)]
                                if pieces:
                                    pieces.pop(0)(t)
                                if pieces and len(pieces) > 30 - (half * 16 + j):
                                    pieces.pop(0)(t)
                        for f in pend:
                            f(t)
                        for f in pieces:
                            f(t)
                    block.tensor(t_slot)

                    def s_slot(s, h=h, ha=ha):
                        def kev(ts):
                            p.wait(s, "kmm2", ha * 4 + ts + 1)
                            p.inc("kevS", s.activation(
                                out=ktp[0:64, ts * 512:(ts + 1) * 512],
                                in_=psA[0:64, (ts % 2) * 512:((ts % 2) + 1) * 512],
                                func=ACT.Identity,
                                bias=bkh[:, ha:ha + 1], scale=0.125))
                        if h < 0:
                            if ha < H:
                                for ts in range(4):
                                    kev(ts)
                            return
                        if ha < H:
                            kev(0)
                            kev(1)
                        if h == 0:
                            p.wait(s, "stqn0", p.val("stqn0"))
                            p.wait(s, "stqn1", p.val("stqn1"))
                        for half in range(2):
                            for j in range(NT):
                                g = h * 32 + half * 16 + j
                                p.wait(s, "stmT", g + 1)
                                if g >= 2:
                                    p.wait(s, "avT", g - 1)
                                p.inc("expS", s.activation(
                                    out=PT[:, (g % 2) * D:(g % 2) * D + D],
                                    in_=psSS[:, (g % 2) * 1024:(g % 2) * 1024 + 1024],
                                    func=ACT.Exp))
                                if half == 0 and j == 1 and ha < H:
                                    kev(2)
                                    kev(3)
                            p.wait(s, "avT", h * 32 + half * 16 + 16)
                            if h >= 2:
                                p.wait(s, "tpD", (h - 1) * 256)
                            if h == 0 and half == 0:
                                p.wait(s, "kbzV", 1)
                            p.inc("uevS", s.activation(
                                out=auT[0:65, (h % 2) * T + half * 1024:
                                        (h % 2) * T + half * 1024 + 1024],
                                                                in_=psV[0:65, 0:1024], func=ACT.Identity))
                    block.scalar(s_slot)

                    def v_slot(v, h=h, ha=ha):
                        if ha < H:
                            p.wait(v, "kevS", (ha + 1) * 4)
                            p.wait(v, "kbzV", 1)
                            if ha >= 2:
                                p.wait(v, "stmT", (ha - 1) * 32)
                            v.tensor_copy(out=KB[0:64, (ha % 2) * T:((ha % 2) + 1) * T],
                                          in_=ktp[0:64, 0:T])
                            p.inc("kbV", v.engine_nop())
                            for qt in range(NT):
                                for sub in range(2):
                                    p.wait(v, "aT", ha * 32 + qt * 2 + sub + 1)
                                    p.inc("rmV", v.tensor_reduce(
                                        out=nrmA[:, qt * 2 + sub:qt * 2 + sub + 1],
                                        in_=psA[:, 0:1024], axis=AX, op=OP.max,
                                        negate=True))
                            v.drain()
                            v.tensor_tensor(out=negm[:, 0:16], in0=nrmA[:, 0:32:2],
                                            in1=nrmA[:, 1:32:2], op=OP.min)
                            p.inc("rmhV", v.engine_nop())
                            p.wait(v, "nmT", ha + 1)
                            if ha >= 2:
                                p.wait(v, "stmA%d" % (ha % 2), (ha // 2) * 16)
                            v.tensor_copy(out=mst[:, (ha % 2) * 128:((ha % 2) + 1) * 128],
                                          in_=psA[0:16, 0:128])
                            p.inc("mstV", v.engine_nop())
                        hb = h - 1
                        if 0 <= hb:
                            p.wait(v, "tpD", (hb + 1) * 256)
                            b0 = (hb % 2) * 1280
                            v.drain()
                            v.tensor_scalar(out=rec_t[:, 0:16],
                                            in0=blk[:, b0 + 64:b0 + 16 * 80:80],
                                            scalar1=1e-8, scalar2=None, op0=OP.add)
                            v.drain()
                            v.reciprocal(out=rec_t[:, 0:16], in_=rec_t[:, 0:16])
                            v.drain()
                            for tt in range(NT):
                                v.tensor_scalar(
                                    out=attn_acc[:, tt * D + hb * 64:
                                                 tt * D + hb * 64 + 64],
                                    in0=blk[:, b0 + tt * 80:b0 + tt * 80 + 64],
                                    scalar1=rec_t[:, tt:tt + 1], scalar2=None,
                                    op0=OP.mult)
                            p.inc("blkV", v.engine_nop())
                    block.vector(v_slot)

                    def sy_slot(sy, h=h, ha=ha):
                        if ha < H:
                            p.wait(sy, "stqn0", p.val("stqn0"))
                            p.wait(sy, "stqn1", p.val("stqn1"))
                            if ha >= 2:
                                p.wait(sy, "stmT", (ha - 1) * 32)
                            p.inc("qmd%d" % (ha % 2), sy.dma_start(
                                out=QM[0:64, (ha % 2) * T:((ha % 2) + 1) * T],
                                in_=qln_d[64 * (ha % 2):64 * (ha % 2) + 64,
                                          (ha // 2) * T:(ha // 2 + 1) * T]), 16)
                        hb = h - 1
                        if hb >= 0:
                            p.wait(sy, "uevS", hb * 2 + 2)
                            if hb >= 1:
                                p.wait(sy, "tpD", hb * 256)
                            if hb >= 2:
                                p.wait(sy, "blkV", hb)
                            for tt in range(NT):
                                p.inc("tpD", sy.dma_start_transpose(
                                    out=blk[:, (hb % 2) * 1280 + tt * 80:
                                            (hb % 2) * 1280 + (tt + 1) * 80],
                                    in_=auT[0:80, (hb % 2) * T + tt * 128:
                                            (hb % 2) * T + (tt + 1) * 128]), 16)
                        if ha < H:
                            p.wait(sy, "mstV", ha + 1)
                            p.inc("stmA%d" % (ha % 2), sy.dma_start(
                                out=m_d[ha % 2:ha % 2 + 1, :],
                                in_=mst[:, (ha % 2) * 128:((ha % 2) + 1) * 128]), 16)
                            p.wait(sy, "stmA%d" % (ha % 2), (ha // 2 + 1) * 16)
                            p.inc("qmr%d" % (ha % 2), sy.dma_start(
                                out=QM[64:65, (ha % 2) * T:((ha % 2) + 1) * T],
                                in_=m_d[ha % 2:ha % 2 + 1, :]), 16)
                    block.sync(sy_slot)

                def sy_tp15(sy):
                    p.wait(sy, "uevS", 15 * 2 + 2)
                    p.wait(sy, "tpD", 15 * 256)
                    p.wait(sy, "blkV", 15)
                    for tt in range(NT):
                        p.inc("tpD", sy.dma_start_transpose(
                            out=blk[:, (15 % 2) * 1280 + tt * 80:
                                    (15 % 2) * 1280 + (tt + 1) * 80],
                            in_=auT[0:80, (15 % 2) * T + tt * 128:
                                    (15 % 2) * T + (tt + 1) * 128]), 16)
                block.sync(sy_tp15)

                def v_qz(v):
                    p.wait(v, "xsV2", 1)
                    p.wait(v, "kmm2", 64)
                    for dk in range(ND):
                        for hf2 in range(2):
                            sl = slice(dk * T + hf2 * 1024, dk * T + (hf2 + 1) * 1024)
                            v.tensor_scalar(out=lnsc[:], in0=xn2T[:, sl],
                                            scalar1=xsb[:, 1:2], scalar2=RBIG,
                                            op0=OP.mult, op1=OP.add)
                            v.drain()
                            v.tensor_scalar(out=xn2T[:, sl], in0=lnsc[:],
                                            scalar1=RBIG, scalar2=None,
                                            op0=OP.subtract)
                            v.drain()
                    p.inc("qzV", v.engine_nop())
                block.vector(v_qz)

                def v_blk15(v):
                    p.wait(v, "tpD", H * 256)
                    b0 = (15 % 2) * 1280
                    v.drain()
                    v.tensor_scalar(out=rec_t[:, 0:16],
                                    in0=blk[:, b0 + 64:b0 + 16 * 80:80],
                                    scalar1=1e-8, scalar2=None, op0=OP.add)
                    v.drain()
                    v.reciprocal(out=rec_t[:, 0:16], in_=rec_t[:, 0:16])
                    v.drain()
                    for tt in range(NT):
                        v.tensor_scalar(
                            out=attn_acc[:, tt * D + 15 * 64:tt * D + 15 * 64 + 64],
                            in0=blk[:, b0 + tt * 80:b0 + tt * 80 + 64],
                            scalar1=rec_t[:, tt:tt + 1], scalar2=None,
                            op0=OP.mult)
                    p.inc("blkV", v.engine_nop())
                block.vector(v_blk15)

            e["mark"]("FFNWO")
            def g_wo(gg):
                p.wait(gg, "kmm2", 64)
                p.inc("ldWo", gg.dma_start(out=wfull[:], in_=d_Wo), 16)
            block.gpsimd(g_wo)

        esC = ExitStack()
        with esC:
            sbC = lambda *a: esC.enter_context(nc.sbuf_tensor(*a))
            w8b = sbC("w8b", [128, ND * D], BF16)
            ffnP = sbC("ffnP", [128, 2 * ND * 512], BF16)
            qbuf = sbC("qbuf", [128, ND * T], BF16)
            accT = sbC("accT", [128, 2 * D], BF16)
            wrow = sbC("wrow", [1, D], F32)
            psF = esC.enter_context(nc.psum_tensor("psF", [128, 1024], F32))
            psG = esC.enter_context(nc.psum_tensor("psG", [128, 2048], F32))
            psTc = esC.enter_context(nc.psum_tensor("psTc", [128, 1024], BF16))

            def g_w8(gg):
                p.wait(gg, "avT", 16 * 32)
                p.inc("ldW8", gg.dma_start(out=w8b[:], in_=d_W8T), 16)
            block.gpsimd(g_w8)


            def v_lnb(v):
                v.tensor_copy(out=lnbc_b[:], in_=col(C_LNB))
                p.inc("lnbV", v.engine_nop())
            block.vector(v_lnb)

            def t_wob(t):
                p.wait(t, "ldWo", 16)
                p.wait(t, "lnbV", 1)
                p.wait(t, "blkV", H)
                p.wait(t, "uevS", 32)
                last = None
                for oh in range(2):
                    for ch in range(ND):
                        last = t.matmul(
                            psF[0:1, oh * 512:(oh + 1) * 512],
                            lnbc_b[:, ch:ch + 1],
                            wfull[:, ch * D + oh * 512:ch * D + (oh + 1) * 512],
                            start=(ch == 0), stop=(ch == ND - 1))
                p.inc("wobT", last)
            block.tensor(t_wob)

            def v_wob(v):
                p.wait(v, "wobT", 1)
                v.tensor_tensor(out=wrow[:], in0=psF[0:1, 0:1024], in1=bo_f[:],
                                op=OP.add)
                v.drain()
                v.tensor_copy(out=rows_b[0:1, R_WOB * D:(R_WOB + 1) * D],
                              in_=wrow[:])
                p.inc("wobV", v.engine_nop())
            block.vector(v_wob)

            for ts in range(4):
                def t_f(t, ts=ts):
                    p.wait(t, "ldW8", 16)
                    p.wait(t, "qzV", 1)
                    p.wait(t, "wobV", 1)
                    if ts >= 2:
                        p.wait(t, "ffV", (ts - 1) * ND)
                    for nch in range(ND):
                        gi = ts * ND + nch
                        if gi >= 2:
                            p.wait(t, "ffV", gi - 1)
                        last = None
                        for dk in range(ND):
                            last = t.matmul(
                                psF[:, (nch % 2) * 512:((nch % 2) + 1) * 512],
                                w8b[:, dk * D + nch * 128:dk * D + (nch + 1) * 128],
                                xn2T[:, dk * T + ts * 512:dk * T + (ts + 1) * 512],
                                start=(dk == 0), stop=(dk == ND - 1))
                        p.inc("ffT", last)
                block.tensor(t_f)

                def v_f(v, ts=ts):
                    for nch in range(ND):
                        gi = ts * ND + nch
                        p.wait(v, "ffT", gi + 1)
                        v.tensor_scalar(
                            out=ffnP[:, (ts % 2) * ND * 512 + nch * 512:
                                     (ts % 2) * ND * 512 + (nch + 1) * 512],
                            in0=psF[:, (nch % 2) * 512:((nch % 2) + 1) * 512],
                            scalar1=col(C_SV)[:, nch:nch + 1],
                            scalar2=col(C_B1)[:, nch:nch + 1],
                            op0=OP.mult, op1=OP.add)
                        p.inc("ffV", v.engine_nop())
                block.vector(v_f)

                for sg in range(2):
                    tts = [ts * 4 + sg * 2, ts * 4 + sg * 2 + 1]

                    def t_acc(t, ts=ts, sg=sg, tts=tts):
                        p.wait(t, "blkV", H)
                        sgi = ts * 2 + sg
                        if sgi > 0:
                            p.wait(t, "finV", sgi * 2)
                        for k, tt in enumerate(tts):
                            if sgi * 2 + k > 0:
                                p.wait(t, "accV", sgi * 2 + k)
                            last = None
                            for ch in range(ND):
                                last = t.transpose(
                                    psTc[:, ch * 128:(ch + 1) * 128],
                                    attn_acc[:, tt * D + ch * 128:
                                             tt * D + (ch + 1) * 128],
                                    ident_b[:])
                            p.inc("accT2", last)
                    block.tensor(t_acc)

                    def v_acc(v, ts=ts, sg=sg, tts=tts):
                        for k, tt in enumerate(tts):
                            p.wait(v, "accT2", (ts * 2 + sg) * 2 + k + 1)
                            v.tensor_copy(out=accT[:, k * D:(k + 1) * D],
                                          in_=psTc[:, 0:1024])
                            p.inc("accV", v.engine_nop())
                    block.vector(v_acc)

                    if ts == 0 and sg == 0:
                        def sy_qb(sy):
                            p.wait(sy, "stqn0", p.val("stqn0"))
                            p.wait(sy, "stqn1", p.val("stqn1"))
                            p.wait(sy, "blkV", H)
                            p.inc("ldqb0", sy.dma_start(out=qbuf[:], in_=qln_d), 16)
                        block.sync(sy_qb)

                    def sy_x1(sy, ts=ts, sg=sg, tts=tts):
                        for k, tt in enumerate(tts):
                            if tt < 4:
                                p.wait(sy, "qcC", NT)
                            else:
                                p.wait(sy, "finV", tt - 3)
                            p.inc("ldf%d" % (tt % 4), sy.dma_start(
                                out=xrot[:, (tt % 4) * D:((tt % 4) + 1) * D],
                                in_=x1_d[tt * 128:(tt + 1) * 128, :]), 16)
                    block.sync(sy_x1)

                    def t_wo(t, ts=ts, sg=sg, tts=tts):
                        p.wait(t, "accV", (ts * 2 + sg) * 2 + 2)
                        p.wait(t, "wobV", 1)
                        if ts + sg > 0:
                            p.wait(t, "finV", (ts * 2 + sg) * 2)
                        for ch in range(ND):
                            gi = (ts * 2 + sg) * ND + ch
                            if ch == 0:
                                p.wait(t, "ldqb0", 16)
                            for k, tt in enumerate(tts):
                                for oh in range(2):
                                    t.matmul(
                                        psG[:, (k * 2 + oh) * 512:(k * 2 + oh + 1) * 512],
                                        accT[:, k * D + ch * 128:k * D + (ch + 1) * 128],
                                        wfull[:, ch * D + oh * 512:ch * D + (oh + 1) * 512],
                                        start=(ch == 0), stop=False)
                                    last = t.matmul(
                                        psG[:, (k * 2 + oh) * 512:(k * 2 + oh + 1) * 512],
                                        qbuf[:, ch * T + tt * 128:
                                             ch * T + (tt + 1) * 128],
                                        wfull[:, ch * D + oh * 512:ch * D + (oh + 1) * 512],
                                        start=False,
                                        stop=False)
                            p.inc("woT", last)
                        for k, tt in enumerate(tts):
                            for oh in range(2):
                                last = t.matmul(
                                    psG[:, (k * 2 + oh) * 512:(k * 2 + oh + 1) * 512],
                                    ones_b[0:1, 0:128],
                                    rows_b[0:1, R_BO * D + oh * 512:
                                           R_BO * D + (oh + 1) * 512],
                                    start=False, stop=True)
                        p.inc("woT2", last)
                    block.tensor(t_wo)

                    def t_ft(t, ts=ts, sg=sg, tts=tts):
                        p.wait(t, "woT2", (ts * 2 + sg) + 1)
                        p.wait(t, "ffV", (ts + 1) * ND)
                        for k, tt in enumerate(tts):
                            p.wait(t, "finV", (ts * 2 + sg) * 2 + k)
                            last = None
                            for nch in range(ND):
                                last = t.transpose(
                                    psTc[:, nch * 128:(nch + 1) * 128],
                                    ffnP[:, (ts % 2) * ND * 512 + nch * 512 +
                                         (tt % 4) * 128:
                                         (ts % 2) * ND * 512 + nch * 512 +
                                         (tt % 4) * 128 + 128],
                                    ident_b[:])
                            p.inc("ftT", last)
                    block.tensor(t_ft)

                    def v_fin(v, ts=ts, sg=sg, tts=tts):
                        for k, tt in enumerate(tts):
                            p.wait(v, "ldf%d" % (tt % 4), (tt // 4 + 1) * 16)
                            p.wait(v, "woT2", (ts * 2 + sg) + 1)
                            p.wait(v, "ftT", (ts * 2 + sg) * 2 + k + 1)
                            if tt >= 2:
                                p.wait(v, "stf%d" % (tt % 2), (tt // 2) * 16)
                            for oh in range(2):
                                v.tensor_tensor(
                                    out=xorot[:, (tt % 2) * D + oh * 512:
                                              (tt % 2) * D + (oh + 1) * 512],
                                    in0=psG[:, (k * 2 + oh) * 512:
                                            (k * 2 + oh + 1) * 512],
                                    in1=xrot[:, (tt % 4) * D + oh * 512:
                                             (tt % 4) * D + (oh + 1) * 512],
                                    op=OP.add)
                            v.drain()
                            if _STAGE >= 5:
                                v.tensor_copy(out=lnsc[:], in_=psTc[:, 0:1024])
                                v.drain()
                                v.tensor_tensor(
                                    out=xorot[:, (tt % 2) * D:((tt % 2) + 1) * D],
                                    in0=xorot[:, (tt % 2) * D:((tt % 2) + 1) * D],
                                    in1=lnsc[:],
                                    op=OP.add)
                                v.drain()
                            p.inc("finV", v.engine_nop())
                    block.vector(v_fin)

                    def sy_fo(sy, ts=ts, sg=sg, tts=tts):
                        for k, tt in enumerate(tts):
                            p.wait(sy, "finV", (ts * 2 + sg) * 2 + k + 1)
                            p.inc("stf%d" % (tt % 2), sy.dma_start(
                                out=d_out[tt * 128:(tt + 1) * 128, :],
                                in_=xorot[:, (tt % 2) * D:((tt % 2) + 1) * D]), 16)
                    block.sync(sy_fo)

            def sy_fin(sy):
                p.wait(sy, "stf0", p.val("stf0"))
                p.wait(sy, "stf1", p.val("stf1"))
            block.sync(sy_fin)



def kernel(**inputs):
    npad, per_core = _host_prep(inputs)
    nc = build(npad)
    res = run_bass_kernel_spmd(nc, per_core, core_ids=list(range(NCORES)))
    out = np.stack([res.results[i]["out"] for i in range(NCORES)], axis=0)
    return out.astype(np.float32)


# version 18 (final)
# speedup vs baseline: 1.2007x; 1.0289x over previous
import os
from contextlib import ExitStack
import numpy as np
import ml_dtypes
import concourse.bass as bass
import concourse.mybir as mybir
from concourse.bass import IndirectOffsetOnAxis
from concourse.bass_utils import run_bass_kernel_spmd

F32 = mybir.dt.float32
BF16 = mybir.dt.bfloat16
I32 = mybir.dt.int32
I8 = mybir.dt.int8
AX = mybir.AxisListType.X
OP = mybir.AluOpType
ACT = mybir.ActivationFunctionType

NCORES = 8
T = 2048
D = 1024
C = 2048
FF = 4096
H = 16
DH = 64
NT = T // 128
ND = D // 128
NCC = C // 128
NF = FF // 128
RBIG = 12582912.0
EPS = 1e-5

_STAGE = int(os.environ.get("MK_STAGE", "99"))



def _route_moe(expert_indices, expert_weights, x_flat):
    flat_idx = np.asarray(expert_indices).reshape(-1)
    flat_w = np.asarray(expert_weights).reshape(-1).astype(np.float32)
    order = np.argsort(flat_idx, kind="stable")
    tok = order // 2
    counts = np.bincount(flat_idx, minlength=NCORES)
    starts = np.concatenate([[0], np.cumsum(counts)])
    npad = max(512, int(np.ceil((counts.max() + 1) / 128.0) * 128))

    xs_l, fw_l = [], []
    for e in range(NCORES):
        n = int(counts[e])
        xs = np.zeros((npad, D), np.float32)
        fw = np.zeros((npad, 1), np.float32)
        xs[1:1 + n] = x_flat[tok[starts[e]:starts[e + 1]]]
        fw[1:1 + n, 0] = flat_w[order[starts[e]:starts[e + 1]]]
        xs_l.append(np.ascontiguousarray(xs))
        fw_l.append(np.ascontiguousarray(fw))

    expert_of_row = flat_idx[order]
    i_loc = 1 + (np.arange(len(order)) - starts[expert_of_row])
    chk = i_loc // 512
    c_k = np.minimum(512, npad - chk * 512)
    agrow = (NCORES * 512 * chk + expert_of_row * c_k + (i_loc - chk * 512))
    gidx_full = np.zeros((NCORES, 2, T), np.int64)
    occ = np.zeros(NCORES * T, np.int64)
    for r in range(len(order)):
        g = int(tok[r])
        gidx_full[g // T, occ[g], g % T] = agrow[r]
        occ[g] += 1
    gidx_l = []
    for c in range(NCORES):
        g2 = gidx_full[c].reshape(2, NT, 128).transpose(2, 0, 1).reshape(128, 32)
        gidx_l.append(np.ascontiguousarray(g2.astype(np.int32)))
    return npad, xs_l, fw_l, gidx_l


def _col8(v):
    return np.ascontiguousarray(np.asarray(v, np.float32).reshape(ND, 128).T)


def _lay(w):
    m = w.shape[1]
    return np.ascontiguousarray(
        np.asarray(w).reshape(ND, 128, m).transpose(1, 0, 2).reshape(128, ND * m))


def _host_prep(inputs):
    x = np.asarray(inputs["L0_attn_hidden_states"], np.float32)
    npad, xs_l, fw_l, gidx_l = _route_moe(
        inputs["L0_ffn_expert_indices"], inputs["L0_ffn_expert_weights"],
        x.reshape(-1, D))

    w_inT = np.asarray(inputs["L0_attn_in_proj_w"], np.float32)[:C].T
    w_outT = np.asarray(inputs["L0_attn_out_proj_w"], np.float32).T
    w_outT_l = np.ascontiguousarray(
        w_outT.reshape(NCC, 128, D).transpose(1, 0, 2).reshape(128, NCC * D))

    cw = np.asarray(inputs["L0_attn_conv_w"], np.float32)[:, 0, :]
    conv_diag = np.zeros((128, NCC * 4 * 128), np.float32)
    for cc in range(NCC):
        for k in range(4):
            conv_diag[:, (cc * 4 + k) * 128:(cc * 4 + k + 1) * 128] = \
                np.diag(cw[cc * 128:(cc + 1) * 128, k])

    w1 = np.asarray(inputs["L0_ffn_w1"], np.float32)
    w2 = np.asarray(inputs["L0_ffn_w2"], np.float32)
    w1_l = [np.ascontiguousarray(
        w1[e].reshape(ND, 128, NF, 128).transpose(2, 1, 0, 3).reshape(NF, 128, ND * 128))
        for e in range(NCORES)]
    w2_l = [np.ascontiguousarray(w2[e].reshape(NF, 128, D)) for e in range(NCORES)]

    shared = dict(
        w_inT=_lay(w_inT), w_outT=w_outT_l, conv_diag=conv_diag,
        conv_b=np.ascontiguousarray(np.asarray(inputs["L0_attn_conv_b"], np.float32).reshape(NCC, 128).T),
        bias0_row=np.ascontiguousarray(np.asarray(inputs["L0_ffn_bias"], np.float32)[None, :]),
        Wq=_lay(np.asarray(inputs["L1_attn_Wq"], np.float32)),
        Wk=_lay(np.asarray(inputs["L1_attn_Wk"], np.float32)),
        Wv=_lay(np.asarray(inputs["L1_attn_Wv"], np.float32)),
        Wo=_lay(np.asarray(inputs["L1_attn_Wo"], np.float32)),
        bq_row=np.ascontiguousarray(np.asarray(inputs["L1_attn_bq"], np.float32)[None, :]),
        bv_row=np.ascontiguousarray(np.asarray(inputs["L1_attn_bv"], np.float32)[None, :]),
        bo_row=np.ascontiguousarray(np.asarray(inputs["L1_attn_bo"], np.float32)[None, :]),
        lnb_col=_col8(inputs["L1_attn_ln_b"]),
        bk_h64=np.ascontiguousarray(
            np.asarray(inputs["L1_attn_bk"], np.float32).reshape(16, 64).T),
        g_col=_col8(inputs["L1_attn_ln_g"]),
        b_col=_col8(inputs["L1_attn_ln_b"]),
        W8T=np.ascontiguousarray(
            np.asarray(inputs["L1_ffn_weight_int8"]).T.reshape(ND, 128, D)
            .transpose(1, 0, 2).reshape(128, ND * D)),
        wscale_col=_col8(np.asarray(inputs["L1_ffn_weight_scale"], np.float32)[:, 0]),
        bias1_col=_col8(inputs["L1_ffn_bias"]),
        ident=np.eye(128, dtype=np.float32),
        ones_row=np.ones((1, T), np.float32),
    )
    per_core = []
    for i in range(NCORES):
        m = dict(shared)
        m["x"] = np.ascontiguousarray(x[i])
        m["xs"] = xs_l[i]
        m["fw"] = np.ascontiguousarray(fw_l[i].reshape(-1, 128).T)
        m["gidx"] = gidx_l[i]
        m["w1"] = w1_l[i]
        m["w2"] = w2_l[i]
        per_core.append(m)
    return npad, per_core



_ALIAS = {}
for _t, _names in {
    "mtrT": ["l0trT", "l2trT", "gmT", "xsT", "qmmT", "vmmT", "qcT"],
    "mlnA": ["l0A", "l2A", "qlA"],
    "init_v": ["haloz", "vzV", "kbzV", "lnbV", "gmV", "xsV", "xsV2"],
    "mlnBv": ["l0Bv", "l2Bv", "qlBv"],
    "mapC": ["l0apC", "l2apC", "qcC"],
    "mtrV": ["l0trV", "l2trV", "qcV"],
    "mlnB": ["l0B", "l2B", "qlB"],
    "qevS": ["kevS"],
    "dsl0": ["ldxsA0", "ldxsC0", "ldxA0", "ldxC0"],
    "dsl1": ["ldxsA1", "ldxsC1", "ldxA1", "ldxC1"],
    "dsl2": ["ldxA2", "ldxC2"],
    "dsl3": ["ldxA3", "ldxC3"],
    "dslB0": ["ldr0", "ldq0", "ldf0"],
    "dslB1": ["ldr1", "ldq1", "ldf1"],
    "dslB2": ["ldr2", "ldq2", "ldf2"],
    "dslB3": ["ldr3", "ldq3", "ldf3"],
    "stx10": ["stq0", "stf0", "stdb0"],
    "stx11": ["stq1", "stf1", "stdb1"],
    "qmA": ["qmd0", "ldqb0"],
    "qmB": ["qmd1", "ldqb1"],
    "ldWq": ["ldWv", "ldWk", "ldWo"],
    "argi": ["argo"],
    "cc": ["cc2"],
}.items():
    for _n in _names:
        _ALIAS[_n] = _t


class P:
    def __init__(self, nc):
        self.nc = nc
        self._phys = {}
        self._base = {}
        self._guards = []

    def _ph(self, name):
        ph = _ALIAS.get(name, name)
        if ph not in self._phys:
            g = self.nc.semaphore(ph)
            self._phys[ph] = [g.__enter__(), 0]
            self._guards.append(g)
        s = self._phys[ph]
        if name not in self._base:
            self._base[name] = s[1]
        return s, self._base[name]

    def _s(self, name):
        s, b = self._ph(name)
        return [s[0], s[1] - b]

    def inc(self, name, instr, n=1):
        s, b = self._ph(name)
        instr.then_inc(s[0], n)
        s[1] += n
        return s[1] - b

    def val(self, name):
        s, b = self._ph(name)
        return s[1] - b

    def wait(self, eng, name, v=None):
        s, b = self._ph(name)
        eng.wait_ge(s[0], b + ((s[1] - b) if v is None else v))

    def close(self):
        for g in reversed(self._guards):
            g.__exit__(None, None, None)



def build(npad):
    nc = bass.Bass("TRN2", target_bir_lowering=False)
    NPT = npad // 128
    NTG_M = (npad + 511) // 512
    tgw_M = [min(512, npad - 512 * t) for t in range(NTG_M)]

    def din(name, shape, dt=F32):
        return nc.dram_tensor(name, shape, dt, kind="ExternalInput").ap()

    d_x = din("x", [T, D])
    d_xs = din("xs", [npad, D])
    d_fw = din("fw", [128, NPT])
    d_gidx = din("gidx", [128, 32], I32)
    d_winT = din("w_inT", [128, ND * C])
    d_woutT = din("w_outT", [128, NCC * D])
    d_cdiag = din("conv_diag", [128, NCC * 4 * 128])
    d_cb = din("conv_b", [128, NCC])
    d_b0 = din("bias0_row", [1, D])
    d_w1 = din("w1", [NF, 128, ND * 128])
    d_w2 = din("w2", [NF, 128, D])
    d_Wq = din("Wq", [128, ND * D])
    d_Wk = din("Wk", [128, ND * D])
    d_Wv = din("Wv", [128, ND * D])
    d_Wo = din("Wo", [128, ND * D])
    d_bq = din("bq_row", [1, D])
    d_bv = din("bv_row", [1, D])
    d_bo = din("bo_row", [1, D])
    d_lnbc = din("lnb_col", [128, ND])
    d_bk = din("bk_h64", [64, 16])
    d_g = din("g_col", [128, ND])
    d_b = din("b_col", [128, ND])
    d_W8T = din("W8T", [128, ND * D], I8)
    d_wsc = din("wscale_col", [128, ND])
    d_b1 = din("bias1_col", [128, ND])
    d_id = din("ident", [128, 128])
    d_ones = din("ones_row", [1, T])
    d_out = nc.dram_tensor("out", [T, D], F32, kind="ExternalOutput").ap()

    ag_in = nc.dram_tensor("ag_in", [npad, D], F32).ap()
    ag_out = nc.dram_tensor("ag_out", [NCORES * npad, D], F32).ap()
    ar_in = nc.dram_tensor("ar_in", [1, 128], F32).ap()
    ar_out = nc.dram_tensor("ar_out", [1, 128], F32).ap()
    x1_d = nc.dram_tensor("x1_d", [T, D], F32).ap()
    q_d = nc.dram_tensor("q_d", [T, D], F32).ap()
    m_d = nc.dram_tensor("m_d", [2, 16 * 128], BF16).ap()
    qln_d = nc.dram_tensor("qln_d", [128, ND * T], BF16).ap()

    es = ExitStack()
    with es:
        _PH = globals().get("_PHASES", None)
        def mark(name):
            if _PH is not None:
                _PH.append((name, nc.n_instructions()))
        sb = lambda *a: es.enter_context(nc.sbuf_tensor(*a))
        xn2T = sb("xn2T", [128, ND * T], BF16)
        xrot = sb("xrot", [128, 4 * D], F32)
        xorot = sb("xorot", [128, 2 * D], F32)
        ident_f = sb("ident_f", [128, 128], F32)
        ident_b = sb("ident_b", [128, 128], BF16)
        ones_b = sb("ones_b", [1, 512], BF16)
        ones_f = sb("ones_f", [1, 128], F32)
        rows_b = sb("rows_b", [1, 6 * D], BF16)
        bo_f = sb("bo_f", [1, D], F32)
        cols_f = sb("cols_f", [128, 8 * ND], F32)
        cb_c = sb("cb_c", [128, NCC], F32)
        bkh = sb("bkh", [64, 16], F32)
        gmax = sb("gmax", [128, 1], F32)
        armx = sb("armx", [1, 128], F32)
        xsp = sb("xsp", [1, 2], F32)
        xsb = sb("xsb", [128, 2], F32)
        gidx_sb = sb("gidx_sb", [128, 32], I32)
        stats_b = sb("stats_b", [128, 12], F32)
        mva = sb("mva", [128, 2 * NT], F32)
        rsa = sb("rsa", [128, NT], F32)
        ln1 = sb("ln1", [128, NT], F32)
        lnsc = sb("lnsc", [128, D], F32)
        lnbf = sb("lnbf", [128, 2 * D], BF16)
        block = es.enter_context(nc.Block())
        p = P(nc)
        R_B0, R_BQ, R_BV, R_BO, R_LNB, R_WOB = range(6)

        def row(i):
            return rows_b[0:1, i * D:(i + 1) * D]

        C_BK8, C_G, C_B, C_WSC, C_B1, C_SV, C_LNB = range(7)

        def col(i):
            return cols_f[:, i * ND:(i + 1) * ND]

        def g_const(g):
            for dst, src in [(ident_f[:], d_id), (ident_b[:], d_id),
                             (ones_b[:], d_ones[0:1, 0:512]), (ones_f[:], d_ones[0:1, 0:128]),
                             (row(R_B0), d_b0), (row(R_BQ), d_bq),
                             (row(R_BV), d_bv), (row(R_BO), d_bo),
                             (bo_f[:], d_bo),
                             (bkh[:], d_bk), (col(C_G), d_g),
                             (col(C_B), d_b), (col(C_WSC), d_wsc),
                             (col(C_B1), d_b1), (col(C_LNB), d_lnbc),
                             (cb_c[:], d_cb), (gidx_sb[:], d_gidx)]:
                p.inc("cst", g.dma_start(out=dst, in_=src), 16)
        block.gpsimd(g_const)
        CST = p.val("cst")

        def v_init(v):
            p.wait(v, "cst", CST)
            v.tensor_scalar(out=bkh[:], in0=bkh[:], scalar1=0.125,
                            scalar2=None, op0=OP.mult)
            p.inc("init_v", v.engine_nop())
        block.vector(v_init)

        def ln_passA(v, xt, slot):
            v.bn_stats(out=stats_b[:, 0:6], in_=xt[:, 0:512])
            v.bn_stats(out=stats_b[:, 6:12], in_=xt[:, 512:1024])
            v.drain()
            v.bn_aggr(out=mva[:, 2 * slot:2 * slot + 2], in_=stats_b[:, 0:12])
            v.drain()

        def ln_passB(ntiles, tag):
            def fv(v):
                p.wait(v, tag + "A", ntiles)
                v.tensor_scalar(out=ln1[:, 0:ntiles], in0=mva[:, 1:2 * ntiles:2],
                                scalar1=EPS, scalar2=None, op0=OP.add)
                v.drain()
                v.reciprocal(out=rsa[:, 0:ntiles], in_=ln1[:, 0:ntiles])
                p.inc(tag + "Bv", v.engine_nop())
            block.vector(fv)

            def fs(s):
                p.wait(s, tag + "Bv")
                s.activation(out=rsa[:, 0:ntiles], in_=rsa[:, 0:ntiles],
                             func=ACT.Sqrt)
                p.inc(tag + "B", s.activation(out=ln1[:, 0:1], in_=ln1[:, 0:1],
                                              func=ACT.Identity))
            block.scalar(fs)
            return p.val(tag + "B")

        def ln_apply(v, xt, out, slot):
            v.tensor_scalar(out=out, in0=xt, scalar1=mva[:, 2 * slot:2 * slot + 1],
                            scalar2=rsa[:, slot:slot + 1], op0=OP.subtract,
                            op1=OP.mult)

        es0 = ExitStack()
        with es0:
            sb0 = lambda *a: es0.enter_context(nc.sbuf_tensor(*a))
            winT = sb0("winT", [128, ND * C], BF16)
            def g_l0w(g):
                p.inc("ldwin", g.dma_start(out=winT[:], in_=d_winT), 16)
            block.gpsimd(g_l0w)

            esM = ExitStack()
            with esM:
                sbM = lambda *a: esM.enter_context(nc.sbuf_tensor(*a))
                xs_t = sbM("xs_t", [128, 2 * D], F32)
                xsnT = sbM("xsnT", [128, ND * 512], BF16)
                hT = sbM("hT", [128, NF * 512], BF16)
                w1b = sbM("w1b", [128, 2 * ND * 128], BF16)
                w2b = sbM("w2b", [128, 2 * D], BF16)
                w1f = sbM("w1f", [128, 2 * ND * 128], F32)
                w2f = sbM("w2f", [128, 4 * D], F32)
                fw_sb = sbM("fw_sb", [128, NPT], F32)
                ys_sb = sbM("ys_sb", [128, 2 * D], F32)
                psM = esM.enter_context(nc.psum_tensor("psM", [128, 2048], F32))
                psTb = esM.enter_context(nc.psum_tensor("psTb", [128, 2048], BF16))
                def g_mld(g):
                    p.inc("ldfw", g.dma_start(out=fw_sb[:], in_=d_fw), 16)
                block.gpsimd(g_mld)

                mark("M_lnA")
                for i in range(NPT):
                    def sy_ld(sy, i=i):
                        if i >= 2:
                            p.wait(sy, "mlnA", i - 1)
                        p.inc("ldxsA%d" % (i % 2), sy.dma_start(
                            out=xs_t[:, (i % 2) * D:((i % 2) + 1) * D],
                            in_=d_xs[i * 128:(i + 1) * 128, :]), 16)
                    block.sync(sy_ld)

                    def v_st(v, i=i):
                        p.wait(v, "ldxsA%d" % (i % 2), (i // 2 + 1) * 16)
                        ln_passA(v, xs_t[:, (i % 2) * D:((i % 2) + 1) * D], i)
                        p.inc("mlnA", v.engine_nop())
                    block.vector(v_st)
                MLNB = ln_passB(NPT, "mln")

                mark("M_tg")
                XLN_K = [0]
                for tg in range(NTG_M):
                    tw = tgw_M[tg]
                    ntl = tw // 128
                    for ii in range(ntl):
                        i = tg * 4 + ii

                        def sy_ld(sy, i=i):
                            if i < 2:
                                p.wait(sy, "mlnA", NPT)
                            else:
                                p.wait(sy, "mtrV", i - 1)
                            if i >= 2:
                                p.wait(sy, "mapC", i - 1)
                            p.inc("ldxsC%d" % (i % 2), sy.dma_start(
                                out=xs_t[:, (i % 2) * D:((i % 2) + 1) * D],
                                in_=d_xs[i * 128:(i + 1) * 128, :]), 16)
                        block.sync(sy_ld)

                        def v_ap(v, i=i):
                            p.wait(v, "ldxsC%d" % (i % 2), (i // 2 + 1) * 16)
                            p.wait(v, "mlnB", MLNB)
                            if i >= 2:
                                p.wait(v, "mtrT", i - 1)
                            ln_apply(v, xs_t[:, (i % 2) * D:((i % 2) + 1) * D],
                                     lnbf[:, (i % 2) * D:((i % 2) + 1) * D], i)
                            p.inc("mapC", v.engine_nop())
                        block.vector(v_ap)

                        def t_tr(t, i=i):
                            p.wait(t, "mapC", i + 1)
                            p.wait(t, "cst", CST)
                            if i >= 2:
                                p.wait(t, "mtrV", i - 1)
                            if i >= 4:
                                p.wait(t, "mhT", (i // 4) * NF)
                            last = None
                            for dk in range(ND):
                                last = t.transpose(
                                    psTb[:, (i % 2) * 1024 + dk * 128:
                                         (i % 2) * 1024 + (dk + 1) * 128],
                                    lnbf[:, (i % 2) * D + dk * 128:
                                         (i % 2) * D + (dk + 1) * 128], ident_b[:])
                            p.inc("mtrT", last)
                        block.tensor(t_tr)

                        def v_ev(v, i=i, ii=ii):
                            p.wait(v, "mtrT", i + 1)
                            for dk in range(ND):
                                v.tensor_copy(
                                    out=xsnT[:, dk * 512 + ii * 128:
                                             dk * 512 + (ii + 1) * 128],
                                    in_=psTb[:, (i % 2) * 1024 + dk * 128:
                                             (i % 2) * 1024 + (dk + 1) * 128])
                            p.inc("mtrV", v.engine_nop())
                        block.vector(v_ev)

                    def sy_w1(sy, tg=tg):
                        for fc in range(NF):
                            gi = tg * NF + fc
                            if gi >= 2:
                                p.wait(sy, "w1cV", gi - 1)
                            p.inc("ldw1%d" % (fc % 2), sy.dma_start(
                                out=w1f[:, (fc % 2) * 1024:((fc % 2) + 1) * 1024],
                                in_=d_w1[fc]), 16)
                    block.sync(sy_w1)

                    def w1cast(s, gi):
                        fc2 = gi % NF
                        p.wait(s, "ldw1%d" % (fc2 % 2), (gi // 2 + 1) * 16)
                        if gi >= 2:
                            p.wait(s, "mhT", gi - 1)
                        p.inc("w1cV", s.copy(
                            out=w1b[:, (fc2 % 2) * 1024:((fc2 % 2) + 1) * 1024],
                            in_=w1f[:, (fc2 % 2) * 1024:((fc2 % 2) + 1) * 1024]))

                    def t_h(t, tg=tg, tw=tw):
                        p.wait(t, "mtrV", tg * 4 + tw // 128)
                        p.wait(t, "myV", p.val("myV"))
                        for fc in range(NF):
                            gi = tg * NF + fc
                            p.wait(t, "w1cV", gi + 1)
                            if gi >= 2:
                                p.wait(t, "mhS", gi - 1)
                            last = None
                            for dk in range(ND):
                                last = t.matmul(
                                    psM[:, (fc % 2) * 512:(fc % 2) * 512 + tw],
                                    w1b[:, (fc % 2) * 1024 + dk * 128:
                                        (fc % 2) * 1024 + (dk + 1) * 128],
                                    xsnT[:, dk * 512:dk * 512 + tw],
                                    start=(dk == 0), stop=(dk == ND - 1))
                            p.inc("mhT", last)
                    block.tensor(t_h)

                    def s_h(s, tg=tg, tw=tw):
                        if tg == 0:
                            w1cast(s, 0)
                            w1cast(s, 1)
                        if tg >= 1:
                            p.wait(s, "myT", tg * 2 * NF)
                        for fc in range(NF):
                            gi = tg * NF + fc
                            p.wait(s, "mhT", gi + 1)
                            p.inc("mhS", s.activation(
                                out=hT[:, fc * 512:fc * 512 + tw],
                                in_=psM[:, (fc % 2) * 512:(fc % 2) * 512 + tw],
                                func=ACT.Gelu))
                            if fc + 2 < NF:
                                w1cast(s, gi + 2)
                    block.scalar(s_h)

                    nhalf = (ntl + 1) // 2
                    for hf in range(nhalf):
                        tis = [hf * 2 + k for k in range(2) if hf * 2 + k < ntl]

                        def sy_w2(sy, tg=tg, hf=hf):
                            for fb in range(NF // 2):
                                gi = (tg * 2 + hf) * NF // 2 + fb
                                if gi >= 2:
                                    p.wait(sy, "w2cV", (gi - 1) * 2)
                                p.inc("ldw2%d" % (fb % 2), sy.dma_start(
                                    out=w2f[:, (fb % 2) * 2048:((fb % 2) + 1) * 2048
                                            ].rearrange("p (a m) -> p a m", a=2),
                                    in_=d_w2[fb * 2:fb * 2 + 2].rearrange(
                                        "a p m -> p a m")), 16)
                        block.sync(sy_w2)

                        def v_w2(v, tg=tg, hf=hf):
                            for fc in range(NF):
                                fb = fc // 2
                                gi = (tg * 2 + hf) * NF + fc
                                p.wait(v, "ldw2%d" % (fb % 2),
                                       ((tg * 2 + hf) * 8 + fb // 2 + 1) * 16)
                                if gi >= 2:
                                    p.wait(v, "myT", gi - 1)
                                p.inc("w2cV", v.tensor_copy(
                                    out=w2b[:, (fc % 2) * D:((fc % 2) + 1) * D],
                                    in_=w2f[:, (fb % 2) * 2048 + (fc % 2) * D:
                                            (fb % 2) * 2048 + ((fc % 2) + 1) * D]))
                        block.vector(v_w2)

                        def s_w2(s, tg=tg, hf=hf):
                            if hf == nhalf - 1 and tg + 1 < NTG_M:
                                w1cast(s, (tg + 1) * NF)
                                w1cast(s, (tg + 1) * NF + 1)
                        block.scalar(s_w2)

                        def t_y(t, tg=tg, hf=hf, tis=tis):
                            p.wait(t, "mhS", (tg + 1) * NF)
                            p.wait(t, "myV", p.val("myV"))
                            for fc in range(NF):
                                gi = (tg * 2 + hf) * NF + fc
                                p.wait(t, "w2cV", gi + 1)
                                last = None
                                for ti in tis:
                                    for oh in range(2):
                                        last = t.matmul(
                                            psM[:, ((ti % 2) * 2 + oh) * 512:
                                                ((ti % 2) * 2 + oh + 1) * 512],
                                            hT[:, fc * 512 + ti * 128:
                                               fc * 512 + (ti + 1) * 128],
                                            w2b[:, (fc % 2) * D + oh * 512:
                                                (fc % 2) * D + (oh + 1) * 512],
                                            start=(fc == 0), stop=(fc == NF - 1))
                                p.inc("myT", last)
                        block.tensor(t_y)

                        def v_y(v, tg=tg, tis=tis):
                            p.wait(v, "ldfw", 16)
                            p.wait(v, "stys", p.val("stys"))
                            p.wait(v, "myT")
                            for ti in tis:
                                for oh in range(2):
                                    v.tensor_scalar(
                                        out=ys_sb[:, (ti % 2) * D + oh * 512:
                                                  (ti % 2) * D + (oh + 1) * 512],
                                        in0=psM[:, ((ti % 2) * 2 + oh) * 512:
                                                ((ti % 2) * 2 + oh + 1) * 512],
                                        scalar1=fw_sb[:, tg * 4 + ti:tg * 4 + ti + 1],
                                        scalar2=None, op0=OP.mult)
                                p.inc("myV", v.engine_nop())
                        block.vector(v_y)

                        def sy_y(sy, tg=tg, tis=tis):
                            p.wait(sy, "myV")
                            for ti in tis:
                                r0 = tg * 512 + ti * 128
                                p.inc("stys", sy.dma_start(
                                    out=ag_in[r0:r0 + 128, :],
                                    in_=ys_sb[:, (ti % 2) * D:((ti % 2) + 1) * D]),
                                    16)
                        block.sync(sy_y)

                    xln_sched = [i for i in range(NT)
                                 if (i // 4 if i < NPT else i % NTG_M) == tg]
                    for i in xln_sched:
                        k = XLN_K[0]
                        XLN_K[0] += 1

                        def g_xld(g, i=i, k=k):
                            if k >= 4:
                                p.wait(g, "l0A", k - 3)
                            p.inc("xla%d" % (k % 4), g.dma_start(
                                out=xrot[:, (k % 4) * D:((k % 4) + 1) * D],
                                in_=d_x[i * 128:(i + 1) * 128, :]), 16)
                        block.gpsimd(g_xld)

                        def v_xst(v, i=i, k=k):
                            p.wait(v, "xla%d" % (k % 4), (k // 4 + 1) * 16)
                            ln_passA(v, xrot[:, (k % 4) * D:((k % 4) + 1) * D], i)
                            p.inc("l0A", v.engine_nop())
                        block.vector(v_xst)

                mark("M_ag")
                def g_ag(g):
                    for k in range(NTG_M):
                        c0 = k * 512
                        ck = min(512, npad - c0)
                        p.wait(g, "stys", ((c0 + ck) // 128) * 16)
                        p.inc("cc", g.collective_compute(
                            "AllGather", OP.bypass,
                            replica_groups=[list(range(NCORES))],
                            ins=[ag_in[c0:c0 + ck, :]],
                            outs=[ag_out[NCORES * c0:NCORES * (c0 + ck), :]]))
                block.gpsimd(g_ag)

            esL = ExitStack()
            with esL:
                sbL = lambda *a: esL.enter_context(nc.sbuf_tensor(*a))
                xnT = sbL("xnT", [128, ND * 512], BF16)
                woutT = sbL("woutT", [128, NCC * D], BF16)
                cdiag = sbL("cdiag", [128, NCC * 4 * 128], BF16)
                y_sb = sbL("y_sb", [128, NCC * 512], BF16)
                xz_sb = sbL("xz_sb", [128, 2 * 520], BF16)
                halo = sbL("halo", [128, NCC * 4], BF16)
                plane = sbL("plane", [128, 4 * D], F32)
                psZ = esL.enter_context(nc.psum_tensor("psZ", [128, 1024], F32))
                psC2 = esL.enter_context(nc.psum_tensor("psC2", [128, 1024], F32))
                psO = esL.enter_context(nc.psum_tensor("psO", [128, 1024], F32))
                psT0 = esL.enter_context(nc.psum_tensor("psT0", [128, 2048], BF16))
                def g_l0w2(g):
                    p.wait(g, "myV", p.val("myV"))
                    p.wait(g, "stys", p.val("stys"))
                    p.inc("ldcd", g.dma_start(out=cdiag[:], in_=d_cdiag), 16)
                    p.inc("ldwo", g.dma_start(out=woutT[:], in_=d_woutT), 16)
                block.gpsimd(g_l0w2)

                mark("L0_lnA")
                L0B = ln_passB(NT, "l0")

                def v_hz(v):
                    p.wait(v, "cst", CST)
                    p.wait(v, "stys", p.val("stys"))
                    p.wait(v, "myV", p.val("myV"))
                    v.memset(halo[:], 0.0)
                    p.inc("haloz", v.engine_nop())
                block.vector(v_hz)

                mark("L0_tg")
                for tg in range(4):
                    for ii in range(4):
                        i = tg * 4 + ii

                        def sy_ld(sy, i=i):
                            if i < 4:
                                p.wait(sy, "l0A", 13 + i)
                            else:
                                p.wait(sy, "l0asm", i - 3)
                            p.inc("ldxC%d" % (i % 4), sy.dma_start(
                                out=xrot[:, (i % 4) * D:((i % 4) + 1) * D],
                                in_=d_x[i * 128:(i + 1) * 128, :]), 16)
                        block.sync(sy_ld)

                        def v_ap(v, i=i):
                            p.wait(v, "ldxC%d" % (i % 4), (i // 4 + 1) * 16)
                            p.wait(v, "l0B", L0B)
                            if i >= 2:
                                p.wait(v, "l0trT", i - 1)
                            if _STAGE == 0 and i >= 2:
                                p.wait(v, "stdbg%d" % (i % 2), (i // 2) * 16)
                            ln_apply(v, xrot[:, (i % 4) * D:((i % 4) + 1) * D],
                                     lnbf[:, (i % 2) * D:((i % 2) + 1) * D]
                                     if _STAGE != 0 else
                                     xorot[:, (i % 2) * D:((i % 2) + 1) * D], i)
                            if _STAGE == 0:
                                v.drain()
                                v.tensor_copy(out=lnbf[:, (i % 2) * D:
                                                       ((i % 2) + 1) * D],
                                              in_=xorot[:, (i % 2) * D:((i % 2) + 1) * D])
                            p.inc("l0apC", v.engine_nop())
                        block.vector(v_ap)

                        if _STAGE == 0:
                            def sy_dbg(sy, i=i):
                                p.wait(sy, "l0apC", i + 1)
                                p.inc("stdbg%d" % (i % 2), sy.dma_start(
                                    out=d_out[i * 128:(i + 1) * 128, :],
                                    in_=xorot[:, (i % 2) * D:((i % 2) + 1) * D]), 16)
                            block.sync(sy_dbg)

                        def t_tr(t, i=i):
                            p.wait(t, "l0apC", i + 1)
                            if i >= 2:
                                p.wait(t, "l0trV", i - 1)
                            if i == 0:
                                p.wait(t, "myV", p.val("myV"))
                            if i >= 4:
                                p.wait(t, "l0zT", (i // 4) * NCC)
                            last = None
                            for dk in range(ND):
                                last = t.transpose(
                                    psT0[:, (i % 2) * 1024 + dk * 128:
                                         (i % 2) * 1024 + (dk + 1) * 128],
                                    lnbf[:, (i % 2) * D + dk * 128:
                                         (i % 2) * D + (dk + 1) * 128], ident_b[:])
                            p.inc("l0trT", last)
                        block.tensor(t_tr)

                        def v_ev(v, i=i, ii=ii):
                            p.wait(v, "l0trT", i + 1)
                            for dk in range(ND):
                                v.tensor_copy(
                                    out=xnT[:, dk * 512 + ii * 128:
                                            dk * 512 + (ii + 1) * 128],
                                    in_=psT0[:, (i % 2) * 1024 + dk * 128:
                                             (i % 2) * 1024 + (dk + 1) * 128])
                            p.inc("l0trV", v.engine_nop())
                        block.vector(v_ev)

                    def t_yp(t, tg=tg):
                        p.wait(t, "l0trV", tg * 4 + 4)
                        if tg == 0:
                            p.wait(t, "ldwin", 16)
                            p.wait(t, "ldcd", 16)
                        for cc in range(NCC):
                            gi = tg * NCC + cc
                            if gi >= 2:
                                p.wait(t, "l0xz", 2 * (gi - 1))
                            last = None
                            for dk in range(ND):
                                last = t.matmul(
                                    psZ[:, (cc % 2) * 512:((cc % 2) + 1) * 512],
                                    winT[:, dk * C + cc * 128:dk * C + (cc + 1) * 128],
                                    xnT[:, dk * 512:(dk + 1) * 512],
                                    start=(dk == 0), stop=(dk == ND - 1))
                            p.inc("l0zT", last)
                            p.wait(t, "l0xz", 2 * gi + 2)
                            if gi >= 2:
                                p.wait(t, "l0sil", gi - 1)
                            last = None
                            for k in range(4):
                                last = t.matmul(
                                    psC2[:, (cc % 2) * 512:((cc % 2) + 1) * 512],
                                    cdiag[:, (cc * 4 + k) * 128:(cc * 4 + k + 1) * 128],
                                    xz_sb[:, (cc % 2) * 520 + k:(cc % 2) * 520 + k + 512],
                                    start=(k == 0), stop=(k == 3))
                            p.inc("l0cvT", last)
                    block.tensor(t_yp)

                    def s_yp(s, tg=tg):
                        for cc in range(NCC):
                            gi = tg * NCC + cc
                            p.wait(s, "l0zT", gi + 1)
                            if gi >= 2:
                                p.wait(s, "l0cvT", gi - 1)
                            if tg == 0 and cc == 0:
                                p.wait(s, "haloz", 1)
                            if tg >= 1 and cc == 0:
                                p.wait(s, "l0opT", tg * 8)
                            s.activation(out=xz_sb[:, (cc % 2) * 520 + 3:
                                                   (cc % 2) * 520 + 515],
                                         in_=psZ[:, (cc % 2) * 512:((cc % 2) + 1) * 512],
                                         func=ACT.Identity)
                            s.copy(out=xz_sb[:, (cc % 2) * 520:(cc % 2) * 520 + 3],
                                   in_=halo[:, cc * 4:cc * 4 + 3])
                            s.drain()
                            p.inc("l0xz", s.copy(
                                out=halo[:, cc * 4:cc * 4 + 3],
                                in_=xz_sb[:, (cc % 2) * 520 + 512:
                                          (cc % 2) * 520 + 515]), 2)
                            p.wait(s, "l0cvT", gi + 1)
                            p.inc("l0sil", s.activation(
                                out=y_sb[:, cc * 512:(cc + 1) * 512],
                                in_=psC2[:, (cc % 2) * 512:((cc % 2) + 1) * 512],
                                func=ACT.Silu, bias=cb_c[:, cc:cc + 1], scale=1.0))
                    block.scalar(s_yp)

                    def g_pl(g, tg=tg):
                        if _STAGE >= 2:
                            for ii in range(4):
                                tt = tg * 4 + ii
                                p.wait(g, "cc", NTG_M)
                                if tt >= 2:
                                    p.wait(g, "l0asm", tt - 1)
                                for pl in range(2):
                                    p.inc("gpl%d" % (tt % 2), g.indirect_dma_start(
                                        out=plane[:, ((tt % 2) * 2 + pl) * D:
                                                  ((tt % 2) * 2 + pl + 1) * D],
                                        out_offset=None,
                                        in_=ag_out,
                                        in_offset=IndirectOffsetOnAxis(
                                            gidx_sb[:, pl * 16 + tt:pl * 16 + tt + 1],
                                            0)), 16)
                    block.gpsimd(g_pl)

                    def t_op(t, tg=tg):
                        p.wait(t, "l0sil", (tg + 1) * NCC)
                        if tg == 0:
                            p.wait(t, "ldwo", 16)
                            p.wait(t, "cst", CST)
                        for ii in range(4):
                            tt = tg * 4 + ii
                            for oh in range(2):
                                if tt >= 1:
                                    p.wait(t, "l0asV", 2 * (tt - 1) + oh + 1)
                                last = None
                                for cc in range(NCC):
                                    last = t.matmul(
                                        psO[:, oh * 512:(oh + 1) * 512],
                                        y_sb[:, cc * 512 + ii * 128:
                                             cc * 512 + (ii + 1) * 128],
                                        woutT[:, cc * D + oh * 512:
                                              cc * D + (oh + 1) * 512],
                                        start=(cc == 0), stop=False)
                                last = t.matmul(
                                    psO[:, oh * 512:(oh + 1) * 512],
                                    ones_b[0:1, 0:128],
                                    rows_b[0:1, R_B0 * D + oh * 512:
                                           R_B0 * D + (oh + 1) * 512],
                                    start=False, stop=True)
                                p.inc("l0opT", last)
                    block.tensor(t_op)

                    def v_asm(v, tg=tg):
                        for ii in range(4):
                            tt = tg * 4 + ii
                            p.wait(v, "ldxC%d" % (tt % 4), (tt // 4 + 1) * 16)
                            if tt >= 2:
                                p.wait(v, "stx1%d" % (tt % 2), (tt // 2) * 16)
                            for oh in range(2):
                                p.wait(v, "l0opT", 2 * tt + oh + 1)
                                v.tensor_tensor(
                                    out=xorot[:, (tt % 2) * D + oh * 512:
                                              (tt % 2) * D + (oh + 1) * 512],
                                    in0=psO[:, oh * 512:(oh + 1) * 512],
                                    in1=xrot[:, (tt % 4) * D + oh * 512:
                                             (tt % 4) * D + (oh + 1) * 512],
                                    op=OP.add)
                                p.inc("l0asV", v.engine_nop())
                            v.drain()
                            if _STAGE >= 2:
                                p.wait(v, "gpl%d" % (tt % 2), (tt // 2 + 1) * 32)
                                for pl in range(2):
                                    v.tensor_tensor(
                                        out=xorot[:, (tt % 2) * D:((tt % 2) + 1) * D],
                                        in0=xorot[:, (tt % 2) * D:((tt % 2) + 1) * D],
                                        in1=plane[:, ((tt % 2) * 2 + pl) * D:
                                                  ((tt % 2) * 2 + pl + 1) * D],
                                        op=OP.add)
                                    v.drain()
                            if _STAGE >= 3:
                                ln_passA(v, xorot[:, (tt % 2) * D:((tt % 2) + 1) * D],
                                         tt)
                                p.inc("l2A", v.engine_nop())
                            p.inc("l0asm", v.engine_nop())
                    block.vector(v_asm)

                    def sy_st(sy, tg=tg):
                        for ii in range(4):
                            tt = tg * 4 + ii
                            p.wait(sy, "l0asm", tt + 1)
                            dst = d_out if _STAGE in (1, 2) else x1_d
                            p.inc("stx1%d" % (tt % 2), sy.dma_start(
                                out=dst[tt * 128:(tt + 1) * 128, :],
                                in_=xorot[:, (tt % 2) * D:((tt % 2) + 1) * D]), 16)
                    block.sync(sy_st)

        if _STAGE <= 2:
            def sy_fin(sy):
                p.wait(sy, "stx10", p.val("stx10"))
                p.wait(sy, "stx11", p.val("stx11"))
            block.sync(sy_fin)
            p.close()
            return nc

        mark("L1")
        _build_l1(nc, p, block, dict(
            d_x=d_x, d_Wq=d_Wq, d_Wk=d_Wk, d_Wv=d_Wv, d_Wo=d_Wo, d_W8T=d_W8T,
            d_out=d_out, x1_d=x1_d, q_d=q_d, qln_d=qln_d,
            ar_in=ar_in, ar_out=ar_out, m_d=m_d,
            xn2T=xn2T, xrot=xrot, xorot=xorot, ident_f=ident_f, ident_b=ident_b,
            ones_b=ones_b, ones_f=ones_f, rows_b=rows_b, bo_f=bo_f,
            cols_f=cols_f, gmax=gmax, armx=armx, xsp=xsp, xsb=xsb,
            stats_b=stats_b, mva=mva, rsa=rsa, ln1=ln1, lnsc=lnsc, lnbf=lnbf,
            bkh=bkh,
            row=row, col=col, CST=CST,
            R_B0=R_B0, R_BQ=R_BQ, R_BV=R_BV, R_BO=R_BO, R_LNB=R_LNB,
            R_WOB=R_WOB, C_BK8=C_BK8, C_G=C_G, C_B=C_B, C_WSC=C_WSC,
            C_B1=C_B1, C_SV=C_SV, C_LNB=C_LNB,
            ln_passA=ln_passA, ln_passB=ln_passB, ln_apply=ln_apply,
            mark=mark))
        p.close()
    return nc


def _build_l1(nc, p, block, e):
    g = dict(e)
    d_Wq, d_Wk, d_Wv, d_Wo, d_W8T = g["d_Wq"], g["d_Wk"], g["d_Wv"], g["d_Wo"], g["d_W8T"]
    d_out, x1_d, q_d, qln_d = g["d_out"], g["x1_d"], g["q_d"], g["qln_d"]
    ar_in, ar_out = g["ar_in"], g["ar_out"]
    m_d = g["m_d"]
    xn2T, xrot, xorot = g["xn2T"], g["xrot"], g["xorot"]
    ident_f, ident_b, ones_b, ones_f = g["ident_f"], g["ident_b"], g["ones_b"], g["ones_f"]
    rows_b, bo_f, cols_f = g["rows_b"], g["bo_f"], g["cols_f"]
    gmax, armx, xsp, xsb = g["gmax"], g["armx"], g["xsp"], g["xsb"]
    mva, rsa, ln1, lnsc, lnbf = g["mva"], g["rsa"], g["ln1"], g["lnsc"], g["lnbf"]
    bkh = g["bkh"]
    row, col, CST = g["row"], g["col"], g["CST"]
    R_BQ, R_BV, R_BO, R_WOB = g["R_BQ"], g["R_BV"], g["R_BO"], g["R_WOB"]
    C_BK8, C_G, C_B, C_WSC, C_B1, C_SV, C_LNB = (
        g["C_BK8"], g["C_G"], g["C_B"], g["C_WSC"], g["C_B1"], g["C_SV"], g["C_LNB"])
    ln_passA, ln_passB, ln_apply = g["ln_passA"], g["ln_passB"], g["ln_apply"]

    es0 = ExitStack()
    with es0:
        attn_acc = es0.enter_context(
            nc.sbuf_tensor("attn_acc", [128, NT * D], BF16))
        wfull = es0.enter_context(
            nc.sbuf_tensor("wfull", [128, ND * D], BF16))
        lnbc_b = es0.enter_context(nc.sbuf_tensor("lnbc_b", [128, ND], BF16))

        esA = ExitStack()
        with esA:
            sbA = lambda *a: esA.enter_context(nc.sbuf_tensor(*a))
            ktp = sbA("ktp", [64, T], BF16)
            VAUG = sbA("VAUG", [128, NT * H * 65], BF16)
            QM = sbA("QM", [65, 2 * T], BF16)
            KB = sbA("KB", [65, 2 * T], BF16)
            PT = sbA("PT", [128, 4 * D], BF16)
            qst = PT
            auT = sbA("auT", [80, 2 * T], BF16)
            blk = sbA("blk", [128, 2 * 16 * 80], BF16)

            nrmA = sbA("nrmA", [128, 32], F32)
            negm = sbA("negm", [128, 16], F32)
            rec_t = sbA("rec_t", [128, 16], F32)
            mst = sbA("mst", [16, 2 * 128], BF16)
            esQ = ExitStack()
            psQ = esQ.enter_context(nc.psum_tensor("psQ", [128, 2048], F32))
            psTq = esQ.enter_context(nc.psum_tensor("psTq", [128, 2048], BF16))

            L0ASM = p.val("l0asm")

            L2B = ln_passB(NT, "l2")

            e["mark"]("LN2C")
            for i in range(NT):
                def sy_ld(sy, i=i):
                    p.wait(sy, "stx1%d" % (i % 2), (i // 2 + 1) * 16)
                    if i >= 4:
                        p.wait(sy, "l2apC", i - 3)
                    p.inc("ldr%d" % (i % 4), sy.dma_start(
                        out=xrot[:, (i % 4) * D:((i % 4) + 1) * D],
                        in_=x1_d[i * 128:(i + 1) * 128, :]), 16)
                block.sync(sy_ld)

                def v_ap(v, i=i):
                    p.wait(v, "ldr%d" % (i % 4), (i // 4 + 1) * 16)
                    p.wait(v, "l2B", L2B)
                    if i >= 2:
                        p.wait(v, "l2trT", i - 1)
                    if _STAGE == 3 and i >= 1:
                        p.wait(v, "stdb%d" % ((i - 1) % 2), ((i - 1) // 2 + 1) * 16)
                    ln_apply(v, xrot[:, (i % 4) * D:((i % 4) + 1) * D],
                             lnbf[:, (i % 2) * D:((i % 2) + 1) * D], i)
                    v.drain()
                    v.tensor_reduce(out=ln1[:, 0:1],
                                    in_=lnbf[:, (i % 2) * D:((i % 2) + 1) * D],
                                    axis=AX,
                                    op=OP.max, apply_absolute_value=True)
                    v.drain()
                    if i == 0:
                        v.tensor_copy(out=gmax[:], in_=ln1[:, 0:1])
                    else:
                        v.tensor_tensor(out=gmax[:], in0=gmax[:],
                                        in1=ln1[:, 0:1], op=OP.max)
                    if _STAGE == 3:
                        v.tensor_copy(out=lnsc[:],
                                      in_=lnbf[:, (i % 2) * D:((i % 2) + 1) * D])
                    p.inc("l2apC", v.engine_nop())
                block.vector(v_ap)

                def t_tr(t, i=i):
                    p.wait(t, "l2apC", i + 1)
                    if i >= 2:
                        p.wait(t, "l2trV", i - 1)
                    if i == 0:
                        p.wait(t, "l0asm", L0ASM)
                    last = None
                    for dk in range(ND):
                        last = t.transpose(
                            psTq[:, (i % 2) * 1024 + dk * 128:
                                 (i % 2) * 1024 + (dk + 1) * 128],
                            lnbf[:, (i % 2) * D + dk * 128:
                                 (i % 2) * D + (dk + 1) * 128], ident_b[:])
                    p.inc("l2trT", last)
                block.tensor(t_tr)

                def v_ev(v, i=i):
                    p.wait(v, "l2trT", i + 1)
                    for dk in range(ND):
                        v.tensor_copy(
                            out=xn2T[:, dk * T + i * 128:dk * T + (i + 1) * 128],
                            in_=psTq[:, (i % 2) * 1024 + dk * 128:
                                     (i % 2) * 1024 + (dk + 1) * 128])
                    p.inc("l2trV", v.engine_nop())
                block.vector(v_ev)

                if _STAGE == 3:
                    def sy_dbg(sy, i=i):
                        p.wait(sy, "l2trV", i + 1)
                        p.inc("stdb%d" % (i % 2), sy.dma_start(
                            out=d_out[i * 128:(i + 1) * 128, :],
                            in_=lnsc[:]), 16)
                    block.sync(sy_dbg)

            if _STAGE == 3:
                def sy_fin(sy):
                    p.wait(sy, "stdb0", p.val("stdb0"))
                    p.wait(sy, "stdb1", p.val("stdb1"))
                block.sync(sy_fin)
                return


            e["mark"]("QP")
            def g_wq(gg):
                p.wait(gg, "l0opT", p.val("l0opT"))
                p.inc("ldWq", gg.dma_start(out=wfull[:], in_=d_Wq), 16)
            block.gpsimd(g_wq)

            for tt in range(NT):
                def t_q(t, tt=tt):
                    p.wait(t, "ldWq", 16)
                    p.wait(t, "l2trV", tt + 1)
                    if tt == 0:
                        p.wait(t, "cst", CST)
                    if tt >= 2:
                        p.wait(t, "qevS", tt - 1)
                    for dh in range(2):
                        last = None
                        for dk in range(ND):
                            last = t.matmul(
                                psQ[:, (tt % 2) * 1024 + dh * 512:
                                    (tt % 2) * 1024 + (dh + 1) * 512],
                                xn2T[:, dk * T + tt * 128:dk * T + (tt + 1) * 128],
                                wfull[:, dk * D + dh * 512:dk * D + (dh + 1) * 512],
                                start=(dk == 0), stop=False)
                        last = t.matmul(
                            psQ[:, (tt % 2) * 1024 + dh * 512:
                                (tt % 2) * 1024 + (dh + 1) * 512],
                            ones_b[0:1, 0:128],
                            rows_b[0:1, R_BQ * D + dh * 512:R_BQ * D + (dh + 1) * 512],
                            start=False, stop=True)
                    p.inc("qmmT", last)
                block.tensor(t_q)

                def s_q(s, tt=tt):
                    p.wait(s, "qmmT", tt + 1)
                    if tt >= 2:
                        p.wait(s, "stq%d" % (tt % 2), (tt // 2) * 16)
                    p.inc("qevS", s.activation(
                        out=xorot[:, (tt % 2) * D:((tt % 2) + 1) * D],
                        in_=psQ[:, (tt % 2) * 1024:(tt % 2) * 1024 + 1024],
                        func=ACT.Identity))
                block.scalar(s_q)

                def v_q(v, tt=tt):
                    p.wait(v, "qevS", tt + 1)
                    ln_passA(v, xorot[:, (tt % 2) * D:((tt % 2) + 1) * D], tt)
                    p.inc("qlA", v.engine_nop())
                block.vector(v_q)

                def sy_q(sy, tt=tt):
                    p.wait(sy, "qlA", tt + 1)
                    p.inc("stq%d" % (tt % 2), sy.dma_start(
                        out=q_d[tt * 128:(tt + 1) * 128, :],
                        in_=xorot[:, (tt % 2) * D:((tt % 2) + 1) * D]), 16)
                block.sync(sy_q)

            QLB = ln_passB(NT, "ql")

            e["mark"]("VP")
            def g_wv(gg):
                p.wait(gg, "qmmT", NT)
                p.inc("ldWv", gg.dma_start(out=wfull[:], in_=d_Wv), 16)
            block.gpsimd(g_wv)

            def v_vz(v):
                p.wait(v, "cst", CST)
                v.memset(VAUG[:, 64:NT * H * 65:65], 1.0)
                p.inc("vzV", v.engine_nop())
            block.vector(v_vz)

            for tt in range(NT):
                def t_v(t, tt=tt):
                    p.wait(t, "ldWv", 16)
                    p.wait(t, "l2trV", NT)
                    if tt == 0:
                        p.wait(t, "qevS", NT)
                    if tt >= 2:
                        p.wait(t, "vevV", tt - 1)
                    for dh in range(2):
                        last = None
                        for dk in range(ND):
                            last = t.matmul(
                                psQ[:, (tt % 2) * 1024 + dh * 512:
                                    (tt % 2) * 1024 + (dh + 1) * 512],
                                xn2T[:, dk * T + tt * 128:dk * T + (tt + 1) * 128],
                                wfull[:, dk * D + dh * 512:dk * D + (dh + 1) * 512],
                                start=(dk == 0), stop=False)
                        last = t.matmul(
                            psQ[:, (tt % 2) * 1024 + dh * 512:
                                (tt % 2) * 1024 + (dh + 1) * 512],
                            ones_b[0:1, 0:128],
                            rows_b[0:1, R_BV * D + dh * 512:R_BV * D + (dh + 1) * 512],
                            start=False, stop=True)
                    p.inc("vmmT", last)
                block.tensor(t_v)

                def v_v(v, tt=tt):
                    p.wait(v, "vmmT", tt + 1)
                    p.wait(v, "vzV", 1)
                    for dh in range(2):
                        base = (tt * H + dh * 8) * 65
                        dst = VAUG[:, base:base + 8 * 65].rearrange(
                            "p (a b) -> p a b", b=65)[:, :, 0:64]
                        srcp = psQ[:, (tt % 2) * 1024 + dh * 512:
                                   (tt % 2) * 1024 + (dh + 1) * 512].rearrange(
                            "p (a b) -> p a b", b=64)
                        v.tensor_copy(out=dst, in_=srcp)
                    p.inc("vevV", v.engine_nop())
                block.vector(v_v)

            def t_gm(t):
                p.wait(t, "l2apC", NT)
                p.wait(t, "vevV", p.val("vevV"))
                p.inc("gmT", t.transpose(psQ[0:1, 0:128], gmax[:, 0:1],
                                         ident_f[:]))
            block.tensor(t_gm)

            def v_gm(v):
                p.wait(v, "gmT", 1)
                v.tensor_copy(out=armx[:], in_=psQ[0:1, 0:128])
                p.inc("gmV", v.engine_nop())
            block.vector(v_gm)

            def sy_gm(sy):
                p.wait(sy, "gmV", 1)
                p.inc("argi", sy.dma_start(out=ar_in, in_=armx[:]), 16)
            block.sync(sy_gm)

            def g_ar(gg):
                p.wait(gg, "argi", 16)
                p.inc("cc2", gg.collective_compute(
                    "AllReduce", OP.max, replica_groups=[list(range(NCORES))],
                    ins=[ar_in], outs=[ar_out]))
            block.gpsimd(g_ar)

            def sy_gm2(sy):
                p.wait(sy, "cc2", 1)
                p.inc("argo", sy.dma_start(out=armx[:], in_=ar_out), 16)
            block.sync(sy_gm2)

            def v_xs(v):
                p.wait(v, "argo", 16)
                v.tensor_reduce(out=xsp[0:1, 0:1], in_=armx[:], axis=AX,
                                op=OP.max)
                v.drain()
                v.tensor_scalar(out=xsp[0:1, 0:1], in0=xsp[0:1, 0:1],
                                scalar1=1.0 / 127.0, scalar2=1e-5,
                                op0=OP.mult, op1=OP.max)
                v.drain()
                v.reciprocal(out=xsp[0:1, 1:2], in_=xsp[0:1, 0:1])
                p.inc("xsV", v.engine_nop())
            block.vector(v_xs)


            e["mark"]("QC")
            for i in range(NT):
                def sy_ld(sy, i=i):
                    p.wait(sy, "stq%d" % (i % 2), (i // 2 + 1) * 16)
                    if i >= 4:
                        p.wait(sy, "qcC", i - 3)
                    p.inc("ldq%d" % (i % 4), sy.dma_start(
                        out=xrot[:, (i % 4) * D:((i % 4) + 1) * D],
                        in_=q_d[i * 128:(i + 1) * 128, :]), 16)
                block.sync(sy_ld)

                def v_ap(v, i=i):
                    p.wait(v, "ldq%d" % (i % 4), (i // 4 + 1) * 16)
                    p.wait(v, "qlB", QLB)
                    if i >= 2:
                        p.wait(v, "qcT", i - 1)
                    ln_apply(v, xrot[:, (i % 4) * D:((i % 4) + 1) * D],
                             lnbf[:, (i % 2) * D:((i % 2) + 1) * D], i)
                    p.inc("qcC", v.engine_nop())
                block.vector(v_ap)

                def t_tr(t, i=i):
                    p.wait(t, "qcC", i + 1)
                    if i >= 2:
                        p.wait(t, "qcV", i - 1)
                    last = None
                    for dk in range(ND):
                        last = t.transpose(
                            psTq[:, (i % 2) * 1024 + dk * 128:
                                 (i % 2) * 1024 + (dk + 1) * 128],
                            lnbf[:, (i % 2) * D + dk * 128:
                                 (i % 2) * D + (dk + 1) * 128], ident_b[:])
                    p.inc("qcT", last)
                block.tensor(t_tr)

                def v_ev(v, i=i):
                    p.wait(v, "qcT", i + 1)
                    if i >= 2:
                        p.wait(v, "stqn%d" % (i % 2), (i // 2) * 16)
                    for dk in range(ND):
                        v.tensor_scalar(
                            out=qst[:, (i % 2) * D + dk * 128:
                                    (i % 2) * D + (dk + 1) * 128],
                            in0=psTq[:, (i % 2) * 1024 + dk * 128:
                                     (i % 2) * 1024 + (dk + 1) * 128],
                            scalar1=col(C_G)[:, dk:dk + 1],
                            scalar2=col(C_B)[:, dk:dk + 1],
                            op0=OP.mult, op1=OP.add)
                    p.inc("qcV", v.engine_nop())
                block.vector(v_ev)

                def sy_st(sy, i=i):
                    p.wait(sy, "qcV", i + 1)
                    dst = qln_d.rearrange("p (a b) -> p a b", a=ND)[
                        :, :, i * 128:(i + 1) * 128]
                    srcq = qst[:, (i % 2) * D:((i % 2) + 1) * D].rearrange(
                        "p (a b) -> p a b", b=128)
                    p.inc("stqn%d" % (i % 2), sy.dma_start(out=dst, in_=srcq), 16)
                block.gpsimd(sy_st)

            def t_xs(t):
                p.wait(t, "xsV", 1)
                p.inc("xsT", t.matmul(psQ[:, 0:2], ones_f[0:1, 0:128],
                                      xsp[0:1, 0:2], start=True, stop=True))
            block.tensor(t_xs)

            def v_xs2(v):
                p.wait(v, "xsT", 1)
                v.tensor_copy(out=xsb[:], in_=psQ[:, 0:2])
                v.drain()
                v.tensor_scalar(out=col(C_SV), in0=col(C_WSC),
                                scalar1=xsb[:, 0:1], scalar2=None, op0=OP.mult)
                v.drain()
                p.inc("xsV2", v.engine_nop())
            block.vector(v_xs2)

            def g_wk(gg):
                p.wait(gg, "vmmT", NT)
                p.inc("ldWk", gg.dma_start(out=wfull[:], in_=d_Wk), 16)
            block.gpsimd(g_wk)

            def v_kbz(v):
                v.memset(KB[64:65, 0:T], 1.0)
                v.memset(KB[64:65, T:2 * T], 1.0)
                v.memset(auT[64:80, :], 0.0)
                p.inc("kbzV", v.engine_nop())
            block.vector(v_kbz)

            esQ.close()
            esB = ExitStack()
            with esB:
                psA = esB.enter_context(nc.psum_tensor("psA", [128, 1024], F32))
                psSS = esB.enter_context(nc.psum_tensor("psSS", [128, 2048], F32))
                psS0 = psSS
                psV = esB.enter_context(nc.psum_tensor("psV", [65, 1024], F32))

                e["mark"]("ATT")
                def kT_pieces(ha):
                    ps = []
                    for ts in range(4):
                        def f(t, ha=ha, ts=ts):
                            if ts == 0:
                                p.wait(t, "ldWk", 16)
                                if ha == 0:
                                    p.wait(t, "qcV", NT)
                                    p.wait(t, "vevV", NT)
                                    p.wait(t, "xsV2", 1)
                                else:
                                    p.wait(t, "mstV", ha)
                                    p.wait(t, "kbV", ha)
                                    p.wait(t, "kevS", ha * 4)
                            if ts >= 2:
                                p.wait(t, "kevS", ha * 4 + ts - 1)
                            last = None
                            for dk in range(ND):
                                last = t.matmul(
                                    psA[0:64, (ts % 2) * 512:((ts % 2) + 1) * 512],
                                    wfull[:, dk * D + ha * 64:dk * D + ha * 64 + 64],
                                    xn2T[:, dk * T + ts * 512:dk * T + (ts + 1) * 512],
                                    start=(dk == 0), stop=(dk == ND - 1))
                            p.inc("kmm2", last)
                        ps.append(f)
                    return ps

                def a_pieces(ha):
                    ps = []
                    for qt in range(NT):
                        for sub in range(2):
                            def f(t, ha=ha, qt=qt, sub=sub):
                                if qt + sub == 0:
                                    p.wait(t, "qmd%d" % (ha % 2),
                                           (ha // 2 + 1) * 16)
                                    p.wait(t, "kevS", (ha + 1) * 4)
                                p.wait(t, "rmV", ha * 32 + qt * 2 + sub)
                                for jj in range(2):
                                    last = t.matmul(
                                        psA[:, jj * 512:(jj + 1) * 512],
                                        QM[0:64, (ha % 2) * T + qt * 128:
                                           (ha % 2) * T + (qt + 1) * 128],
                                        ktp[0:64,
                                            sub * 1024 + jj * 512:
                                            sub * 1024 + (jj + 1) * 512],
                                        start=True, stop=True)
                                p.inc("aT", last)
                            ps.append(f)

                    def fn(t, ha=ha):
                        p.wait(t, "rmhV", ha + 1)
                        p.inc("nmT", t.transpose(psA[0:16, 0:128],
                                                 negm[:, 0:16], ident_f[:]))
                    ps.append(fn)
                    return ps

                for h in range(-1, H):
                    ha = h + 1
                    e["mark"]("slot%d" % h)

                    def t_slot(t, h=h, ha=ha):
                        pieces = []
                        if ha < H:
                            pieces = kT_pieces(ha)[2:] + a_pieces(ha)
                        if ha + 1 < H:
                            pieces = pieces + kT_pieces(ha + 1)[:2]
                        if h < 0:
                            for f in kT_pieces(0)[:2]:
                                f(t)
                            for f in pieces:
                                f(t)
                            return
                        p.wait(t, "qmr%d" % (h % 2), (h // 2 + 1) * 16)
                        p.wait(t, "kbV", h + 1)
                        pend = []

                        def emit_av(t, h, half, j):
                            g = h * 32 + half * 16 + j
                            p.wait(t, "expS", g + 1)
                            if j == 0:
                                p.wait(t, "uevS", h * 2 + half)
                            for qs in range(2):
                                last = t.matmul(
                                    psV[0:65, qs * 512:(qs + 1) * 512],
                                    VAUG[:, (j * H + h) * 65:(j * H + h) * 65 + 65],
                                    PT[:, (g % 2) * D + qs * 512:
                                       (g % 2) * D + (qs + 1) * 512],
                                    start=(j == 0), stop=(j == NT - 1))
                            p.inc("avT", last)

                        for half in range(2):
                            for j in range(NT):
                                g = h * 32 + half * 16 + j
                                if g >= 2:
                                    p.wait(t, "expS", g - 1)
                                last = None
                                for qs in range(2):
                                    last = t.matmul(
                                        psSS[:, (g % 2) * 1024 + qs * 512:
                                             (g % 2) * 1024 + (qs + 1) * 512],
                                        KB[:, (h % 2) * T + j * 128:
                                           (h % 2) * T + (j + 1) * 128],
                                        QM[:, (h % 2) * T + half * 1024 + qs * 512:
                                           (h % 2) * T + half * 1024 + (qs + 1) * 512],
                                        start=True, stop=True)
                                p.inc("stmT", last)
                                for f in pend:
                                    f(t)
                                pend = [lambda t, h=h, half=half, j=j:
                                        emit_av(t, h, half, j)]
                                if pieces:
                                    pieces.pop(0)(t)
                                if pieces and len(pieces) > 30 - (half * 16 + j):
                                    pieces.pop(0)(t)
                        for f in pend:
                            f(t)
                        for f in pieces:
                            f(t)
                    block.tensor(t_slot)

                    def s_slot(s, h=h, ha=ha):
                        def kev(ts):
                            p.wait(s, "kmm2", ha * 4 + ts + 1)
                            p.inc("kevS", s.activation(
                                out=ktp[0:64, ts * 512:(ts + 1) * 512],
                                in_=psA[0:64, (ts % 2) * 512:((ts % 2) + 1) * 512],
                                func=ACT.Identity,
                                bias=bkh[:, ha:ha + 1], scale=0.125))
                        if h < 0:
                            if ha < H:
                                for ts in range(4):
                                    kev(ts)
                            return
                        if ha < H:
                            kev(0)
                            kev(1)
                        if h == 0:
                            p.wait(s, "stqn0", p.val("stqn0"))
                            p.wait(s, "stqn1", p.val("stqn1"))
                        for half in range(2):
                            for j in range(NT):
                                g = h * 32 + half * 16 + j
                                p.wait(s, "stmT", g + 1)
                                if g >= 2:
                                    p.wait(s, "avT", g - 1)
                                p.inc("expS", s.activation(
                                    out=PT[:, (g % 2) * D:(g % 2) * D + D],
                                    in_=psSS[:, (g % 2) * 1024:(g % 2) * 1024 + 1024],
                                    func=ACT.Exp))
                                if half == 0 and j == 1 and ha < H:
                                    kev(2)
                                    kev(3)
                            p.wait(s, "avT", h * 32 + half * 16 + 16)
                            if h >= 2:
                                p.wait(s, "tpD", (h - 1) * 256)
                            if h == 0 and half == 0:
                                p.wait(s, "kbzV", 1)
                            p.inc("uevS", s.activation(
                                out=auT[0:65, (h % 2) * T + half * 1024:
                                        (h % 2) * T + half * 1024 + 1024],
                                                                in_=psV[0:65, 0:1024], func=ACT.Identity))
                    block.scalar(s_slot)

                    def v_slot(v, h=h, ha=ha):
                        if ha < H:
                            p.wait(v, "kevS", (ha + 1) * 4)
                            p.wait(v, "kbzV", 1)
                            if ha >= 2:
                                p.wait(v, "stmT", (ha - 1) * 32)
                            v.tensor_copy(out=KB[0:64, (ha % 2) * T:((ha % 2) + 1) * T],
                                          in_=ktp[0:64, 0:T])
                            p.inc("kbV", v.engine_nop())
                            for qt in range(NT):
                                for sub in range(2):
                                    p.wait(v, "aT", ha * 32 + qt * 2 + sub + 1)
                                    p.inc("rmV", v.tensor_reduce(
                                        out=nrmA[:, qt * 2 + sub:qt * 2 + sub + 1],
                                        in_=psA[:, 0:1024], axis=AX, op=OP.max,
                                        negate=True))
                            v.drain()
                            v.tensor_tensor(out=negm[:, 0:16], in0=nrmA[:, 0:32:2],
                                            in1=nrmA[:, 1:32:2], op=OP.min)
                            p.inc("rmhV", v.engine_nop())
                            p.wait(v, "nmT", ha + 1)
                            if ha >= 2:
                                p.wait(v, "stmA%d" % (ha % 2), (ha // 2) * 16)
                            v.tensor_copy(out=mst[:, (ha % 2) * 128:((ha % 2) + 1) * 128],
                                          in_=psA[0:16, 0:128])
                            p.inc("mstV", v.engine_nop())
                        hb = h - 1
                        if 0 <= hb:
                            p.wait(v, "tpD", (hb + 1) * 256)
                            b0 = (hb % 2) * 1280
                            v.drain()
                            v.tensor_scalar(out=rec_t[:, 0:16],
                                            in0=blk[:, b0 + 64:b0 + 16 * 80:80],
                                            scalar1=1e-8, scalar2=None, op0=OP.add)
                            v.drain()
                            v.reciprocal(out=rec_t[:, 0:16], in_=rec_t[:, 0:16])
                            v.drain()
                            for tt in range(NT):
                                v.tensor_scalar(
                                    out=attn_acc[:, tt * D + hb * 64:
                                                 tt * D + hb * 64 + 64],
                                    in0=blk[:, b0 + tt * 80:b0 + tt * 80 + 64],
                                    scalar1=rec_t[:, tt:tt + 1], scalar2=None,
                                    op0=OP.mult)
                            p.inc("blkV", v.engine_nop())
                    block.vector(v_slot)

                    def sy_slot(sy, h=h, ha=ha):
                        if ha < H:
                            p.wait(sy, "stqn0", p.val("stqn0"))
                            p.wait(sy, "stqn1", p.val("stqn1"))
                            if ha >= 2:
                                p.wait(sy, "stmT", (ha - 1) * 32)
                            p.inc("qmd%d" % (ha % 2), sy.dma_start(
                                out=QM[0:64, (ha % 2) * T:((ha % 2) + 1) * T],
                                in_=qln_d[64 * (ha % 2):64 * (ha % 2) + 64,
                                          (ha // 2) * T:(ha // 2 + 1) * T]), 16)
                        hb = h - 1
                        if hb >= 0:
                            p.wait(sy, "uevS", hb * 2 + 2)
                            if hb >= 1:
                                p.wait(sy, "tpD", hb * 256)
                            if hb >= 2:
                                p.wait(sy, "blkV", hb)
                            for tt in range(NT):
                                p.inc("tpD", sy.dma_start_transpose(
                                    out=blk[:, (hb % 2) * 1280 + tt * 80:
                                            (hb % 2) * 1280 + (tt + 1) * 80],
                                    in_=auT[0:80, (hb % 2) * T + tt * 128:
                                            (hb % 2) * T + (tt + 1) * 128]), 16)
                        if ha < H:
                            p.wait(sy, "mstV", ha + 1)
                            p.inc("stmA%d" % (ha % 2), sy.dma_start(
                                out=m_d[ha % 2:ha % 2 + 1, :],
                                in_=mst[:, (ha % 2) * 128:((ha % 2) + 1) * 128]), 16)
                            p.wait(sy, "stmA%d" % (ha % 2), (ha // 2 + 1) * 16)
                            p.inc("qmr%d" % (ha % 2), sy.dma_start(
                                out=QM[64:65, (ha % 2) * T:((ha % 2) + 1) * T],
                                in_=m_d[ha % 2:ha % 2 + 1, :]), 16)
                    block.sync(sy_slot)

                def sy_tp15(sy):
                    p.wait(sy, "uevS", 15 * 2 + 2)
                    p.wait(sy, "tpD", 15 * 256)
                    p.wait(sy, "blkV", 15)
                    for tt in range(NT):
                        p.inc("tpD", sy.dma_start_transpose(
                            out=blk[:, (15 % 2) * 1280 + tt * 80:
                                    (15 % 2) * 1280 + (tt + 1) * 80],
                            in_=auT[0:80, (15 % 2) * T + tt * 128:
                                    (15 % 2) * T + (tt + 1) * 128]), 16)
                block.sync(sy_tp15)

                def v_qz(v):
                    p.wait(v, "xsV2", 1)
                    p.wait(v, "kmm2", 64)
                    for dk in range(ND):
                        for hf2 in range(2):
                            sl = slice(dk * T + hf2 * 1024, dk * T + (hf2 + 1) * 1024)
                            v.tensor_scalar(out=lnsc[:], in0=xn2T[:, sl],
                                            scalar1=xsb[:, 1:2], scalar2=RBIG,
                                            op0=OP.mult, op1=OP.add)
                            v.drain()
                            v.tensor_scalar(out=xn2T[:, sl], in0=lnsc[:],
                                            scalar1=RBIG, scalar2=None,
                                            op0=OP.subtract)
                            v.drain()
                    p.inc("qzV", v.engine_nop())
                block.vector(v_qz)

                def v_blk15(v):
                    p.wait(v, "tpD", H * 256)
                    b0 = (15 % 2) * 1280
                    v.drain()
                    v.tensor_scalar(out=rec_t[:, 0:16],
                                    in0=blk[:, b0 + 64:b0 + 16 * 80:80],
                                    scalar1=1e-8, scalar2=None, op0=OP.add)
                    v.drain()
                    v.reciprocal(out=rec_t[:, 0:16], in_=rec_t[:, 0:16])
                    v.drain()
                    for tt in range(NT):
                        v.tensor_scalar(
                            out=attn_acc[:, tt * D + 15 * 64:tt * D + 15 * 64 + 64],
                            in0=blk[:, b0 + tt * 80:b0 + tt * 80 + 64],
                            scalar1=rec_t[:, tt:tt + 1], scalar2=None,
                            op0=OP.mult)
                    p.inc("blkV", v.engine_nop())
                block.vector(v_blk15)

            e["mark"]("FFNWO")
            def g_wo(gg):
                p.wait(gg, "kmm2", 64)
                p.inc("ldWo", gg.dma_start(out=wfull[:], in_=d_Wo), 16)
            block.gpsimd(g_wo)

        esC = ExitStack()
        with esC:
            sbC = lambda *a: esC.enter_context(nc.sbuf_tensor(*a))
            w8b = sbC("w8b", [128, ND * D], BF16)
            ffnP = sbC("ffnP", [128, 2 * ND * 512], BF16)
            qbuf = sbC("qbuf", [128, ND * T], BF16)
            accT = sbC("accT", [128, 2 * D], BF16)
            wrow = sbC("wrow", [1, D], F32)
            psF = esC.enter_context(nc.psum_tensor("psF", [128, 1024], F32))
            psG = esC.enter_context(nc.psum_tensor("psG", [128, 2048], F32))
            psTc = esC.enter_context(nc.psum_tensor("psTc", [128, 1024], BF16))

            def g_w8(gg):
                p.wait(gg, "avT", 16 * 32)
                p.inc("ldW8", gg.dma_start(out=w8b[:], in_=d_W8T), 16)
            block.gpsimd(g_w8)


            def v_lnb(v):
                v.tensor_copy(out=lnbc_b[:], in_=col(C_LNB))
                p.inc("lnbV", v.engine_nop())
            block.vector(v_lnb)

            def t_wob(t):
                p.wait(t, "ldWo", 16)
                p.wait(t, "lnbV", 1)
                p.wait(t, "blkV", H)
                p.wait(t, "uevS", 32)
                last = None
                for oh in range(2):
                    for ch in range(ND):
                        last = t.matmul(
                            psF[0:1, oh * 512:(oh + 1) * 512],
                            lnbc_b[:, ch:ch + 1],
                            wfull[:, ch * D + oh * 512:ch * D + (oh + 1) * 512],
                            start=(ch == 0), stop=(ch == ND - 1))
                p.inc("wobT", last)
            block.tensor(t_wob)

            def v_wob(v):
                p.wait(v, "wobT", 1)
                v.tensor_tensor(out=wrow[:], in0=psF[0:1, 0:1024], in1=bo_f[:],
                                op=OP.add)
                v.drain()
                v.tensor_copy(out=rows_b[0:1, R_WOB * D:(R_WOB + 1) * D],
                              in_=wrow[:])
                p.inc("wobV", v.engine_nop())
            block.vector(v_wob)

            for ts in range(4):
                def t_f(t, ts=ts):
                    p.wait(t, "ldW8", 16)
                    p.wait(t, "qzV", 1)
                    p.wait(t, "wobV", 1)
                    if ts >= 2:
                        p.wait(t, "ffV", (ts - 1) * ND)
                    for nch in range(ND):
                        gi = ts * ND + nch
                        if gi >= 2:
                            p.wait(t, "ffV", gi - 1)
                        last = None
                        for dk in range(ND):
                            last = t.matmul(
                                psF[:, (nch % 2) * 512:((nch % 2) + 1) * 512],
                                w8b[:, dk * D + nch * 128:dk * D + (nch + 1) * 128],
                                xn2T[:, dk * T + ts * 512:dk * T + (ts + 1) * 512],
                                start=(dk == 0), stop=(dk == ND - 1))
                        p.inc("ffT", last)
                block.tensor(t_f)

                def v_f(v, ts=ts):
                    for nch in range(ND):
                        gi = ts * ND + nch
                        p.wait(v, "ffT", gi + 1)
                        v.tensor_scalar(
                            out=ffnP[:, (ts % 2) * ND * 512 + nch * 512:
                                     (ts % 2) * ND * 512 + (nch + 1) * 512],
                            in0=psF[:, (nch % 2) * 512:((nch % 2) + 1) * 512],
                            scalar1=col(C_SV)[:, nch:nch + 1],
                            scalar2=col(C_B1)[:, nch:nch + 1],
                            op0=OP.mult, op1=OP.add)
                        p.inc("ffV", v.engine_nop())
                block.vector(v_f)

                for sg in range(2):
                    tts = [ts * 4 + sg * 2, ts * 4 + sg * 2 + 1]

                    def t_acc(t, ts=ts, sg=sg, tts=tts):
                        p.wait(t, "blkV", H)
                        sgi = ts * 2 + sg
                        if sgi > 0:
                            p.wait(t, "finV", sgi * 2)
                        for k, tt in enumerate(tts):
                            if sgi * 2 + k > 0:
                                p.wait(t, "accV", sgi * 2 + k)
                            last = None
                            for ch in range(ND):
                                last = t.transpose(
                                    psTc[:, ch * 128:(ch + 1) * 128],
                                    attn_acc[:, tt * D + ch * 128:
                                             tt * D + (ch + 1) * 128],
                                    ident_b[:])
                            p.inc("accT2", last)
                    block.tensor(t_acc)

                    def v_acc(v, ts=ts, sg=sg, tts=tts):
                        for k, tt in enumerate(tts):
                            p.wait(v, "accT2", (ts * 2 + sg) * 2 + k + 1)
                            v.tensor_copy(out=accT[:, k * D:(k + 1) * D],
                                          in_=psTc[:, 0:1024])
                            p.inc("accV", v.engine_nop())
                    block.vector(v_acc)

                    if ts == 0 and sg == 0:
                        def sy_qb(sy):
                            p.wait(sy, "stqn0", p.val("stqn0"))
                            p.wait(sy, "stqn1", p.val("stqn1"))
                            p.wait(sy, "blkV", H)
                            p.inc("ldqb0", sy.dma_start(out=qbuf[:], in_=qln_d), 16)
                        block.sync(sy_qb)

                    def sy_x1(sy, ts=ts, sg=sg, tts=tts):
                        for k, tt in enumerate(tts):
                            if tt < 4:
                                p.wait(sy, "qcC", NT)
                            else:
                                p.wait(sy, "finV", tt - 3)
                            p.inc("ldf%d" % (tt % 4), sy.dma_start(
                                out=xrot[:, (tt % 4) * D:((tt % 4) + 1) * D],
                                in_=x1_d[tt * 128:(tt + 1) * 128, :]), 16)
                    block.sync(sy_x1)

                    def t_wo(t, ts=ts, sg=sg, tts=tts):
                        p.wait(t, "accV", (ts * 2 + sg) * 2 + 2)
                        p.wait(t, "wobV", 1)
                        if ts + sg > 0:
                            p.wait(t, "finV", (ts * 2 + sg) * 2)
                        for ch in range(ND):
                            gi = (ts * 2 + sg) * ND + ch
                            if ch == 0:
                                p.wait(t, "ldqb0", 16)
                            for k, tt in enumerate(tts):
                                for oh in range(2):
                                    t.matmul(
                                        psG[:, (k * 2 + oh) * 512:(k * 2 + oh + 1) * 512],
                                        accT[:, k * D + ch * 128:k * D + (ch + 1) * 128],
                                        wfull[:, ch * D + oh * 512:ch * D + (oh + 1) * 512],
                                        start=(ch == 0), stop=False)
                                    last = t.matmul(
                                        psG[:, (k * 2 + oh) * 512:(k * 2 + oh + 1) * 512],
                                        qbuf[:, ch * T + tt * 128:
                                             ch * T + (tt + 1) * 128],
                                        wfull[:, ch * D + oh * 512:ch * D + (oh + 1) * 512],
                                        start=False,
                                        stop=False)
                            p.inc("woT", last)
                        for k, tt in enumerate(tts):
                            for oh in range(2):
                                last = t.matmul(
                                    psG[:, (k * 2 + oh) * 512:(k * 2 + oh + 1) * 512],
                                    ones_b[0:1, 0:128],
                                    rows_b[0:1, R_BO * D + oh * 512:
                                           R_BO * D + (oh + 1) * 512],
                                    start=False, stop=True)
                        p.inc("woT2", last)
                    block.tensor(t_wo)

                    def t_ft(t, ts=ts, sg=sg, tts=tts):
                        p.wait(t, "woT2", (ts * 2 + sg) + 1)
                        p.wait(t, "ffV", (ts + 1) * ND)
                        for k, tt in enumerate(tts):
                            p.wait(t, "finV", (ts * 2 + sg) * 2 + k)
                            last = None
                            for nch in range(ND):
                                last = t.transpose(
                                    psTc[:, nch * 128:(nch + 1) * 128],
                                    ffnP[:, (ts % 2) * ND * 512 + nch * 512 +
                                         (tt % 4) * 128:
                                         (ts % 2) * ND * 512 + nch * 512 +
                                         (tt % 4) * 128 + 128],
                                    ident_b[:])
                            p.inc("ftT", last)
                    block.tensor(t_ft)

                    def v_fin(v, ts=ts, sg=sg, tts=tts):
                        for k, tt in enumerate(tts):
                            p.wait(v, "ldf%d" % (tt % 4), (tt // 4 + 1) * 16)
                            p.wait(v, "woT2", (ts * 2 + sg) + 1)
                            p.wait(v, "ftT", (ts * 2 + sg) * 2 + k + 1)
                            if tt >= 2:
                                p.wait(v, "stf%d" % (tt % 2), (tt // 2) * 16)
                            for oh in range(2):
                                v.tensor_tensor(
                                    out=xorot[:, (tt % 2) * D + oh * 512:
                                              (tt % 2) * D + (oh + 1) * 512],
                                    in0=psG[:, (k * 2 + oh) * 512:
                                            (k * 2 + oh + 1) * 512],
                                    in1=xrot[:, (tt % 4) * D + oh * 512:
                                             (tt % 4) * D + (oh + 1) * 512],
                                    op=OP.add)
                            v.drain()
                            if _STAGE >= 5:
                                v.tensor_copy(out=lnsc[:], in_=psTc[:, 0:1024])
                                v.drain()
                                v.tensor_tensor(
                                    out=xorot[:, (tt % 2) * D:((tt % 2) + 1) * D],
                                    in0=xorot[:, (tt % 2) * D:((tt % 2) + 1) * D],
                                    in1=lnsc[:],
                                    op=OP.add)
                                v.drain()
                            p.inc("finV", v.engine_nop())
                    block.vector(v_fin)

                    def sy_fo(sy, ts=ts, sg=sg, tts=tts):
                        for k, tt in enumerate(tts):
                            p.wait(sy, "finV", (ts * 2 + sg) * 2 + k + 1)
                            p.inc("stf%d" % (tt % 2), sy.dma_start(
                                out=d_out[tt * 128:(tt + 1) * 128, :],
                                in_=xorot[:, (tt % 2) * D:((tt % 2) + 1) * D]), 16)
                    block.sync(sy_fo)

            def sy_fin(sy):
                p.wait(sy, "stf0", p.val("stf0"))
                p.wait(sy, "stf1", p.val("stf1"))
            block.sync(sy_fin)



def kernel(**inputs):
    npad, per_core = _host_prep(inputs)
    nc = build(npad)
    res = run_bass_kernel_spmd(nc, per_core, core_ids=list(range(NCORES)))
    out = np.stack([res.results[i]["out"] for i in range(NCORES)], axis=0)
    return out.astype(np.float32)
